# version 02
# baseline (speedup 1.0000x reference)
import numpy as np
import ml_dtypes
from contextlib import ExitStack

import concourse.bass as bass
import concourse.mybir as mybir
from concourse.bass_utils import run_bass_kernel_spmd
from concourse.bass_core_rust import IndirectOffsetOnAxis

F32 = mybir.dt.float32
BF16 = mybir.dt.bfloat16
FP8 = mybir.dt.float8e4
AF = mybir.ActivationFunctionType
ALU = mybir.AluOpType
AX = mybir.AxisListType

NCORES = 8
T = 16384
D = 2048
SEG = T // NCORES
C = 128
DK = 64
DV = 128
VH = 2
KS = 4
TB = 512
NTB = T // TB
SCALE = float(DK) ** -0.5
NEG = -1.0e30

STAGE = "ALL"
import os as _os
SCAN_LVL = int(_os.environ.get("SCAN_LVL", "99"))
TAP = _os.environ.get("TAP", "")


def build_nc(stage="ALL"):
    nc = bass.Bass("TRN2", target_bir_lowering=False)

    def din(name, shape, dt=F32):
        return nc.dram_tensor(name, shape, dt, kind="ExternalInput").ap()

    x_seg = din("x_seg", [SEG, D])
    w_qkvT = din("w_qkvT", [D, 384])
    w_zT = din("w_zT", [D, 256])
    w_baT = din("w_baT", [D, 16])
    w_outT = din("w_outT", [D, D])
    w2_slice = din("w2_slice", [D // NCORES, D])
    convd_q = din("convd_q", [128, KS * DK])
    convd_k = din("convd_k", [128, KS * DK])
    convd_v0 = din("convd_v0", [128, KS * 128])
    convd_v1 = din("convd_v1", [128, KS * 128])
    smalls = din("smalls", [1, 192])
    maskA = din("maskA", [C, C])
    maskR = din("maskR", [C, C])
    maskFt = din("maskFt", [C, C])
    triu_b = din("triu_b", [C, C], BF16)
    ident_b = din("ident_b", [128, 128], BF16)
    ident_f = din("ident_f", [128, 128], F32)
    gtoff = din("gtoff", [128, 16], mybir.dt.int32)

    y_seg = nc.dram_tensor("y_seg", [SEG, D], F32, kind="ExternalOutput").ap()

    cc_ln_in = nc.dram_tensor("cc_ln_in", [D, SEG], BF16, kind="Internal").ap()
    cc_ln_out = nc.dram_tensor("cc_ln_out", [NCORES * D, SEG], BF16, kind="Internal").ap()
    cc_q2_in = nc.dram_tensor("cc_q2_in", [D // NCORES, D], FP8, kind="Internal").ap()
    cc_q2_out = nc.dram_tensor("cc_q2_out", [D, D], FP8, kind="Internal").ap()
    cc_gt_in = nc.dram_tensor("cc_gt_in", [VH * DV, T], BF16, kind="Internal").ap()
    cc_gt_out = nc.dram_tensor("cc_gt_out", [NCORES * VH * DV, T], BF16, kind="Internal").ap()
    dbg = nc.dram_tensor("dbg", [256, 2048], BF16, kind="ExternalOutput").ap() \
        if stage in ("A", "B1", "B") else None

    es = ExitStack()

    def sb(name, shape, dt=F32):
        return es.enter_context(nc.sbuf_tensor(name, shape, dt))

    def pt(name, shape, dt=F32):
        return es.enter_context(nc.psum_tensor(name, shape, dt))

    SYNC, VEC, SCA, TEN, GPS = [], [], [], [], []
    STEPLOG = []
    nc._steplog = STEPLOG

    class S:
        def __init__(self, name):
            self.h = es.enter_context(nc.semaphore(name))
            self.n = 0

    def step(lst, sem, waits, fn, k=1):
        wl = {}
        for t in waits:
            if t is None:
                continue
            s_, n_ = t
            if n_ > wl.get(s_, -1):
                wl[s_] = n_
        tgt = None
        if sem is not None:
            sem.n += k
            tgt = sem.n

        def closure(e, wl=wl, fn=fn, sem=sem, k=k):
            for s_, n_ in wl.items():
                if n_ > 0:
                    e.wait_ge(s_.h, n_)
            last = fn(e)
            if sem is not None and last is not None:
                last.then_inc(sem.h, k)

        STEPLOG.append((id(lst), {id(s_): n_ for s_, n_ in wl.items()},
                        id(sem) if sem else None, tgt, len(lst)))
        lst.append(closure)
        return (sem, tgt) if sem is not None else None

    with es:
        ones_r = sb("ones_r", [1, 128])
        ones_t = sb("ones_t", [128, 128])
        one_col = sb("one_col", [128, 1])
        mA = sb("mA", [C, C]); mR = sb("mR", [C, C]); mFt = sb("mFt", [C, C])
        trb = sb("trb", [C, C], BF16)
        idb = sb("idb", [128, 128], BF16)
        idf = sb("idf", [128, 128], F32)
        small_row = sb("small_row", [1, 192])
        pbc = sb("pbc", [128, 192])
        nw_t = sb("nw_t", [128, DV])
        smc = sb("smc", [128, 8])

        wstage = [sb(f"wstage{i}", [128, D]) for i in range(2)]
        atile = sb("atile", [128, D])
        sqtile = sb("sqtile", [128, D])
        acol = sb("acol", [128, 8])
        gto = sb("gto", [128, 16], mybir.dt.int32)
        gt_o = [sb(f"gt_o{k}", [128, 1], mybir.dt.int32) for k in range(16)]

        es_b = es.enter_context(ExitStack())

        def sbb(name, shape, dt=F32):
            return es_b.enter_context(nc.sbuf_tensor(name, shape, dt))

        wqkvT = [sbb(f"wqkvT{k}", [128, 384], BF16) for k in range(16)]
        wzT = [sbb(f"wzT{k}", [128, 256], BF16) for k in range(16)]
        wbaT = [sbb(f"wbaT{k}", [128, 16], BF16) for k in range(16)]
        cd_q = sbb("cd_q", [128, KS * DK], BF16)
        cd_k = sbb("cd_k", [128, KS * DK], BF16)
        cd_v0 = sbb("cd_v0", [128, KS * 128], BF16)
        cd_v1 = sbb("cd_v1", [128, KS * 128], BF16)

        lnbf = [sbb(f"lnbf{i}", [128, D], BF16) for i in range(2)]
        tstg = [sbb(f"tstg{i}", [128, 128], BF16) for i in range(2)]

        lnx = [[sbb(f"lnx{i}_{k}", [128, TB], BF16) for k in range(16)] for i in range(2)]
        mix_qk = sbb("mix_qk", [128, 3 + TB], BF16)
        mix_v0 = sbb("mix_v0", [128, 3 + TB], BF16)
        mix_v1 = sbb("mix_v1", [128, 3 + TB], BF16)
        act_q = [sbb(f"act_q{i}", [64, TB], BF16) for i in range(2)]
        act_k = [sbb(f"act_k{i}", [64, TB], BF16) for i in range(2)]
        act_v0 = [sbb(f"act_v0{i}", [128, TB], BF16) for i in range(2)]
        act_v1 = [sbb(f"act_v1{i}", [128, TB], BF16) for i in range(2)]
        z_ring = [sbb(f"z_ring{i}", [128, 256], BF16) for i in range(8)]
        ba_ring = [sbb(f"ba_ring{i}", [128, 4]) for i in range(8)]

        kv_t = [sbb(f"kv_t{i}", [128, 128], BF16) for i in range(2)]
        v_t = [sbb(f"v_t{i}", [128, 256], BF16) for i in range(2)]
        sqw = [sbb(f"sqw{i}", [128, 128]) for i in range(2)]
        eT = [[sbb(f"eT{i}_{j}", [128, 128]) for j in range(3)] for i in range(2)]
        gsp = [sbb(f"gsp{i}", [128, 16], BF16) for i in range(2)]
        colpA = [sbb(f"colpA{i}", [128, 96]) for i in range(2)]
        colpB = [sbb(f"colpB{i}", [128, 96]) for i in range(2)]
        rows6 = [sbb(f"rows6_{i}", [1, 768]) for i in range(2)]
        cums = [sbb(f"cums{i}", [128, 8]) for i in range(2)]
        colw = [sbb(f"colw{i}", [128, 24]) for i in range(2)]
        lamc = [sbb(f"lamc{i}", [128, 8]) for i in range(2)]
        qkb = [sbb(f"qkb{i}", [128, 128], BF16) for i in range(2)]
        Amat = [[sbb(f"Amat{i}_{v}", [C, C], BF16) for v in range(VH)] for i in range(2)]
        Rmat = [[sbb(f"Rmat{i}_{v}", [C, C], BF16) for v in range(VH)] for i in range(2)]
        Apow = [[sbb(f"Apow{i}_{j}", [C, C], BF16) for j in range(2)] for i in range(2)]
        Rpow = [[sbb(f"Rpow{i}_{j}", [C, C], BF16) for j in range(2)] for i in range(2)]
        Ftm = [[sbb(f"Ftm{i}_{v}", [C, C], BF16) for v in range(VH)] for i in range(2)]
        Xs = [[sbb(f"Xs{i}_{j}", [C, DK + DV], BF16) for j in range(2)] for i in range(2)]
        rhs0 = [[sbb(f"rhs0{i}_{v}", [C, DK + DV], BF16) for v in range(VH)] for i in range(2)]
        kbar = [[sbb(f"kbar{i}_{v}", [C, DK], BF16) for v in range(VH)] for i in range(2)]
        wtT = [sbb(f"wtT{i}", [DK, C], BF16) for i in range(2)]
        uu = [[sbb(f"uu{i}_{v}", [C, DV], BF16) for v in range(VH)] for i in range(2)]
        S_f = [[sbb(f"S_f{v}_{j}", [DK, DV]) for j in range(2)] for v in range(VH)]
        S_b = [sbb(f"S_b{v}", [DK, DV], BF16) for v in range(VH)]
        o_w = [sbb(f"o_w{i}", [128, 256]) for i in range(2)]
        zs_t = [sbb(f"zs_t{i}", [128, 256], BF16) for i in range(2)]
        gat = [sbb(f"gat{i}", [128, 256], BF16) for i in range(2)]
        gstrip = [sbb(f"gstrip{v}", [128, 2048], BF16) for v in range(VH)]


        P = [pt(f"P{i}", [128, 512]) for i in range(7)]
        PT = pt("PT", [128, 1024], BF16)

        sd = S("sd"); sg = S("sg"); sv = S("sv"); ss = S("ss"); st = S("st"); scc = S("scc")

        block = es.enter_context(nc.Block())

        tok_cl = step(SYNC, sd, [], lambda e: (
            e.dma_start(out=mA[:], in_=maskA[:]).then_inc(sd.h, 16),
            e.dma_start(out=mR[:], in_=maskR[:]).then_inc(sd.h, 16),
            e.dma_start(out=mFt[:], in_=maskFt[:]).then_inc(sd.h, 16),
            e.dma_start(out=trb[:], in_=triu_b[:]).then_inc(sd.h, 16),
            e.dma_start(out=idb[:], in_=ident_b[:]).then_inc(sd.h, 16),
            e.dma_start(out=idf[:], in_=ident_f[:]).then_inc(sd.h, 16),
            e.dma_start(out=gto[:], in_=gtoff[:]).then_inc(sd.h, 16),
            e.dma_start(out=small_row[:], in_=smalls[:]).then_inc(sd.h, 16),
            None)[-1], k=16 * 8)

        tok_ms = step(VEC, sv, [], lambda e: (
            e.memset(ones_r[:], 1.0),
            e.memset(ones_t[:], 1.0),
            e.memset(one_col[:], 1.0),
            e.memset(gsp[0][:], 0.0),
            e.memset(gsp[1][:], 0.0),
            e.memset(colpA[0][:], 0.0),
            e.memset(colpA[1][:], 0.0),
            e.memset(colpB[0][:], 0.0),
            e.memset(colpB[1][:], 0.0),
            e.memset(mix_qk[:, 0:3], 0.0),
            e.memset(mix_v0[:, 0:3], 0.0),
            e.memset(mix_v1[:, 0:3], 0.0))[-1])

        tok_s0 = step(VEC, sv, [], lambda e: (
            [e.memset(S_f[v][0][:], 0.0) for v in range(VH)],
            [e.memset(S_b[v][:], 0.0) for v in range(VH)])[-1][-1])

        tok_bmm = step(TEN, st, [tok_cl, tok_ms], lambda e:
                       e.matmul(P[0][:, 0:192], ones_r[:], small_row[:],
                                start=True, stop=True))
        tok_bcp = step(VEC, sv, [tok_bmm], lambda e: (
            e.tensor_copy(out=pbc[:], in_=P[0][:, 0:192])))
        tok_nw = step(VEC, sv, [tok_bcp], lambda e:
                      e.tensor_copy(out=nw_t[:], in_=pbc[:, 64:64 + DV]))
        tok_smc1 = step(VEC, sv, [tok_bcp], lambda e: (
            e.tensor_copy(out=smc[:, 0:2], in_=pbc[:, 0:2]),
            e.tensor_copy(out=smc[:, 4:6], in_=pbc[:, 4:6]))[-1])
        tok_eA = step(SCA, ss, [tok_smc1], lambda e:
                      e.activation(out=smc[:, 2:4], in_=pbc[:, 2:4], func=AF.Exp))
        tok_smc = step(VEC, sv, [tok_eA], lambda e: (
            e.tensor_scalar(out=smc[:, 2:4], in0=smc[:, 2:4], scalar1=-1.0,
                            scalar2=None, op0=ALU.mult),
            e.tensor_tensor(out=smc[:, 7:8], in0=smc[:, 4:5], in1=smc[:, 5:6], op=ALU.mult))[-1])
        tok_smc2 = step(VEC, sv, [tok_smc], lambda e:
                        e.reciprocal(out=smc[:, 6:7], in_=smc[:, 7:8]))

        wcast = []
        for kt in range(16):
            tl = step(SYNC, sd, [wcast[kt - 2] if kt >= 2 else None], lambda e, kt=kt: (
                e.dma_start(out=wstage[kt % 2][:, 0:384],
                            in_=w_qkvT[kt * 128:(kt + 1) * 128, :]).then_inc(sd.h, 16),
                e.dma_start(out=wstage[kt % 2][:, 384:640],
                            in_=w_zT[kt * 128:(kt + 1) * 128, :]).then_inc(sd.h, 16),
                e.dma_start(out=wstage[kt % 2][:, 640:656],
                            in_=w_baT[kt * 128:(kt + 1) * 128, :]).then_inc(sd.h, 16),
                None)[-1], k=48)
            tc = step(VEC, sv, [tl], lambda e, kt=kt: (
                e.tensor_copy(out=wqkvT[kt][:], in_=wstage[kt % 2][:, 0:384]),
                e.tensor_copy(out=wzT[kt][:], in_=wstage[kt % 2][:, 384:640]),
                e.tensor_copy(out=wbaT[kt][:], in_=wstage[kt % 2][:, 640:656]))[-1])
            wcast.append(tc)
        tok_w = wcast[-1]

        cdst = sbb("cdst", [128, 1536])
        tcl = step(SYNC, sd, [], lambda e: (
            e.dma_start(out=cdst[:, 0:256], in_=convd_q[:]).then_inc(sd.h, 16),
            e.dma_start(out=cdst[:, 256:512], in_=convd_k[:]).then_inc(sd.h, 16),
            e.dma_start(out=cdst[:, 512:1024], in_=convd_v0[:]).then_inc(sd.h, 16),
            e.dma_start(out=cdst[:, 1024:1536], in_=convd_v1[:]).then_inc(sd.h, 16),
            None)[-1], k=64)
        tok_cd = step(VEC, sv, [tcl], lambda e: (
            e.tensor_copy(out=cd_q[:], in_=cdst[:, 0:256]),
            e.tensor_copy(out=cd_k[:], in_=cdst[:, 256:512]),
            e.tensor_copy(out=cd_v0[:], in_=cdst[:, 512:1024]),
            e.tensor_copy(out=cd_v1[:], in_=cdst[:, 1024:1536]))[-1])

        ln_done = [None, None]
        piece_dma = [None, None]
        last_piece_dma = None
        tr_prev = None
        for i in range(16):
            lp = i % 2
            tx = step(SYNC, sd, [ln_done[lp], tok_cd], lambda e, i=i:
                      e.dma_start(out=atile[:], in_=x_seg[i * 128:(i + 1) * 128, :]), k=16)
            t1a = step(VEC, sv, [tx], lambda e:
                       e.reduce_sum(out=acol[:, 0:1], in_=atile[:], axis=AX.X))
            t1 = step(VEC, sv, [t1a], lambda e:
                      e.tensor_scalar(out=acol[:, 1:2], in0=acol[:, 0:1], scalar1=1.0 / D,
                                      scalar2=None, op0=ALU.mult))
            t2 = step(VEC, sv, [t1], lambda e:
                      e.tensor_scalar(out=atile[:], in0=atile[:], scalar1=acol[:, 1:2],
                                      scalar2=None, op0=ALU.subtract))
            t3 = step(SCA, ss, [t2], lambda e:
                      e.activation(out=sqtile[:], in_=atile[:], func=AF.Square))
            t4a = step(VEC, sv, [t3], lambda e:
                       e.reduce_sum(out=acol[:, 2:3], in_=sqtile[:], axis=AX.X))
            t4 = step(VEC, sv, [t4a], lambda e:
                      e.tensor_scalar(out=acol[:, 3:4], in0=acol[:, 2:3], scalar1=1.0 / D,
                                      scalar2=1e-5, op0=ALU.mult, op1=ALU.add))
            t5 = step(SCA, ss, [t4], lambda e:
                      e.activation(out=acol[:, 4:5], in_=acol[:, 3:4], func=AF.Sqrt))
            t6 = step(VEC, sv, [t5], lambda e:
                      e.reciprocal(out=acol[:, 5:6], in_=acol[:, 4:5]))
            t7 = step(VEC, sv, [t6], lambda e, lp=lp:
                      e.tensor_scalar(out=lnbf[lp][:], in0=atile[:], scalar1=acol[:, 5:6],
                                      scalar2=None, op0=ALU.mult))
            for fb in range(16):
                pp = fb % 2
                trp = step(TEN, st, [t7, piece_dma[pp], tr_prev], lambda e, lp=lp, fb=fb:
                           e.transpose(PT[:, (fb % 4) * 128:(fb % 4 + 1) * 128],
                                       lnbf[lp][:, fb * 128:(fb + 1) * 128], idb[:]))
                tcp = step(VEC, sv, [trp], lambda e, pp=pp, fb=fb:
                           e.tensor_copy(out=tstg[pp][:],
                                         in_=PT[:, (fb % 4) * 128:(fb % 4 + 1) * 128]))
                tr_prev = tcp
                tdm = step(SYNC, sd, [tcp], lambda e, pp=pp, fb=fb, i=i:
                           e.dma_start(out=cc_ln_in[fb * 128:(fb + 1) * 128,
                                                    i * 128:(i + 1) * 128],
                                       in_=tstg[pp][:]), k=16)
                piece_dma[pp] = tdm
                last_piece_dma = tdm
            ln_done[lp] = t7
        strip_dma = [last_piece_dma]

        q2_dma = []
        for i in range(2):
            tw = step(SYNC, sd, [tok_cd, q2_dma[-1] if q2_dma else None],
                      lambda e, i=i: e.dma_start(out=wstage[1][:, 0:D],
                                                 in_=w2_slice[i * 128:(i + 1) * 128, :]), k=16)
            tq = step(VEC, sv, [tw, tok_smc2, ln_done[0], ln_done[1]], lambda e:
                      e.tensor_scalar(out=atile[:].bitcast(FP8)[:, 0:D], in0=wstage[1][:, 0:D],
                                      scalar1=smc[:, 5:6], scalar2=None, op0=ALU.mult))
            td = step(SYNC, sd, [tq], lambda e, i=i:
                      e.dma_start(out=cc_q2_in[i * 128:(i + 1) * 128, :],
                                  in_=atile[:].bitcast(FP8)[:, 0:D]), k=16)
            q2_dma.append(td)

        tok_ag12 = step(GPS, scc, [strip_dma[-1], q2_dma[-1]], lambda e: (
            e.collective_compute("AllGather", ALU.bypass,
                                 replica_groups=[list(range(NCORES))],
                                 ins=[cc_ln_in[:].opt()],
                                 outs=[cc_ln_out[:].opt()]).then_inc(scc.h, 1),
            e.collective_compute("AllGather", ALU.bypass,
                                 replica_groups=[list(range(NCORES))],
                                 ins=[cc_q2_in[:].opt()],
                                 outs=[cc_q2_out[:].opt()]).then_inc(scc.h, 1),
            None)[-1], k=2)

        if stage == "A":
            tdbg = step(GPS, sg, [tok_ag12], lambda e: (
                [e.dma_start(out=dbg[r * 16:(r + 1) * 16, :],
                             in_=cc_ln_out[r * 1024 + 0:r * 1024 + 16, :]).then_inc(sg.h, 16)
                 for r in range(16)],
                None)[-1], k=16 * 16)
            GPS.append(lambda e: e.wait_ge(sg.h, sg.n))
            _finish(block, SYNC, VEC, SCA, TEN, GPS)
            return nc

        ip_tok = [None, None]
        act_tok = [None, None]
        conv_tok = [None, None]
        chunk_ten_end = [tok_s0, tok_s0]
        chunk_sb = tok_s0
        zsil_tok = [None] * 8
        ba_tok = [None] * 8
        gs_dma = []
        gs_cp = [None, None]

        NBLK = 1 if stage == "B1" else NTB

        for b in range(NBLK):
            par = b % 2
            seg_i = (b * TB) // SEG
            off = (b * TB) % SEG

            tlnx = step(SYNC, sd, [tok_ag12, ip_tok[par]], lambda e, par=par, seg_i=seg_i, off=off: (
                [e.dma_start(out=lnx[par][kt][:],
                             in_=cc_ln_out[seg_i * D + kt * 128:seg_i * D + (kt + 1) * 128,
                                           off:off + TB]).then_inc(sd.h, 16)
                 for kt in range(16)],
                None)[-1], k=16 * 16)

            tip = step(TEN, st, [tlnx, tok_w, conv_tok[par], act_tok[par],
                                 (sv, sv.n), (ss, ss.n)], lambda e, par=par: (
                [(e.matmul(P[0][:], wqkvT[kt][:, 0:128], lnx[par][kt][:],
                           start=(kt == 0), stop=(kt == 15)),
                  e.matmul(P[1][:], wqkvT[kt][:, 128:256], lnx[par][kt][:],
                           start=(kt == 0), stop=(kt == 15)),
                  e.matmul(P[2][:], wqkvT[kt][:, 256:384], lnx[par][kt][:],
                           start=(kt == 0), stop=(kt == 15))) for kt in range(16)],
                [[e.matmul((P[3] if sq < 2 else P[4])[:, (sq % 2) * 256:(sq % 2) * 256 + 256],
                           lnx[par][kt][:, sq * 128:(sq + 1) * 128], wzT[kt][:],
                           start=(kt == 0), stop=(kt == 15)) for kt in range(16)]
                 for sq in range(4)],
                [[e.matmul(P[5][:, sq * 16:(sq + 1) * 16],
                           lnx[par][kt][:, sq * 128:(sq + 1) * 128], wbaT[kt][:],
                           start=(kt == 0), stop=(kt == 15)) for kt in range(16)]
                 for sq in range(4)])[-1][-1][-1])
            ip_tok[par] = tip

            def mk_evict(b, par, tip):
                def f(e):
                    if b > 0:
                        e.tensor_copy(out=mix_qk[:, 0:3], in_=mix_qk[:, TB:TB + 3])
                        e.tensor_copy(out=mix_v0[:, 0:3], in_=mix_v0[:, TB:TB + 3])
                        e.tensor_copy(out=mix_v1[:, 0:3], in_=mix_v1[:, TB:TB + 3])
                    e.tensor_copy(out=mix_qk[:, 3:3 + TB], in_=P[0][:])
                    e.tensor_copy(out=mix_v0[:, 3:3 + TB], in_=P[1][:])
                    e.tensor_copy(out=mix_v1[:, 3:3 + TB], in_=P[2][:])
                    last = None
                    for sq in range(4):
                        pz = P[3] if sq < 2 else P[4]
                        e.tensor_copy(out=z_ring[(b * 4 + sq) % 8][:],
                                      in_=pz[:, (sq % 2) * 256:(sq % 2) * 256 + 256])
                        last = e.tensor_copy(out=ba_ring[(b * 4 + sq) % 8][:],
                                             in_=P[5][:, sq * 16:sq * 16 + 4])
                    return last
                return f
            zguards = [zsil_tok[(b * 4 + sq) % 8] for sq in range(4)] + \
                      [ba_tok[(b * 4 + sq) % 8] for sq in range(4)]
            tmx = step(VEC, sv, [tip, conv_tok[par]] + zguards, mk_evict(b, par, tip))

            tcv = step(TEN, st, [tmx, act_tok[par], chunk_ten_end[0], chunk_ten_end[1],
                                 (sv, sv.n), (ss, ss.n)],
                       lambda e, par=par: (
                [e.matmul(P[0][0:64, :], cd_q[:, tap * DK:(tap + 1) * DK],
                          mix_qk[:, tap:tap + TB],
                          start=(tap == 0), stop=(tap == KS - 1)) for tap in range(KS)],
                [e.matmul(P[6][0:64, 0:512], cd_k[:, tap * DK:(tap + 1) * DK],
                          mix_qk[:, tap:tap + TB],
                          start=(tap == 0), stop=(tap == KS - 1)) for tap in range(KS)],
                [e.matmul(P[1][:], cd_v0[:, tap * 128:(tap + 1) * 128], mix_v0[:, tap:tap + TB],
                          start=(tap == 0), stop=(tap == KS - 1)) for tap in range(KS)],
                [e.matmul(P[2][:], cd_v1[:, tap * 128:(tap + 1) * 128], mix_v1[:, tap:tap + TB],
                          start=(tap == 0), stop=(tap == KS - 1)) for tap in range(KS)])[-1][-1])
            conv_tok[par] = tcv

            tac = step(SCA, ss, [tcv], lambda e, par=par: (
                e.activation(out=act_q[par][:], in_=P[0][0:64, :], func=AF.Silu),
                e.activation(out=act_k[par][:], in_=P[6][0:64, 0:512], func=AF.Silu),
                e.activation(out=act_v0[par][:], in_=P[1][:], func=AF.Silu),
                e.activation(out=act_v1[par][:], in_=P[2][:], func=AF.Silu))[-1])
            act_tok[par] = tac

            for cl in range(4):
                if SCAN_LVL == 0:
                    break
                cc = b * 4 + cl
                cp = cc % 2
                ccol = slice(cl * 128, (cl + 1) * 128)
                zr = z_ring[cc % 8]
                bar = ba_ring[cc % 8]
                guard = chunk_ten_end[cp]

                t_tr = step(TEN, st, [tac, guard, (sv, sv.n), (ss, ss.n)],
                            lambda e, par=par, ccol=ccol: (
                    e.transpose(PT[:, 0:64], act_q[par][:, ccol], idb[0:64, 0:64]),
                    e.transpose(PT[:, 64:128], act_k[par][:, ccol], idb[0:64, 0:64]),
                    e.transpose(PT[:, 128:256], act_v0[par][:, ccol], idb[:]),
                    e.transpose(PT[:, 256:384], act_v1[par][:, ccol], idb[:]))[-1])
                t_trc = step(VEC, sv, [t_tr], lambda e, cp=cp: (
                    e.tensor_copy(out=kv_t[cp][:], in_=PT[:, 0:128]),
                    e.tensor_copy(out=v_t[cp][:], in_=PT[:, 128:384]))[-1])

                t_sq = step(SCA, ss, [t_trc], lambda e, cp=cp:
                            e.activation(out=sqw[cp][:], in_=kv_t[cp][:], func=AF.Square))
                t_ssq = step(VEC, sv, [t_sq], lambda e, cp=cp: (
                    e.reduce_sum(out=colw[cp][:, 0:1], in_=sqw[cp][:, 0:64], axis=AX.X),
                    e.reduce_sum(out=colw[cp][:, 1:2], in_=sqw[cp][:, 64:128], axis=AX.X))[-1])
                t_nr = step(SCA, ss, [t_ssq], lambda e, cp=cp:
                            e.activation(out=colw[cp][:, 2:4], in_=colw[cp][:, 0:2], func=AF.Sqrt))
                t_rna = step(VEC, sv, [t_nr], lambda e, cp=cp:
                             e.tensor_scalar(out=colw[cp][:, 2:4], in0=colw[cp][:, 2:4],
                                             scalar1=1e-12, scalar2=None, op0=ALU.max))
                t_rn = step(VEC, sv, [t_rna], lambda e, cp=cp:
                            e.reciprocal(out=colw[cp][:, 4:6], in_=colw[cp][:, 2:4]))
                t_lnrk = step(SCA, ss, [t_rn], lambda e, cp=cp:
                              e.activation(out=colw[cp][:, 6:7], in_=colw[cp][:, 5:6], func=AF.Ln))

                if SCAN_LVL < 11:
                    chunk_ten_end[cp] = (st, st.n)
                    continue
                t_apd = step(VEC, sv, [tmx, tok_smc2, guard], lambda e, cp=cp, bar=bar:
                             e.tensor_tensor(out=colw[cp][:, 8:10], in0=bar[:, 2:4],
                                             in1=smc[:, 0:2], op=ALU.add))
                t_abs = step(SCA, ss, [t_apd], lambda e, cp=cp, bar=bar: (
                    e.activation(out=colw[cp][:, 10:12], in_=colw[cp][:, 8:10], func=AF.Abs),
                    e.activation(out=colw[cp][:, 12:14], in_=bar[:, 0:2], func=AF.Abs))[-1])
                t_ex = step(SCA, ss, [t_abs], lambda e, cp=cp: (
                    e.activation(out=colw[cp][:, 10:12], in_=colw[cp][:, 10:12],
                                 func=AF.Exp, scale=-1.0),
                    e.activation(out=colw[cp][:, 12:14], in_=colw[cp][:, 12:14],
                                 func=AF.Exp, scale=-1.0))[-1])
                t_l1p = step(SCA, ss, [t_ex], lambda e, cp=cp: (
                    e.activation(out=colw[cp][:, 10:12], in_=colw[cp][:, 10:12],
                                 func=AF.Ln, bias=one_col[:]),
                    e.activation(out=colw[cp][:, 12:14], in_=colw[cp][:, 12:14],
                                 func=AF.Ln, bias=one_col[:]))[-1])
                t_bet = step(SCA, ss, [t_l1p], lambda e, cp=cp, bar=bar:
                             e.activation(out=colw[cp][:, 14:16], in_=bar[:, 0:2], func=AF.Sigmoid))
                ba_tok[cc % 8] = t_bet
                t_ga = step(VEC, sv, [t_l1p, t_bet], lambda e, cp=cp, bar=bar: (
                    e.tensor_scalar(out=cums[cp][:, 4:6], in0=colw[cp][:, 8:10],
                                    scalar1=0.0, scalar2=None, op0=ALU.max),
                    e.tensor_scalar(out=cums[cp][:, 6:8], in0=bar[:, 0:2],
                                    scalar1=0.0, scalar2=None, op0=ALU.min))[-1])
                t_gb = step(VEC, sv, [t_ga], lambda e, cp=cp: (
                    e.tensor_tensor(out=cums[cp][:, 4:6], in0=cums[cp][:, 4:6],
                                    in1=colw[cp][:, 10:12], op=ALU.add),
                    e.tensor_tensor(out=cums[cp][:, 6:8], in0=cums[cp][:, 6:8],
                                    in1=colw[cp][:, 12:14], op=ALU.subtract))[-1])
                t_g = step(VEC, sv, [t_gb], lambda e, cp=cp:
                           e.tensor_tensor(out=cums[cp][:, 4:6], in0=cums[cp][:, 4:6],
                                           in1=smc[:, 2:4], op=ALU.mult))

                t_gh = step(VEC, sv, [t_g], lambda e, cp=cp:
                            e.tensor_copy(out=gsp[cp][:, 0:2], in_=cums[cp][:, 4:6]))
                t_gl = step(VEC, sv, [t_gh], lambda e, cp=cp:
                            e.tensor_tensor(out=gsp[cp][:, 8:10], in0=cums[cp][:, 4:6],
                                            in1=gsp[cp][:, 0:2], op=ALU.subtract))

                if SCAN_LVL < 12:
                    chunk_ten_end[cp] = (st, st.n)
                    continue
                t_cm = step(TEN, st, [t_gl], lambda e, cp=cp: (
                    e.matmul(P[5][:, 64:72], trb[:], gsp[cp][:, 0:8], start=True, stop=False),
                    e.matmul(P[5][:, 64:72], trb[:], gsp[cp][:, 8:16], start=False, stop=True)
                )[-1])
                t_cuma = step(VEC, sv, [t_cm, t_lnrk], lambda e, cp=cp:
                              e.tensor_copy(out=cums[cp][:, 0:2], in_=P[5][:, 64:66]))
                t_cum = step(VEC, sv, [t_cuma], lambda e, cp=cp: (
                    e.tensor_copy(out=colpA[cp][:, 0:1], in_=cums[cp][:, 0:1]),
                    e.tensor_copy(out=colpA[cp][:, 32:33], in_=cums[cp][:, 1:2]),
                    e.tensor_tensor(out=colpA[cp][:, 64:65], in0=colw[cp][:, 6:7],
                                    in1=cums[cp][:, 0:1], op=ALU.subtract),
                    e.tensor_tensor(out=colpB[cp][:, 0:1], in0=colw[cp][:, 6:7],
                                    in1=cums[cp][:, 1:2], op=ALU.subtract),
                    e.tensor_tensor(out=colpB[cp][:, 32:33], in0=cums[cp][:, 0:1],
                                    in1=cums[cp][:, 6:7], op=ALU.add),
                    e.tensor_tensor(out=colpB[cp][:, 64:65], in0=cums[cp][:, 1:2],
                                    in1=cums[cp][:, 7:8], op=ALU.add))[-1])
                t_cum2 = step(VEC, sv, [t_cum], lambda e, cp=cp: (
                    e.tensor_tensor(out=colpB[cp][:, 32:33], in0=colpB[cp][:, 32:33],
                                    in1=colw[cp][:, 6:7], op=ALU.add),
                    e.tensor_tensor(out=colpB[cp][:, 64:65], in0=colpB[cp][:, 64:65],
                                    in1=colw[cp][:, 6:7], op=ALU.add))[-1])

                if SCAN_LVL < 13:
                    chunk_ten_end[cp] = (st, st.n)
                    continue
                t_rpt = step(TEN, st, [t_cum2], lambda e, cp=cp: (
                    e.transpose(P[6][0:96, 0:128].bitcast(F32), colpA[cp][:], idf[:]),
                    e.transpose(P[6][0:96, 128:256].bitcast(F32), colpB[cp][:], idf[:]))[-1])
                t_rows = step(VEC, sv, [t_rpt], lambda e, cp=cp: (
                    e.tensor_copy(out=rows6[cp][0:1, 0:128], in_=P[6][0:1, 0:128]),
                    e.tensor_copy(out=rows6[cp][0:1, 128:256], in_=P[6][32:33, 0:128]),
                    e.tensor_copy(out=rows6[cp][0:1, 256:384], in_=P[6][64:65, 0:128]),
                    e.tensor_copy(out=rows6[cp][0:1, 384:512], in_=P[6][0:1, 128:256]),
                    e.tensor_copy(out=rows6[cp][0:1, 512:640], in_=P[6][32:33, 128:256]),
                    e.tensor_copy(out=rows6[cp][0:1, 640:768], in_=P[6][64:65, 128:256]))[-1])

                def rowv_(cp, name, v):
                    k6 = {("cum", 0): 0, ("cum", 1): 1, ("c2", 0): 2,
                          ("c2", 1): 3, ("c1", 0): 4, ("c1", 1): 5}[(name, v)]
                    return (rows6[cp][0:1, k6 * 128:(k6 + 1) * 128], 0)

                def bc_row(e, pslice, name, v, cp, start=True, stop=True):
                    r_, b_ = rowv_(cp, name, v)
                    return e.matmul(pslice, ones_t[b_:b_ + 1, :], r_, start=start, stop=stop)

                def bc_col(e, pslice, name, v, cp, start=True, stop=True):
                    r_, b_ = rowv_(cp, name, v)
                    return e.matmul(pslice, r_, ones_t[b_:b_ + 1, :], start=start, stop=stop)

                if SCAN_LVL < 14:
                    chunk_ten_end[cp] = (st, st.n)
                    continue
                t_cC = step(TEN, st, [t_rows], lambda e, cp=cp: (
                    bc_row(e, P[5][:, 128:256], "cum", 0, cp),
                    bc_row(e, P[5][:, 256:384], "cum", 1, cp))[-1])
                t_lamc = step(SCA, ss, [t_cC], lambda e, cp=cp: (
                    e.activation(out=lamc[cp][:, 0:1], in_=P[5][:, 255:256], func=AF.Exp),
                    e.activation(out=lamc[cp][:, 1:2], in_=P[5][:, 383:384], func=AF.Exp))[-1])
                t_cCc = step(VEC, sv, [t_cC], lambda e, cp=cp: (
                    e.tensor_copy(out=lamc[cp][:, 2:3], in_=P[5][:, 255:256]),
                    e.tensor_copy(out=lamc[cp][:, 3:4], in_=P[5][:, 383:384]))[-1])

                if SCAN_LVL < 15:
                    chunk_ten_end[cp] = (st, st.n)
                    continue
                t_kk = step(TEN, st, [tac, guard], lambda e, par=par, ccol=ccol: (
                    e.matmul(P[3][:, 0:128], act_k[par][:, ccol],
                             act_k[par][:, ccol], start=True, stop=True),
                    e.matmul(P[3][:, 128:256], act_k[par][:, ccol],
                             act_q[par][:, ccol], start=True, stop=True))[-1])
                t_qkb = step(VEC, sv, [t_kk], lambda e, cp=cp:
                             e.tensor_copy(out=qkb[cp][:], in_=P[3][:, 128:256]))

                t_lam = step(SCA, ss, [t_cum], lambda e, cp=cp:
                             e.activation(out=colw[cp][:, 16:18], in_=cums[cp][:, 0:2],
                                          func=AF.Exp))

                if SCAN_LVL < 16:
                    chunk_ten_end[cp] = (st, st.n)
                    continue
                per_v_sb = []
                o_cons_prev = None
                sn_prev = None
                ms_prev = None
                for v in range(VH):
                    t_bA = step(TEN, st, [t_rows, o_cons_prev], lambda e, cp=cp, v=v: (
                        bc_row(e, P[6][:, 128:256], "c2", v, cp),
                        bc_col(e, P[6][:, 256:384], "c1", v, cp))[-1])
                    t_dAa = step(VEC, sv, [t_bA], lambda e, cp=cp:
                        e.tensor_tensor(out=eT[cp][0][:], in0=P[6][:, 256:384], in1=mA[:],
                                        op=ALU.add))
                    t_dA = step(VEC, sv, [t_dAa], lambda e, cp=cp:
                        e.tensor_tensor(out=eT[cp][0][:], in0=P[6][:, 128:256],
                                        in1=eT[cp][0][:], op=ALU.add))
                    t_xA = step(SCA, ss, [t_dA], lambda e, cp=cp:
                                e.activation(out=eT[cp][0][:], in_=eT[cp][0][:], func=AF.Exp))
                    t_A = step(VEC, sv, [t_xA, t_kk], lambda e, cp=cp, v=v:
                               e.tensor_tensor(out=Amat[cp][v][:], in0=eT[cp][0][:],
                                               in1=P[3][:, 0:128], op=ALU.mult))

                    t_bR = step(TEN, st, [t_rows, t_dA], lambda e, cp=cp, v=v: (
                        bc_row(e, P[6][:, 128:256], "c1", v, cp),
                        bc_col(e, P[6][:, 256:384], "c2", v, cp))[-1])
                    t_dRa = step(VEC, sv, [t_bR], lambda e, cp=cp:
                        e.tensor_tensor(out=eT[cp][1][:], in0=P[6][:, 256:384], in1=mR[:],
                                        op=ALU.add))
                    t_dR = step(VEC, sv, [t_dRa], lambda e, cp=cp:
                        e.tensor_tensor(out=eT[cp][1][:], in0=P[6][:, 128:256],
                                        in1=eT[cp][1][:], op=ALU.add))
                    t_xR = step(SCA, ss, [t_dR], lambda e, cp=cp:
                                e.activation(out=eT[cp][1][:], in_=eT[cp][1][:], func=AF.Exp))
                    t_R = step(VEC, sv, [t_xR, t_kk], lambda e, cp=cp, v=v:
                               e.tensor_tensor(out=Rmat[cp][v][:], in0=eT[cp][1][:],
                                               in1=P[3][:, 0:128], op=ALU.mult))

                    t_bF = step(TEN, st, [t_rows, t_dR], lambda e, cp=cp, v=v: (
                        bc_row(e, P[6][:, 128:256], "cum", v, cp),
                        bc_col(e, P[6][:, 256:384], "c2", v, cp))[-1])
                    t_dFa = step(VEC, sv, [t_bF], lambda e, cp=cp:
                        e.tensor_tensor(out=eT[cp][2][:], in0=P[6][:, 256:384], in1=mFt[:],
                                        op=ALU.add))
                    t_dF = step(VEC, sv, [t_dFa], lambda e, cp=cp:
                        e.tensor_tensor(out=eT[cp][2][:], in0=P[6][:, 128:256],
                                        in1=eT[cp][2][:], op=ALU.add))
                    t_xF = step(SCA, ss, [t_dF], lambda e, cp=cp:
                                e.activation(out=eT[cp][2][:], in_=eT[cp][2][:], func=AF.Exp))
                    t_F = step(VEC, sv, [t_xF, t_qkb], lambda e, cp=cp, v=v:
                               e.tensor_tensor(out=Ftm[cp][v][:], in0=eT[cp][2][:],
                                               in1=qkb[cp][:], op=ALU.mult))

                    if SCAN_LVL < 17:
                        continue
                    t_rhsa = step(VEC, sv, [t_lam, t_bet, t_rn, t_trc, t_cCc, guard],
                                  lambda e, cp=cp, v=v: (
                        e.tensor_tensor(out=colw[cp][:, 18:19], in0=colw[cp][:, 14 + v:15 + v],
                                        in1=colw[cp][:, 16 + v:17 + v], op=ALU.mult),
                        e.tensor_scalar(out=rhs0[cp][v][:, DK:DK + DV],
                                        in0=v_t[cp][:, v * DV:(v + 1) * DV],
                                        scalar1=colw[cp][:, 14 + v:15 + v],
                                        scalar2=None, op0=ALU.mult),
                        e.tensor_tensor(out=colw[cp][:, 19:20], in0=lamc[cp][:, 2 + v:3 + v],
                                        in1=cums[cp][:, v:v + 1], op=ALU.subtract))[-1])
                    t_rhsb = step(VEC, sv, [t_rhsa], lambda e, cp=cp, v=v:
                        e.tensor_tensor(out=colw[cp][:, 18:19], in0=colw[cp][:, 18:19],
                                        in1=colw[cp][:, 5:6], op=ALU.mult))
                    t_rhs = step(VEC, sv, [t_rhsb], lambda e, cp=cp, v=v:
                        e.tensor_scalar(out=rhs0[cp][v][:, 0:DK], in0=kv_t[cp][:, 64:128],
                                        scalar1=colw[cp][:, 18:19], scalar2=None, op0=ALU.mult))
                    t_kbe = step(SCA, ss, [t_rhs], lambda e, cp=cp:
                                 e.activation(out=colw[cp][:, 20:21], in_=colw[cp][:, 19:20],
                                              func=AF.Exp))
                    t_kba = step(VEC, sv, [t_kbe], lambda e, cp=cp, v=v:
                        e.tensor_tensor(out=colw[cp][:, 21:22], in0=colw[cp][:, 20:21],
                                        in1=colw[cp][:, 5:6], op=ALU.mult))
                    t_kb = step(VEC, sv, [t_kba], lambda e, cp=cp, v=v:
                        e.tensor_scalar(out=kbar[cp][v][:], in0=kv_t[cp][:, 64:128],
                                        scalar1=colw[cp][:, 21:22], scalar2=None,
                                        op0=ALU.mult))

                    if SCAN_LVL < 18:
                        continue
                    xt_cur, tok_x = rhs0[cp][v], t_rhs
                    a_cur, r_cur = Amat[cp][v], Rmat[cp][v]
                    tok_a, tok_r = t_A, t_R
                    xi = 0
                    ai = 0
                    p = 1
                    sign = -1.0
                    while p < C:
                        t_mx = step(TEN, st, [tok_r, tok_x], lambda e, r_cur=r_cur, xt_cur=xt_cur:
                                    e.matmul(P[0][:, 0:192], r_cur[:], xt_cur[:],
                                             start=True, stop=True))
                        xt_new = Xs[cp][xi]
                        tok_x = step(VEC, sv, [t_mx], lambda e, xt_cur=xt_cur, xt_new=xt_new,
                                     sign=sign:
                                     e.scalar_tensor_tensor(out=xt_new[:], in0=P[0][:, 0:192],
                                                            scalar=sign, in1=xt_cur[:],
                                                            op0=ALU.mult, op1=ALU.add))
                        xt_cur = xt_new
                        xi ^= 1
                        sign = 1.0
                        if p * 2 < C:
                            t_sq2 = step(TEN, st, [tok_a, tok_r, t_mx],
                                         lambda e, a_cur=a_cur, r_cur=r_cur: (
                                e.matmul(P[1][:, 0:128], r_cur[:], a_cur[:],
                                         start=True, stop=True),
                                e.matmul(P[2][:, 0:128], a_cur[:], r_cur[:],
                                         start=True, stop=True))[-1])
                            a_new, r_new = Apow[cp][ai], Rpow[cp][ai]
                            tok_a = step(VEC, sv, [t_sq2], lambda e, a_new=a_new:
                                         e.tensor_copy(out=a_new[:], in_=P[1][:, 0:128]))
                            tok_r = step(VEC, sv, [t_sq2], lambda e, r_new=r_new:
                                         e.tensor_copy(out=r_new[:], in_=P[2][:, 0:128]))
                            a_cur, r_cur = a_new, r_new
                            ai ^= 1
                        p *= 2
                    Xfin, t_solve = xt_cur, tok_x

                    if SCAN_LVL < 19:
                        continue
                    t_wtt = step(TEN, st, [t_solve], lambda e, Xfin=Xfin:
                                 e.transpose(PT[0:64, 512:640], Xfin[:, 0:DK], idb[:]))
                    t_wtc = step(VEC, sv, [t_wtt], lambda e, cp=cp:
                                 e.tensor_copy(out=wtT[cp][:], in_=PT[0:64, 512:640]))
                    t_wst = step(TEN, st, [t_wtc, chunk_sb], lambda e, cp=cp, v=v:
                                 e.matmul(P[4][:, 0:128], wtT[cp][:], S_b[v][:],
                                          start=True, stop=True))
                    t_u = step(VEC, sv, [t_wst], lambda e, cp=cp, v=v, Xfin=Xfin:
                               e.scalar_tensor_tensor(out=uu[cp][v][:], in0=P[4][:, 0:128],
                                                      scalar=-1.0, in1=Xfin[:, DK:DK + DV],
                                                      op0=ALU.mult, op1=ALU.add))

                    t_omm = step(TEN, st, [t_u, t_F, chunk_sb, tac, o_cons_prev],
                                 lambda e, cp=cp, v=v, par=par, ccol=ccol: (
                        e.matmul(P[6][:, 0:128], act_q[par][:, ccol], S_b[v][:],
                                 start=True, stop=True),
                        e.matmul(P[6][:, 128:256], Ftm[cp][v][:], uu[cp][v][:],
                                 start=True, stop=True))[-1])

                    t_smm = step(TEN, st, [t_u, t_kb, t_omm, sn_prev], lambda e, cp=cp, v=v:
                                 e.matmul(P[4][0:64, 128:256], kbar[cp][v][:], uu[cp][v][:],
                                          start=True, stop=True))

                    Sf_old, Sf_new = S_f[v][cc % 2], S_f[v][(cc + 1) % 2]
                    t_sn = step(VEC, sv, [t_smm, t_lamc], lambda e, cp=cp, v=v,
                                Sf_old=Sf_old, Sf_new=Sf_new:
                                e.scalar_tensor_tensor(out=Sf_new[:], in0=Sf_old[:],
                                                       scalar=lamc[cp][0:64, v:v + 1],
                                                       in1=P[4][0:64, 128:256],
                                                       op0=ALU.mult, op1=ALU.add))
                    t_sb2 = step(VEC, sv, [t_sn, t_omm, t_wst], lambda e, v=v, Sf_new=Sf_new:
                                 e.tensor_copy(out=S_b[v][:], in_=Sf_new[:]))
                    per_v_sb.append(t_sb2)
                    sn_prev = t_sn

                    t_oa = step(VEC, sv, [t_omm, t_lam, t_rn], lambda e, cp=cp, v=v:
                        e.tensor_scalar(out=o_w[cp][:, v * DV:(v + 1) * DV],
                                        in0=P[6][:, 0:128],
                                        scalar1=colw[cp][:, 16 + v:17 + v],
                                        scalar2=None, op0=ALU.mult))
                    t_ob = step(VEC, sv, [t_oa], lambda e, cp=cp, v=v:
                        e.tensor_tensor(out=o_w[cp][:, v * DV:(v + 1) * DV],
                                        in0=P[6][:, 128:256],
                                        in1=o_w[cp][:, v * DV:(v + 1) * DV], op=ALU.add))
                    t_o = step(VEC, sv, [t_ob], lambda e, cp=cp, v=v:
                        e.tensor_scalar(out=o_w[cp][:, v * DV:(v + 1) * DV],
                                        in0=o_w[cp][:, v * DV:(v + 1) * DV],
                                        scalar1=colw[cp][:, 4:5], scalar2=SCALE,
                                        op0=ALU.mult, op1=ALU.mult))
                    o_cons_prev = t_o

                    t_osq = step(SCA, ss, [t_o, ms_prev], lambda e, cp=cp, v=v:
                                 e.activation(out=sqw[cp][:], in_=o_w[cp][:, v * DV:(v + 1) * DV],
                                              func=AF.Square))
                    t_ms2a = step(VEC, sv, [t_osq], lambda e, cp=cp:
                        e.reduce_sum(out=colw[cp][:, 22:23], in_=sqw[cp][:, 0:DV], axis=AX.X))
                    t_ms2 = step(VEC, sv, [t_ms2a], lambda e, cp=cp:
                        e.tensor_scalar(out=colw[cp][:, 22:23], in0=colw[cp][:, 22:23],
                                        scalar1=1.0 / DV, scalar2=1e-6,
                                        op0=ALU.mult, op1=ALU.add))
                    ms_prev = t_ms2
                    t_rmq = step(SCA, ss, [t_ms2], lambda e, cp=cp:
                                 e.activation(out=colw[cp][:, 23:24], in_=colw[cp][:, 22:23],
                                              func=AF.Sqrt))
                    t_rr = step(VEC, sv, [t_rmq], lambda e, cp=cp:
                                e.reciprocal(out=colw[cp][:, 22:23], in_=colw[cp][:, 23:24]))
                    t_sz = step(SCA, ss, [tmx, t_osq], lambda e, cp=cp, v=v, zr=zr:
                                e.activation(out=zs_t[cp][:, v * DV:(v + 1) * DV],
                                             in_=zr[:, v * DV:(v + 1) * DV], func=AF.Silu))
                    zsil_tok[cc % 8] = t_sz
                    t_gta = step(VEC, sv, [t_rr, t_sz, tok_nw], lambda e, cp=cp, v=v:
                        e.tensor_scalar(out=o_w[cp][:, v * DV:(v + 1) * DV],
                                        in0=o_w[cp][:, v * DV:(v + 1) * DV],
                                        scalar1=colw[cp][:, 22:23], scalar2=None, op0=ALU.mult))
                    t_gtb = step(VEC, sv, [t_gta], lambda e, cp=cp, v=v:
                        e.tensor_tensor(out=o_w[cp][:, v * DV:(v + 1) * DV],
                                        in0=o_w[cp][:, v * DV:(v + 1) * DV],
                                        in1=nw_t[:], op=ALU.mult))
                    t_gt = step(VEC, sv, [t_gtb], lambda e, cp=cp, v=v:
                        e.tensor_tensor(out=gat[cp][:, v * DV:(v + 1) * DV],
                                        in0=o_w[cp][:, v * DV:(v + 1) * DV],
                                        in1=zs_t[cp][:, v * DV:(v + 1) * DV],
                                        op=ALU.mult))

                    t_gtt = step(TEN, st, [t_gt, gs_dma[-1] if (cc % 16 == 0 and len(gs_dma))
                                           else None],
                                 lambda e, cp=cp, v=v:
                                 e.transpose(PT[:, 640:768], gat[cp][:, v * DV:(v + 1) * DV],
                                             idb[:]))
                    t_gsc = step(VEC, sv, [t_gtt], lambda e, cp=cp, v=v, cc=cc:
                                 e.tensor_copy(out=gstrip[v][:, (cc % 16) * 128:(cc % 16 + 1) * 128],
                                               in_=PT[:, 640:768]))
                    gs_cp[v] = t_gsc

                if per_v_sb:
                    chunk_sb = per_v_sb[-1]
                chunk_ten_end[cp] = (st, st.n)

                if stage == "B1" and cc == 0 and TAP == "canary":
                    tcan = step(VEC, sv, [(st, st.n), (ss, ss.n)], lambda e:
                                e.memset(eT[0][0][:], 3.25))
                    tap_tok = step(SYNC, sd, [tcan], lambda e:
                                   e.dma_start(out=y_seg[0:128, 0:128],
                                               in_=eT[0][0][:, 0:128]), k=16)
                    tap_tok2 = step(GPS, sg, [tcan], lambda e:
                                    e.dma_start(out=y_seg[128:256, 0:128],
                                                in_=eT[0][0][:, 0:128]), k=16)
                    chunk_ten_end[cp] = tap_tok
                    chunk_sb = tap_tok
                elif stage == "B1" and cc == 0 and TAP:
                    def tapf(e, cp=cp):
                        tiles = []
                        if TAP == "act":
                            tiles = [(act_q[0][:], 0), (act_k[0][:], 64),
                                     (act_v0[0][:, 0:512], 128), (act_v1[0][:, 0:512], 192)]
                        elif TAP == "vecs":
                            tiles = [(cums[cp][:], 0), (colw[cp][:], 8), (lamc[cp][:], 40),
                                     (kv_t[cp][:].bitcast(BF16), 48),
                                     (gsp[cp][:].bitcast(BF16), 192)]
                        elif TAP == "AR":
                            tiles = [(Amat[cp][0][:], 0), (Rmat[cp][0][:], 64),
                                     (Ftm[cp][0][:], 128), (qkb[cp][:], 192)]
                        elif TAP == "XU":
                            tiles = [(rhs0[cp][0][:], 0), (uu[cp][0][:], 96),
                                     (kbar[cp][0][:], 160), (S_b[0][:], 192)]
                        elif TAP == "cd":
                            tiles = [(cd_q[:], 0), (cd_k[:], 128)]
                        elif TAP == "mask":
                            tiles = [(trb[:], 0), (idb[:], 128)]
                        elif TAP == "ws":
                            tiles = [(wstage[0][:, 0:1024].bitcast(BF16), 0)]
                        elif TAP == "wq":
                            tiles = [(wqkvT[0][:], 0), (wzT[0][:], 128)]
                        elif TAP == "mix":
                            tiles = [(mix_qk[:, 0:512], 0)]
                        elif TAP == "O":
                            tiles = [(o_w[cp][:].bitcast(BF16), 0), (gat[cp][:], 64),
                                     (zs_t[cp][:], 128)]
                        last = None
                        for ap_, r0 in tiles:
                            pr, fr = ap_.shape[0], ap_.shape[1]
                            dst = dbg[r0:r0 + min(pr, 256 - r0), 0:fr]
                            if str(ap_.dtype) != "dt.bfloat16":
                                ap_ = ap_.bitcast(BF16)
                                dst = dbg[r0:r0 + min(pr, 256 - r0), 0:ap_.shape[1]]
                            last = e.dma_start(out=dst, in_=ap_).then_inc(sg.h, 16)
                        return None
                    ntap = {"act": 4, "vecs": 5, "AR": 4, "XU": 4, "O": 3,
                            "cd": 2, "mix": 1, "ws": 1, "wq": 2, "mask": 2}[TAP]
                    tap_tok = step(GPS, sg, [(sv, sv.n), (ss, ss.n), (st, st.n)],
                                   tapf, k=16 * ntap)
                    chunk_ten_end[cp] = tap_tok
                    chunk_sb = tap_tok

                if (cc % 16 == 15 or cc == NBLK * 4 - 1) and SCAN_LVL >= 19:
                    c0 = (cc // 16) * 16 * 128
                    wgt = (cc % 16 + 1) * 128
                    tgd = step(GPS, sg, [gs_cp[0], gs_cp[1]],
                               lambda e, c0=c0, wgt=wgt: (
                        e.dma_start(out=cc_gt_in[0:128, c0:c0 + wgt],
                                    in_=gstrip[0][:, 0:wgt]).then_inc(sg.h, 16),
                        e.dma_start(out=cc_gt_in[128:256, c0:c0 + wgt],
                                    in_=gstrip[1][:, 0:wgt]).then_inc(sg.h, 16),
                        None)[-1], k=32)
                    gs_dma.append(tgd)

        if stage in ("B1", "B"):
            step(SYNC, sd, [tok_ms], lambda e:
                 e.dma_start(out=y_seg[1920:2048, 0:128], in_=ones_t[:]), k=16)
            step(SYNC, sd, [tok_ag12], lambda e:
                 e.dma_start(out=y_seg[1920:2048, 128:256], in_=ones_t[:]), k=16)
            step(SYNC, sd, [ip_tok[0] or tok_ms], lambda e:
                 e.dma_start(out=y_seg[1920:2048, 256:384], in_=ones_t[:]), k=16)
            step(SYNC, sd, [act_tok[0] or tok_ms], lambda e:
                 e.dma_start(out=y_seg[1920:2048, 384:512], in_=ones_t[:]), k=16)
            step(SYNC, sd, [chunk_ten_end[0]], lambda e:
                 e.dma_start(out=y_seg[1920:2048, 512:640], in_=ones_t[:]), k=16)
            step(SYNC, sd, [(sv, sv.n)], lambda e:
                 e.dma_start(out=y_seg[1920:2048, 640:768], in_=ones_t[:]), k=16)
            step(SYNC, sd, [(ss, ss.n)], lambda e:
                 e.dma_start(out=y_seg[1920:2048, 768:896], in_=ones_t[:]), k=16)
            step(SYNC, sd, [(st, st.n)], lambda e:
                 e.dma_start(out=y_seg[1920:2048, 896:1024], in_=ones_t[:]), k=16)
            def dump(e):
                cpd = 1
                items = [
                    (cums[cpd][:], 256, 0, F32),
                    (colw[cpd][:], 256, 16, F32),
                    (lamc[cpd][:], 256, 64, F32),
                    (kv_t[cpd][:], 256, 100, BF16),
                    (v_t[cpd][:], 256, 200, BF16),
                    (Amat[cpd][0][:], 384, 0, BF16),
                    (Rmat[cpd][0][:], 384, 100, BF16),
                    (Ftm[cpd][0][:], 384, 200, BF16),
                    (rhs0[cpd][0][:], 512, 0, BF16),
                    (uu[cpd][0][:], 512, 200, BF16),
                    (kbar[cpd][0][:], 512, 400, BF16),
                    (o_w[cpd][:], 640, 0, F32),
                    (gat[cpd][:], 640, 300, BF16),
                    (S_b[0][:], 768, 0, BF16),
                    (gsp[cpd][:], 768, 200, BF16),
                    (ba_ring[3][:], 768, 300, F32),
                    (z_ring[3][:], 896, 0, BF16),
                    (smc[:], 896, 200, F32),
                    (dtb_t[:, 0:2] if False else smc[:, 0:2], 896, 220, F32),
                ]
                last = None
                for ap_, r0, c0, dt_ in items:
                    pr, fr = ap_.shape[0], ap_.shape[1]
                    if dt_ == F32:
                        dst = y_seg[r0:r0 + pr, c0:c0 + fr]
                    else:
                        dst = y_seg[r0:r0 + pr, c0:c0 + (fr + 1) // 2].bitcast(BF16)[:, 0:fr]
                    last = e.dma_start(out=dst, in_=ap_).then_inc(sd.h, 16)
                return None
            tre = step(VEC, sv, [(st, st.n), (ss, ss.n), (sg, sg.n)], lambda e:
                       e.tensor_tensor(out=colw[1][:, 8:10], in0=ba_ring[3][:, 2:4],
                                       in1=smc[:, 0:2], op=ALU.add))
            step(SYNC, sd, [(sv, sv.n), (ss, ss.n), (st, st.n), (sg, sg.n), tre],
                 dump, k=16 * 19)
            lastg = gs_dma[-1] if gs_dma else None
            tdbg = step(SYNC, sd, [lastg, gs_cp[0], gs_cp[1],
                                   (sv, sv.n), (ss, ss.n), (st, st.n)], lambda e:
                        e.dma_start(out=y_seg[0:256, 0:1024].bitcast(BF16)[:, 0:2048],
                                    in_=cc_gt_in[:, 0:2048]), k=16)
            GPS.append(lambda e: e.wait_ge(sg.h, sg.n))
            _finish(block, SYNC, VEC, SCA, TEN, GPS)
            return nc

        tok_ag3 = step(GPS, scc, [gs_dma[-1]], lambda e:
                       e.collective_compute("AllGather", ALU.bypass,
                                            replica_groups=[list(range(NCORES))],
                                            ins=[cc_gt_in[:].opt()], outs=[cc_gt_out[:].opt()]))

        es_b.close()
        gt_t = [sb(f"gt_t{k}", [128, 128], BF16) for k in range(16)]
        xc_t = sb("xc_t", [128, D])
        at_t = sb("at_t", [128, D])
        x2_t = sb("x2_t", [128, D])
        f_t = sb("f_t", [128, D])
        d1qT = sb("d1qT", [128, D], FP8)
        wout_r = [sb(f"wout_r{k}", [128, D], BF16) for k in range(16)]
        q2_r = [sb(f"q2_r{k}", [128, D], FP8) for k in range(16)]

        wl_prev = None
        for kt in range(16):
            tw = step(SYNC, sd, [tok_ag3 if kt == 0 else wl_prev,
                                 ip_tok[0], ip_tok[1]], lambda e, kt=kt:
                      e.dma_start(out=wstage[kt % 2][:, 0:D],
                                  in_=w_outT[kt * 128:(kt + 1) * 128, :]), k=16)
            wl_prev = step(VEC, sv, [tw], lambda e, kt=kt:
                           e.tensor_copy(out=wout_r[kt][:], in_=wstage[kt % 2][:, 0:D]))
        tok_wout = wl_prev

        def q2v(kt, half):
            return q2_r[kt][:].bitcast(FP8)[:, half * 1024:(half + 1) * 1024]
        tq2 = step(SYNC, sd, [tok_ag3, ip_tok[0], ip_tok[1]], lambda e: (
            [e.dma_start(out=q2_r[kt][:],
                         in_=cc_q2_out[kt * 128:(kt + 1) * 128, :]).then_inc(sd.h, 16)
             for kt in range(16)],
            None)[-1], k=16 * 16)

        y_dma = []
        prev_ffn = None
        prev_fe = None
        prev_tgl = None
        for i in range(16):
            tof = step(VEC, sv, [tok_cl, prev_tgl], lambda e, i=i: (
                [e.tensor_scalar(out=gt_o[kt][:], in0=gto[:, kt:kt + 1],
                                 scalar1=i * 128, scalar2=None, op0=ALU.add)
                 for kt in range(16)])[-1])
            tgl = step(GPS, sg, [tof, tok_ag3, y_dma[i - 1] if i else None,
                                 tok_wout, tq2], lambda e: (
                [e.indirect_dma_start(out=gt_t[kt][:], out_offset=None,
                                      in_=cc_gt_out[0:128, 0:128],
                                      in_offset=IndirectOffsetOnAxis(ap=gt_o[kt][:], axis=1)
                                      ).then_inc(sg.h, 16)
                 for kt in range(16)],
                None)[-1], k=16 * 16)
            prev_tgl = tgl
            txl = step(SYNC, sd, [tgl], lambda e, i=i:
                       e.dma_start(out=xc_t[:], in_=x_seg[i * 128:(i + 1) * 128, :]), k=16)

            t_op = step(TEN, st, [tgl, tok_wout, prev_ffn, prev_fe,
                                  (sv, sv.n) if i == 0 else None,
                                  (ss, ss.n) if i == 0 else None], lambda e: (
                [[e.matmul(P[j][:], gt_t[kt][:], wout_r[kt][:, j * 512:(j + 1) * 512],
                           start=(kt == 0), stop=(kt == 15)) for kt in range(16)]
                 for j in range(4)])[-1][-1])
            t_ope = step(VEC, sv, [t_op], lambda e: (
                [e.tensor_copy(out=at_t[:, j * 512:(j + 1) * 512], in_=P[j][:])
                 for j in range(4)])[-1])

            t1a = step(VEC, sv, [t_ope], lambda e:
                       e.reduce_sum(out=acol[:, 0:1], in_=at_t[:], axis=AX.X))
            t1 = step(VEC, sv, [t1a], lambda e:
                      e.tensor_scalar(out=acol[:, 1:2], in0=acol[:, 0:1], scalar1=1.0 / D,
                                      scalar2=None, op0=ALU.mult))
            t2 = step(VEC, sv, [t1], lambda e:
                      e.tensor_scalar(out=at_t[:], in0=at_t[:], scalar1=acol[:, 1:2],
                                      scalar2=None, op0=ALU.subtract))
            t3 = step(SCA, ss, [t2], lambda e:
                      e.activation(out=sqtile[:], in_=at_t[:], func=AF.Square))
            t4a = step(VEC, sv, [t3], lambda e:
                       e.reduce_sum(out=acol[:, 2:3], in_=sqtile[:], axis=AX.X))
            t4 = step(VEC, sv, [t4a], lambda e:
                      e.tensor_scalar(out=acol[:, 3:4], in0=acol[:, 2:3], scalar1=1.0 / D,
                                      scalar2=1e-5, op0=ALU.mult, op1=ALU.add))
            t5 = step(SCA, ss, [t4], lambda e:
                      e.activation(out=acol[:, 4:5], in_=acol[:, 3:4], func=AF.Sqrt))
            t6 = step(VEC, sv, [t5], lambda e:
                      e.reciprocal(out=acol[:, 5:6], in_=acol[:, 4:5]))
            t7a = step(VEC, sv, [t6, txl], lambda e:
                       e.tensor_scalar(out=at_t[:], in0=at_t[:], scalar1=acol[:, 5:6],
                                       scalar2=None, op0=ALU.mult))
            t7 = step(VEC, sv, [t7a], lambda e:
                      e.tensor_tensor(out=x2_t[:], in0=xc_t[:], in1=at_t[:], op=ALU.add))

            t8a = step(VEC, sv, [t7], lambda e:
                       e.reduce_sum(out=acol[:, 0:1], in_=x2_t[:], axis=AX.X))
            t8 = step(VEC, sv, [t8a], lambda e:
                      e.tensor_scalar(out=acol[:, 1:2], in0=acol[:, 0:1], scalar1=1.0 / D,
                                      scalar2=None, op0=ALU.mult))
            t9 = step(VEC, sv, [t8], lambda e:
                      e.tensor_scalar(out=at_t[:], in0=x2_t[:], scalar1=acol[:, 1:2],
                                      scalar2=None, op0=ALU.subtract))
            t10 = step(SCA, ss, [t9], lambda e:
                       e.activation(out=sqtile[:], in_=at_t[:], func=AF.Square))
            t11a = step(VEC, sv, [t10], lambda e:
                        e.reduce_sum(out=acol[:, 2:3], in_=sqtile[:], axis=AX.X))
            t11 = step(VEC, sv, [t11a], lambda e:
                       e.tensor_scalar(out=acol[:, 3:4], in0=acol[:, 2:3], scalar1=1.0 / D,
                                       scalar2=1e-5, op0=ALU.mult, op1=ALU.add))
            t12 = step(SCA, ss, [t11], lambda e:
                       e.activation(out=acol[:, 4:5], in_=acol[:, 3:4], func=AF.Sqrt))
            t13 = step(VEC, sv, [t12], lambda e:
                       e.reciprocal(out=acol[:, 5:6], in_=acol[:, 4:5]))
            t14 = step(VEC, sv, [t13], lambda e:
                       e.tensor_scalar(out=at_t[:], in0=at_t[:], scalar1=acol[:, 5:6],
                                       scalar2=None, op0=ALU.mult))

            tqz = t14
            for kt in range(16):
                tt = step(TEN, st, [tqz], lambda e, kt=kt:
                          e.transpose(P[4][:, 0:128].bitcast(F32),
                                      at_t[:, kt * 128:(kt + 1) * 128], idf[:]))
                tqz = step(VEC, sv, [tt, tok_smc2], lambda e, kt=kt:
                           e.tensor_scalar(out=d1qT[:, kt * 128:(kt + 1) * 128],
                                           in0=P[4][:, 0:128], scalar1=smc[:, 4:5],
                                           scalar2=None, op0=ALU.mult))

            t_fm = step(TEN, st, [tqz, tq2], lambda e: (
                [[e.matmul(P[j][:], d1qT[:, kt * 128:(kt + 1) * 128],
                           q2_r[kt][:, j * 512:(j + 1) * 512],
                           start=(kt == 0), stop=(kt == 15)) for kt in range(16)]
                 for j in range(4)])[-1][-1])
            prev_ffn = t_fm
            t_fe = step(VEC, sv, [t_fm], lambda e: (
                [e.tensor_scalar(out=f_t[:, j * 512:(j + 1) * 512], in0=P[j][:],
                                 scalar1=smc[:, 6:7], scalar2=None, op0=ALU.mult)
                 for j in range(4)])[-1])

            u1a = step(VEC, sv, [t_fe], lambda e:
                       e.reduce_sum(out=acol[:, 0:1], in_=f_t[:], axis=AX.X))
            u1 = step(VEC, sv, [u1a], lambda e:
                      e.tensor_scalar(out=acol[:, 1:2], in0=acol[:, 0:1], scalar1=1.0 / D,
                                      scalar2=None, op0=ALU.mult))
            u2 = step(VEC, sv, [u1], lambda e:
                      e.tensor_scalar(out=f_t[:], in0=f_t[:], scalar1=acol[:, 1:2],
                                      scalar2=None, op0=ALU.subtract))
            u3 = step(SCA, ss, [u2], lambda e:
                      e.activation(out=sqtile[:], in_=f_t[:], func=AF.Square))
            u4a = step(VEC, sv, [u3], lambda e:
                       e.reduce_sum(out=acol[:, 2:3], in_=sqtile[:], axis=AX.X))
            u4 = step(VEC, sv, [u4a], lambda e:
                      e.tensor_scalar(out=acol[:, 3:4], in0=acol[:, 2:3], scalar1=1.0 / D,
                                      scalar2=1e-5, op0=ALU.mult, op1=ALU.add))
            u5 = step(SCA, ss, [u4], lambda e:
                      e.activation(out=acol[:, 4:5], in_=acol[:, 3:4], func=AF.Sqrt))
            u6 = step(VEC, sv, [u5], lambda e:
                      e.reciprocal(out=acol[:, 5:6], in_=acol[:, 4:5]))
            u7a = step(VEC, sv, [u6], lambda e:
                       e.tensor_scalar(out=f_t[:], in0=f_t[:], scalar1=acol[:, 5:6],
                                       scalar2=None, op0=ALU.mult))
            u7 = step(VEC, sv, [u7a], lambda e:
                      e.tensor_tensor(out=f_t[:], in0=f_t[:], in1=x2_t[:], op=ALU.add))

            ty = step(GPS, sg, [u7], lambda e, i=i:
                      e.dma_start(out=y_seg[i * 128:(i + 1) * 128, :], in_=f_t[:]), k=16)
            y_dma.append(ty)
            prev_fe = t_fe

        GPS.append(lambda e: e.wait_ge(sg.h, sg.n))
        _finish(block, SYNC, VEC, SCA, TEN, GPS)
    return nc


def _finish(block, SYNC, VEC, SCA, TEN, GPS):
    @block.sync
    def _(e):
        for f in SYNC:
            f(e)

    @block.vector
    def _(e):
        for f in VEC:
            f(e)

    @block.scalar
    def _(e):
        for f in SCA:
            f(e)

    @block.tensor
    def _(e):
        for f in TEN:
            f(e)

    @block.gpsimd
    def _(e):
        for f in GPS:
            f(e)


_NC_CACHE = {}


def _get_nc():
    if STAGE not in _NC_CACHE:
        _NC_CACHE[STAGE] = build_nc(STAGE)
    return _NC_CACHE[STAGE]


def make_in_maps(inputs):
    x = np.ascontiguousarray(np.asarray(inputs["L0_attn_x"], np.float32).reshape(T, D))
    wip = np.asarray(inputs["L0_attn_in_proj_w"], np.float32)
    convw = np.asarray(inputs["L0_attn_conv_w"], np.float32)
    dt_bias = np.asarray(inputs["L0_attn_dt_bias"], np.float32)
    a_log = np.asarray(inputs["L0_attn_A_log"], np.float32)
    nw = np.asarray(inputs["L0_attn_norm_weight"], np.float32)
    wout = np.asarray(inputs["L0_attn_out_proj_w"], np.float32)
    w2 = np.asarray(inputs["L0_ffn_input2"], np.float32)
    scl = np.asarray(inputs["L0_ffn_scale"], np.float32).reshape(1, 1)
    scl1 = np.asarray(inputs["L0_ffn_scale1"], np.float32).reshape(1, 1)

    ii = np.arange(C)
    mA_ = np.where(ii[None, :] < ii[:, None], 0.0, NEG).astype(np.float32)
    mR_ = np.where(ii[None, :] > ii[:, None], 0.0, NEG).astype(np.float32)
    mF_ = np.where(ii[None, :] >= ii[:, None], 0.0, NEG).astype(np.float32)
    tri = (ii[:, None] <= ii[None, :]).astype(ml_dtypes.bfloat16)
    idb_ = np.eye(128, dtype=ml_dtypes.bfloat16)
    idf_ = np.eye(128, dtype=np.float32)

    in_maps = []
    for c_ in range(NCORES):
        kh = c_
        q_rows = wip[kh * DK:(kh + 1) * DK]
        k_rows = wip[512 + kh * DK:512 + (kh + 1) * DK]
        v_rows = wip[1024 + c_ * 256:1024 + (c_ + 1) * 256]
        z_rows = wip[3072 + c_ * 256:3072 + (c_ + 1) * 256]
        b_rows = wip[5120 + 2 * c_:5120 + 2 * c_ + 2]
        a_rows = wip[5136 + 2 * c_:5136 + 2 * c_ + 2]
        w_qkv = np.concatenate([q_rows, k_rows, v_rows], axis=0)
        w_ba = np.concatenate([b_rows, a_rows,
                               np.zeros((12, D), np.float32)], axis=0)

        cw_q = convw[kh * DK:(kh + 1) * DK]
        cw_k = convw[512 + kh * DK:512 + (kh + 1) * DK]
        cw_v = convw[1024 + c_ * 256:1024 + (c_ + 1) * 256]
        dq = np.zeros((128, KS * DK), np.float32)
        dk_ = np.zeros((128, KS * DK), np.float32)
        dv0 = np.zeros((128, KS * 128), np.float32)
        dv1 = np.zeros((128, KS * 128), np.float32)
        for tap in range(KS):
            dq[0:DK, tap * DK:(tap + 1) * DK] = np.diag(cw_q[:, tap])
            dk_[DK:128, tap * DK:(tap + 1) * DK] = np.diag(cw_k[:, tap])
            dv0[:, tap * 128:(tap + 1) * 128] = np.diag(cw_v[0:128, tap])
            dv1[:, tap * 128:(tap + 1) * 128] = np.diag(cw_v[128:256, tap])

        smalls = np.zeros((1, 192), np.float32)
        smalls[0, 0:2] = dt_bias[2 * c_:2 * c_ + 2]
        smalls[0, 2:4] = a_log[2 * c_:2 * c_ + 2]
        smalls[0, 4:5] = scl[0]
        smalls[0, 5:6] = scl1[0]
        smalls[0, 64:64 + DV] = nw

        in_maps.append({
            "x_seg": np.ascontiguousarray(x[c_ * SEG:(c_ + 1) * SEG]),
            "w_qkvT": np.ascontiguousarray(w_qkv.T),
            "w_zT": np.ascontiguousarray(z_rows.T),
            "w_baT": np.ascontiguousarray(w_ba.T),
            "w_outT": np.ascontiguousarray(wout.T),
            "w2_slice": np.ascontiguousarray(w2[c_ * 256:(c_ + 1) * 256]),
            "convd_q": dq, "convd_k": dk_, "convd_v0": dv0, "convd_v1": dv1,
            "smalls": smalls,
            "maskA": mA_, "maskR": mR_, "maskFt": mF_,
            "triu_b": tri, "ident_b": idb_, "ident_f": idf_,
            "gtoff": (np.arange(16)[None, :] * 128 * T
                      + np.arange(128)[:, None] * T + 2048 * c_).astype(np.int32),
        })
    return in_maps


def kernel(**inputs):
    nc = _get_nc()
    in_maps = make_in_maps(inputs)
    res = run_bass_kernel_spmd(nc, in_maps, core_ids=list(range(NCORES)))
    out = np.concatenate([res.results[c_]["y_seg"] for c_ in range(NCORES)], axis=0)
    return out.reshape(1, T, D)


# version 03
# speedup vs baseline: 1.0076x; 1.0076x over previous
import numpy as np
import ml_dtypes
from contextlib import ExitStack

import concourse.bass as bass
import concourse.mybir as mybir
from concourse.bass_utils import run_bass_kernel_spmd
from concourse.bass_core_rust import IndirectOffsetOnAxis

F32 = mybir.dt.float32
BF16 = mybir.dt.bfloat16
FP8 = mybir.dt.float8e4
AF = mybir.ActivationFunctionType
ALU = mybir.AluOpType
AX = mybir.AxisListType

NCORES = 8
T = 16384
D = 2048
SEG = T // NCORES
C = 128
DK = 64
DV = 128
VH = 2
KS = 4
TB = 512
NTB = T // TB
SCALE = float(DK) ** -0.5
NEG = -1.0e30

STAGE = "ALL"
import os as _os
SCAN_LVL = int(_os.environ.get("SCAN_LVL", "99"))
TAP = _os.environ.get("TAP", "")


def build_nc(stage="ALL"):
    nc = bass.Bass("TRN2", target_bir_lowering=False)

    def din(name, shape, dt=F32):
        return nc.dram_tensor(name, shape, dt, kind="ExternalInput").ap()

    x_seg = din("x_seg", [SEG, D])
    w_qkvT = din("w_qkvT", [D, 384], BF16)
    w_zT = din("w_zT", [D, 256], BF16)
    w_baT = din("w_baT", [D, 16], BF16)
    w_outT = din("w_outT", [D, D], BF16)
    w2_slice = din("w2_slice", [D // NCORES, D])
    convd_q = din("convd_q", [128, KS * DK], BF16)
    convd_k = din("convd_k", [128, KS * DK], BF16)
    convd_v0 = din("convd_v0", [128, KS * 128], BF16)
    convd_v1 = din("convd_v1", [128, KS * 128], BF16)
    smalls = din("smalls", [1, 192])
    maskA = din("maskA", [C, C])
    maskR = din("maskR", [C, C])
    maskFt = din("maskFt", [C, C])
    triu_b = din("triu_b", [C, C], BF16)
    ident_b = din("ident_b", [128, 128], BF16)
    ident_f = din("ident_f", [128, 128], F32)
    gtoff = din("gtoff", [128, 16], mybir.dt.int32)

    y_seg = nc.dram_tensor("y_seg", [SEG, D], F32, kind="ExternalOutput").ap()

    cc_ln_in = nc.dram_tensor("cc_ln_in", [D, SEG], BF16, kind="Internal").ap()
    cc_ln_out = nc.dram_tensor("cc_ln_out", [NCORES * D, SEG], BF16, kind="Internal").ap()
    cc_q2_in = nc.dram_tensor("cc_q2_in", [D // NCORES, D], FP8, kind="Internal").ap()
    cc_q2_out = nc.dram_tensor("cc_q2_out", [D, D], FP8, kind="Internal").ap()
    cc_gt_in = nc.dram_tensor("cc_gt_in", [VH * DV, T], BF16, kind="Internal").ap()
    cc_gt_out = nc.dram_tensor("cc_gt_out", [NCORES * VH * DV, T], BF16, kind="Internal").ap()
    dbg = nc.dram_tensor("dbg", [256, 2048], BF16, kind="ExternalOutput").ap() \
        if stage in ("A", "B1", "B") else None

    es = ExitStack()

    def sb(name, shape, dt=F32):
        return es.enter_context(nc.sbuf_tensor(name, shape, dt))

    def pt(name, shape, dt=F32):
        return es.enter_context(nc.psum_tensor(name, shape, dt))

    SYNC, VEC, SCA, TEN, GPS = [], [], [], [], []
    STEPLOG = []
    nc._steplog = STEPLOG

    class S:
        def __init__(self, name):
            self.h = es.enter_context(nc.semaphore(name))
            self.n = 0

    def step(lst, sem, waits, fn, k=1):
        wl = {}
        for t in waits:
            if t is None:
                continue
            s_, n_ = t
            if n_ > wl.get(s_, -1):
                wl[s_] = n_
        tgt = None
        if sem is not None:
            sem.n += k
            tgt = sem.n

        def closure(e, wl=wl, fn=fn, sem=sem, k=k):
            for s_, n_ in wl.items():
                if n_ > 0:
                    e.wait_ge(s_.h, n_)
            last = fn(e)
            if sem is not None and last is not None:
                last.then_inc(sem.h, k)

        STEPLOG.append((id(lst), {id(s_): n_ for s_, n_ in wl.items()},
                        id(sem) if sem else None, tgt, len(lst)))
        lst.append(closure)
        return (sem, tgt) if sem is not None else None

    with es:
        ones_r = sb("ones_r", [1, 128])
        ones_t = sb("ones_t", [128, 128])
        one_col = sb("one_col", [128, 1])
        mA = sb("mA", [C, C]); mR = sb("mR", [C, C]); mFt = sb("mFt", [C, C])
        trb = sb("trb", [C, C], BF16)
        idb = sb("idb", [128, 128], BF16)
        idf = sb("idf", [128, 128], F32)
        small_row = sb("small_row", [1, 192])
        pbc = sb("pbc", [128, 192])
        nw_t = sb("nw_t", [128, DV])
        smc = sb("smc", [128, 8])

        wstage = [sb(f"wstage{i}", [128, D]) for i in range(2)]
        atile = sb("atile", [128, D])
        sqtile = sb("sqtile", [128, D])
        acol = sb("acol", [128, 8])
        gto = sb("gto", [128, 16], mybir.dt.int32)
        gt_o = [sb(f"gt_o{k}", [128, 1], mybir.dt.int32) for k in range(16)]

        es_b = es.enter_context(ExitStack())

        def sbb(name, shape, dt=F32):
            return es_b.enter_context(nc.sbuf_tensor(name, shape, dt))

        wqkvT = [sbb(f"wqkvT{k}", [128, 384], BF16) for k in range(16)]
        wzT = [sbb(f"wzT{k}", [128, 256], BF16) for k in range(16)]
        wbaT = [sbb(f"wbaT{k}", [128, 16], BF16) for k in range(16)]
        cd_q = sbb("cd_q", [128, KS * DK], BF16)
        cd_k = sbb("cd_k", [128, KS * DK], BF16)
        cd_v0 = sbb("cd_v0", [128, KS * 128], BF16)
        cd_v1 = sbb("cd_v1", [128, KS * 128], BF16)

        lnbf = [sbb(f"lnbf{i}", [128, D], BF16) for i in range(2)]
        tstg = [sbb(f"tstg{i}", [128, 128], BF16) for i in range(2)]

        lnx = [[sbb(f"lnx{i}_{k}", [128, TB], BF16) for k in range(16)] for i in range(2)]
        mix_qk = sbb("mix_qk", [128, 3 + TB], BF16)
        mix_v0 = sbb("mix_v0", [128, 3 + TB], BF16)
        mix_v1 = sbb("mix_v1", [128, 3 + TB], BF16)
        act_q = [sbb(f"act_q{i}", [64, TB], BF16) for i in range(2)]
        act_k = [sbb(f"act_k{i}", [64, TB], BF16) for i in range(2)]
        act_v0 = [sbb(f"act_v0{i}", [128, TB], BF16) for i in range(2)]
        act_v1 = [sbb(f"act_v1{i}", [128, TB], BF16) for i in range(2)]
        z_ring = [sbb(f"z_ring{i}", [128, 256], BF16) for i in range(8)]
        ba_ring = [sbb(f"ba_ring{i}", [128, 4]) for i in range(8)]

        kv_t = [sbb(f"kv_t{i}", [128, 128], BF16) for i in range(2)]
        v_t = [sbb(f"v_t{i}", [128, 256], BF16) for i in range(2)]
        sqw = [sbb(f"sqw{i}", [128, 128]) for i in range(2)]
        eT = [[sbb(f"eT{i}_{j}", [128, 128]) for j in range(3)] for i in range(2)]
        gsp = [sbb(f"gsp{i}", [128, 16], BF16) for i in range(2)]
        colpA = [sbb(f"colpA{i}", [128, 96]) for i in range(2)]
        colpB = [sbb(f"colpB{i}", [128, 96]) for i in range(2)]
        rows6 = [sbb(f"rows6_{i}", [1, 768]) for i in range(2)]
        cums = [sbb(f"cums{i}", [128, 8]) for i in range(2)]
        colw = [sbb(f"colw{i}", [128, 24]) for i in range(2)]
        lamc = [sbb(f"lamc{i}", [128, 8]) for i in range(2)]
        qkb = [sbb(f"qkb{i}", [128, 128], BF16) for i in range(2)]
        Amat = [[sbb(f"Amat{i}_{v}", [C, C], BF16) for v in range(VH)] for i in range(2)]
        Rmat = [[sbb(f"Rmat{i}_{v}", [C, C], BF16) for v in range(VH)] for i in range(2)]
        Apow = [[sbb(f"Apow{i}_{j}", [C, C], BF16) for j in range(2)] for i in range(2)]
        Rpow = [[sbb(f"Rpow{i}_{j}", [C, C], BF16) for j in range(2)] for i in range(2)]
        Ftm = [[sbb(f"Ftm{i}_{v}", [C, C], BF16) for v in range(VH)] for i in range(2)]
        Xs = [[sbb(f"Xs{i}_{j}", [C, DK + DV], BF16) for j in range(2)] for i in range(2)]
        rhs0 = [[sbb(f"rhs0{i}_{v}", [C, DK + DV], BF16) for v in range(VH)] for i in range(2)]
        kbar = [[sbb(f"kbar{i}_{v}", [C, DK], BF16) for v in range(VH)] for i in range(2)]
        wtT = [sbb(f"wtT{i}", [DK, C], BF16) for i in range(2)]
        uu = [[sbb(f"uu{i}_{v}", [C, DV], BF16) for v in range(VH)] for i in range(2)]
        S_f = [[sbb(f"S_f{v}_{j}", [DK, DV]) for j in range(2)] for v in range(VH)]
        S_b = [sbb(f"S_b{v}", [DK, DV], BF16) for v in range(VH)]
        o_w = [sbb(f"o_w{i}", [128, 256]) for i in range(2)]
        zs_t = [sbb(f"zs_t{i}", [128, 256], BF16) for i in range(2)]
        gat = [sbb(f"gat{i}", [128, 256], BF16) for i in range(2)]
        gstrip = [sbb(f"gstrip{v}", [128, 2048], BF16) for v in range(VH)]


        P = [pt(f"P{i}", [128, 512]) for i in range(7)]
        PT = pt("PT", [128, 1024], BF16)

        sd = S("sd"); sg = S("sg"); sv = S("sv"); ss = S("ss"); st = S("st"); scc = S("scc")
        scc1 = S("scc1"); scc2 = S("scc2"); scc3 = S("scc3")

        block = es.enter_context(nc.Block())

        tok_cl = step(SYNC, sd, [], lambda e: (
            e.dma_start(out=mA[:], in_=maskA[:]).then_inc(sd.h, 16),
            e.dma_start(out=mR[:], in_=maskR[:]).then_inc(sd.h, 16),
            e.dma_start(out=mFt[:], in_=maskFt[:]).then_inc(sd.h, 16),
            e.dma_start(out=trb[:], in_=triu_b[:]).then_inc(sd.h, 16),
            e.dma_start(out=idb[:], in_=ident_b[:]).then_inc(sd.h, 16),
            e.dma_start(out=idf[:], in_=ident_f[:]).then_inc(sd.h, 16),
            e.dma_start(out=gto[:], in_=gtoff[:]).then_inc(sd.h, 16),
            e.dma_start(out=small_row[:], in_=smalls[:]).then_inc(sd.h, 16),
            None)[-1], k=16 * 8)

        tok_ms = step(VEC, sv, [], lambda e: (
            e.memset(ones_r[:], 1.0),
            e.memset(ones_t[:], 1.0),
            e.memset(one_col[:], 1.0),
            e.memset(gsp[0][:], 0.0),
            e.memset(gsp[1][:], 0.0),
            e.memset(colpA[0][:], 0.0),
            e.memset(colpA[1][:], 0.0),
            e.memset(colpB[0][:], 0.0),
            e.memset(colpB[1][:], 0.0),
            e.memset(mix_qk[:, 0:3], 0.0),
            e.memset(mix_v0[:, 0:3], 0.0),
            e.memset(mix_v1[:, 0:3], 0.0))[-1])

        tok_s0 = step(VEC, sv, [], lambda e: (
            [e.memset(S_f[v][0][:], 0.0) for v in range(VH)],
            [e.memset(S_b[v][:], 0.0) for v in range(VH)])[-1][-1])

        tok_bmm = step(TEN, st, [tok_cl, tok_ms], lambda e:
                       e.matmul(P[0][:, 0:192], ones_r[:], small_row[:],
                                start=True, stop=True))
        tok_bcp = step(VEC, sv, [tok_bmm], lambda e: (
            e.tensor_copy(out=pbc[:], in_=P[0][:, 0:192])))
        tok_nw = step(VEC, sv, [tok_bcp], lambda e:
                      e.tensor_copy(out=nw_t[:], in_=pbc[:, 64:64 + DV]))
        tok_smc1 = step(VEC, sv, [tok_bcp], lambda e: (
            e.tensor_copy(out=smc[:, 0:2], in_=pbc[:, 0:2]),
            e.tensor_copy(out=smc[:, 4:6], in_=pbc[:, 4:6]))[-1])
        tok_eA = step(SCA, ss, [tok_smc1], lambda e:
                      e.activation(out=smc[:, 2:4], in_=pbc[:, 2:4], func=AF.Exp))
        tok_smc = step(VEC, sv, [tok_eA], lambda e: (
            e.tensor_scalar(out=smc[:, 2:4], in0=smc[:, 2:4], scalar1=-1.0,
                            scalar2=None, op0=ALU.mult),
            e.tensor_tensor(out=smc[:, 7:8], in0=smc[:, 4:5], in1=smc[:, 5:6], op=ALU.mult))[-1])
        tok_smc2 = step(VEC, sv, [tok_smc], lambda e:
                        e.reciprocal(out=smc[:, 6:7], in_=smc[:, 7:8]))

        tok_w = None
        for kt in range(16):
            tok_w = step(SYNC, sd, [], lambda e, kt=kt: (
                e.dma_start(out=wqkvT[kt][:],
                            in_=w_qkvT[kt * 128:(kt + 1) * 128, :]).then_inc(sd.h, 16),
                e.dma_start(out=wzT[kt][:],
                            in_=w_zT[kt * 128:(kt + 1) * 128, :]).then_inc(sd.h, 16),
                e.dma_start(out=wbaT[kt][:],
                            in_=w_baT[kt * 128:(kt + 1) * 128, :]).then_inc(sd.h, 16),
                None)[-1], k=48)

        tok_cd = step(SYNC, sd, [], lambda e: (
            e.dma_start(out=cd_q[:], in_=convd_q[:]).then_inc(sd.h, 16),
            e.dma_start(out=cd_k[:], in_=convd_k[:]).then_inc(sd.h, 16),
            e.dma_start(out=cd_v0[:], in_=convd_v0[:]).then_inc(sd.h, 16),
            e.dma_start(out=cd_v1[:], in_=convd_v1[:]).then_inc(sd.h, 16),
            None)[-1], k=64)

        ln_done = [None, None]
        piece_dma = [None, None]
        last_piece_dma = None
        tr_prev = None
        for i in range(16):
            lp = i % 2
            tx = step(SYNC, sd, [ln_done[lp], tok_cd], lambda e, i=i:
                      e.dma_start(out=atile[:], in_=x_seg[i * 128:(i + 1) * 128, :]), k=16)
            t1a = step(VEC, sv, [tx], lambda e:
                       e.reduce_sum(out=acol[:, 0:1], in_=atile[:], axis=AX.X))
            t1 = step(VEC, sv, [t1a], lambda e:
                      e.tensor_scalar(out=acol[:, 1:2], in0=acol[:, 0:1], scalar1=1.0 / D,
                                      scalar2=None, op0=ALU.mult))
            t2 = step(VEC, sv, [t1], lambda e:
                      e.tensor_scalar(out=atile[:], in0=atile[:], scalar1=acol[:, 1:2],
                                      scalar2=None, op0=ALU.subtract))
            t3 = step(SCA, ss, [t2], lambda e:
                      e.activation(out=sqtile[:], in_=atile[:], func=AF.Square))
            t4a = step(VEC, sv, [t3], lambda e:
                       e.reduce_sum(out=acol[:, 2:3], in_=sqtile[:], axis=AX.X))
            t4 = step(VEC, sv, [t4a], lambda e:
                      e.tensor_scalar(out=acol[:, 3:4], in0=acol[:, 2:3], scalar1=1.0 / D,
                                      scalar2=1e-5, op0=ALU.mult, op1=ALU.add))
            t5 = step(SCA, ss, [t4], lambda e:
                      e.activation(out=acol[:, 4:5], in_=acol[:, 3:4], func=AF.Sqrt))
            t6 = step(VEC, sv, [t5], lambda e:
                      e.reciprocal(out=acol[:, 5:6], in_=acol[:, 4:5]))
            t7 = step(VEC, sv, [t6], lambda e, lp=lp:
                      e.tensor_scalar(out=lnbf[lp][:], in0=atile[:], scalar1=acol[:, 5:6],
                                      scalar2=None, op0=ALU.mult))
            for fb in range(16):
                pp = fb % 2
                trp = step(TEN, st, [t7, piece_dma[pp], tr_prev], lambda e, lp=lp, fb=fb:
                           e.transpose(PT[:, (fb % 4) * 128:(fb % 4 + 1) * 128],
                                       lnbf[lp][:, fb * 128:(fb + 1) * 128], idb[:]))
                tcp = step(VEC, sv, [trp], lambda e, pp=pp, fb=fb:
                           e.tensor_copy(out=tstg[pp][:],
                                         in_=PT[:, (fb % 4) * 128:(fb % 4 + 1) * 128]))
                tr_prev = tcp
                tdm = step(SYNC, sd, [tcp], lambda e, pp=pp, fb=fb, i=i:
                           e.dma_start(out=cc_ln_in[fb * 128:(fb + 1) * 128,
                                                    i * 128:(i + 1) * 128],
                                       in_=tstg[pp][:]), k=16)
                piece_dma[pp] = tdm
                last_piece_dma = tdm
            ln_done[lp] = t7
        strip_dma = [last_piece_dma]

        q2_dma = []
        for i in range(2):
            tw = step(SYNC, sd, [tok_cd, q2_dma[-1] if q2_dma else None],
                      lambda e, i=i: e.dma_start(out=wstage[1][:, 0:D],
                                                 in_=w2_slice[i * 128:(i + 1) * 128, :]), k=16)
            tq = step(VEC, sv, [tw, tok_smc2, ln_done[0], ln_done[1]], lambda e:
                      e.tensor_scalar(out=atile[:].bitcast(FP8)[:, 0:D], in0=wstage[1][:, 0:D],
                                      scalar1=smc[:, 5:6], scalar2=None, op0=ALU.mult))
            td = step(SYNC, sd, [tq], lambda e, i=i:
                      e.dma_start(out=cc_q2_in[i * 128:(i + 1) * 128, :],
                                  in_=atile[:].bitcast(FP8)[:, 0:D]), k=16)
            q2_dma.append(td)

        step(GPS, None, [strip_dma[-1], q2_dma[-1]], lambda e: (
            e.collective_compute("AllGather", ALU.bypass,
                                 replica_groups=[list(range(NCORES))],
                                 ins=[cc_ln_in[:].opt()],
                                 outs=[cc_ln_out[:].opt()]).then_inc(scc1.h, 1),
            e.collective_compute("AllGather", ALU.bypass,
                                 replica_groups=[list(range(NCORES))],
                                 ins=[cc_q2_in[:].opt()],
                                 outs=[cc_q2_out[:].opt()]).then_inc(scc2.h, 1),
            None)[-1])
        scc1.n = 1; scc2.n = 1
        tok_ag12 = (scc1, 1)
        tok_q2ag = (scc2, 1)

        if stage == "A":
            tdbg = step(GPS, sg, [tok_ag12], lambda e: (
                [e.dma_start(out=dbg[r * 16:(r + 1) * 16, :],
                             in_=cc_ln_out[r * 1024 + 0:r * 1024 + 16, :]).then_inc(sg.h, 16)
                 for r in range(16)],
                None)[-1], k=16 * 16)
            GPS.append(lambda e: e.wait_ge(sg.h, sg.n))
            _finish(block, SYNC, VEC, SCA, TEN, GPS)
            return nc

        ip_tok = [None, None]
        act_tok = [None, None]
        conv_tok = [None, None]
        chunk_ten_end = [tok_s0, tok_s0]
        chunk_sb = tok_s0
        zsil_tok = [None] * 8
        ba_tok = [None] * 8
        gs_dma = []
        gs_cp = [None, None]

        NBLK = 1 if stage == "B1" else NTB

        for b in range(NBLK):
            par = b % 2
            seg_i = (b * TB) // SEG
            off = (b * TB) % SEG

            tlnx = step(SYNC, sd, [tok_ag12, ip_tok[par]], lambda e, par=par, seg_i=seg_i, off=off: (
                [e.dma_start(out=lnx[par][kt][:],
                             in_=cc_ln_out[seg_i * D + kt * 128:seg_i * D + (kt + 1) * 128,
                                           off:off + TB]).then_inc(sd.h, 16)
                 for kt in range(16)],
                None)[-1], k=16 * 16)

            tip = step(TEN, st, [tlnx, tok_w, conv_tok[par], act_tok[par],
                                 (sv, sv.n), (ss, ss.n)], lambda e, par=par: (
                [(e.matmul(P[0][:], wqkvT[kt][:, 0:128], lnx[par][kt][:],
                           start=(kt == 0), stop=(kt == 15)),
                  e.matmul(P[1][:], wqkvT[kt][:, 128:256], lnx[par][kt][:],
                           start=(kt == 0), stop=(kt == 15)),
                  e.matmul(P[2][:], wqkvT[kt][:, 256:384], lnx[par][kt][:],
                           start=(kt == 0), stop=(kt == 15))) for kt in range(16)],
                [[e.matmul((P[3] if sq < 2 else P[4])[:, (sq % 2) * 256:(sq % 2) * 256 + 256],
                           lnx[par][kt][:, sq * 128:(sq + 1) * 128], wzT[kt][:],
                           start=(kt == 0), stop=(kt == 15)) for kt in range(16)]
                 for sq in range(4)],
                [[e.matmul(P[5][:, sq * 16:(sq + 1) * 16],
                           lnx[par][kt][:, sq * 128:(sq + 1) * 128], wbaT[kt][:],
                           start=(kt == 0), stop=(kt == 15)) for kt in range(16)]
                 for sq in range(4)])[-1][-1][-1])
            ip_tok[par] = tip

            def mk_evict(b, par, tip):
                def f(e):
                    if b > 0:
                        e.tensor_copy(out=mix_qk[:, 0:3], in_=mix_qk[:, TB:TB + 3])
                        e.tensor_copy(out=mix_v0[:, 0:3], in_=mix_v0[:, TB:TB + 3])
                        e.tensor_copy(out=mix_v1[:, 0:3], in_=mix_v1[:, TB:TB + 3])
                    e.tensor_copy(out=mix_qk[:, 3:3 + TB], in_=P[0][:])
                    e.tensor_copy(out=mix_v0[:, 3:3 + TB], in_=P[1][:])
                    e.tensor_copy(out=mix_v1[:, 3:3 + TB], in_=P[2][:])
                    last = None
                    for sq in range(4):
                        pz = P[3] if sq < 2 else P[4]
                        e.tensor_copy(out=z_ring[(b * 4 + sq) % 8][:],
                                      in_=pz[:, (sq % 2) * 256:(sq % 2) * 256 + 256])
                        last = e.tensor_copy(out=ba_ring[(b * 4 + sq) % 8][:],
                                             in_=P[5][:, sq * 16:sq * 16 + 4])
                    return last
                return f
            zguards = [zsil_tok[(b * 4 + sq) % 8] for sq in range(4)] + \
                      [ba_tok[(b * 4 + sq) % 8] for sq in range(4)]
            tmx = step(VEC, sv, [tip, conv_tok[par]] + zguards, mk_evict(b, par, tip))

            tcv = step(TEN, st, [tmx, act_tok[par], chunk_ten_end[0], chunk_ten_end[1],
                                 (sv, sv.n), (ss, ss.n)],
                       lambda e, par=par: (
                [e.matmul(P[0][0:64, :], cd_q[:, tap * DK:(tap + 1) * DK],
                          mix_qk[:, tap:tap + TB],
                          start=(tap == 0), stop=(tap == KS - 1)) for tap in range(KS)],
                [e.matmul(P[6][0:64, 0:512], cd_k[:, tap * DK:(tap + 1) * DK],
                          mix_qk[:, tap:tap + TB],
                          start=(tap == 0), stop=(tap == KS - 1)) for tap in range(KS)],
                [e.matmul(P[1][:], cd_v0[:, tap * 128:(tap + 1) * 128], mix_v0[:, tap:tap + TB],
                          start=(tap == 0), stop=(tap == KS - 1)) for tap in range(KS)],
                [e.matmul(P[2][:], cd_v1[:, tap * 128:(tap + 1) * 128], mix_v1[:, tap:tap + TB],
                          start=(tap == 0), stop=(tap == KS - 1)) for tap in range(KS)])[-1][-1])
            conv_tok[par] = tcv

            tac = step(SCA, ss, [tcv], lambda e, par=par: (
                e.activation(out=act_q[par][:], in_=P[0][0:64, :], func=AF.Silu),
                e.activation(out=act_k[par][:], in_=P[6][0:64, 0:512], func=AF.Silu),
                e.activation(out=act_v0[par][:], in_=P[1][:], func=AF.Silu),
                e.activation(out=act_v1[par][:], in_=P[2][:], func=AF.Silu))[-1])
            act_tok[par] = tac

            for cl in range(4):
                if SCAN_LVL == 0:
                    break
                cc = b * 4 + cl
                cp = cc % 2
                ccol = slice(cl * 128, (cl + 1) * 128)
                zr = z_ring[cc % 8]
                bar = ba_ring[cc % 8]
                guard = chunk_ten_end[cp]

                t_tr = step(TEN, st, [tac, guard, (sv, sv.n), (ss, ss.n)],
                            lambda e, par=par, ccol=ccol: (
                    e.transpose(PT[:, 0:64], act_q[par][:, ccol], idb[0:64, 0:64]),
                    e.transpose(PT[:, 64:128], act_k[par][:, ccol], idb[0:64, 0:64]),
                    e.transpose(PT[:, 128:256], act_v0[par][:, ccol], idb[:]),
                    e.transpose(PT[:, 256:384], act_v1[par][:, ccol], idb[:]))[-1])
                t_trc = step(VEC, sv, [t_tr], lambda e, cp=cp: (
                    e.tensor_copy(out=kv_t[cp][:], in_=PT[:, 0:128]),
                    e.tensor_copy(out=v_t[cp][:], in_=PT[:, 128:384]))[-1])

                t_sq = step(SCA, ss, [t_trc], lambda e, cp=cp:
                            e.activation(out=sqw[cp][:], in_=kv_t[cp][:], func=AF.Square))
                t_ssq = step(VEC, sv, [t_sq], lambda e, cp=cp: (
                    e.reduce_sum(out=colw[cp][:, 0:1], in_=sqw[cp][:, 0:64], axis=AX.X),
                    e.reduce_sum(out=colw[cp][:, 1:2], in_=sqw[cp][:, 64:128], axis=AX.X))[-1])
                t_nr = step(SCA, ss, [t_ssq], lambda e, cp=cp:
                            e.activation(out=colw[cp][:, 2:4], in_=colw[cp][:, 0:2], func=AF.Sqrt))
                t_rna = step(VEC, sv, [t_nr], lambda e, cp=cp:
                             e.tensor_scalar(out=colw[cp][:, 2:4], in0=colw[cp][:, 2:4],
                                             scalar1=1e-12, scalar2=None, op0=ALU.max))
                t_rn = step(VEC, sv, [t_rna], lambda e, cp=cp:
                            e.reciprocal(out=colw[cp][:, 4:6], in_=colw[cp][:, 2:4]))
                t_lnrk = step(SCA, ss, [t_rn], lambda e, cp=cp:
                              e.activation(out=colw[cp][:, 6:7], in_=colw[cp][:, 5:6], func=AF.Ln))

                if SCAN_LVL < 11:
                    chunk_ten_end[cp] = (st, st.n)
                    continue
                t_apd = step(VEC, sv, [tmx, tok_smc2, guard], lambda e, cp=cp, bar=bar:
                             e.tensor_tensor(out=colw[cp][:, 8:10], in0=bar[:, 2:4],
                                             in1=smc[:, 0:2], op=ALU.add))
                t_abs = step(SCA, ss, [t_apd], lambda e, cp=cp, bar=bar: (
                    e.activation(out=colw[cp][:, 10:12], in_=colw[cp][:, 8:10], func=AF.Abs),
                    e.activation(out=colw[cp][:, 12:14], in_=bar[:, 0:2], func=AF.Abs))[-1])
                t_ex = step(SCA, ss, [t_abs], lambda e, cp=cp: (
                    e.activation(out=colw[cp][:, 10:12], in_=colw[cp][:, 10:12],
                                 func=AF.Exp, scale=-1.0),
                    e.activation(out=colw[cp][:, 12:14], in_=colw[cp][:, 12:14],
                                 func=AF.Exp, scale=-1.0))[-1])
                t_l1p = step(SCA, ss, [t_ex], lambda e, cp=cp: (
                    e.activation(out=colw[cp][:, 10:12], in_=colw[cp][:, 10:12],
                                 func=AF.Ln, bias=one_col[:]),
                    e.activation(out=colw[cp][:, 12:14], in_=colw[cp][:, 12:14],
                                 func=AF.Ln, bias=one_col[:]))[-1])
                t_bet = step(SCA, ss, [t_l1p], lambda e, cp=cp, bar=bar:
                             e.activation(out=colw[cp][:, 14:16], in_=bar[:, 0:2], func=AF.Sigmoid))
                ba_tok[cc % 8] = t_bet
                t_ga = step(VEC, sv, [t_l1p, t_bet], lambda e, cp=cp, bar=bar: (
                    e.tensor_scalar(out=cums[cp][:, 4:6], in0=colw[cp][:, 8:10],
                                    scalar1=0.0, scalar2=None, op0=ALU.max),
                    e.tensor_scalar(out=cums[cp][:, 6:8], in0=bar[:, 0:2],
                                    scalar1=0.0, scalar2=None, op0=ALU.min))[-1])
                t_gb = step(VEC, sv, [t_ga], lambda e, cp=cp: (
                    e.tensor_tensor(out=cums[cp][:, 4:6], in0=cums[cp][:, 4:6],
                                    in1=colw[cp][:, 10:12], op=ALU.add),
                    e.tensor_tensor(out=cums[cp][:, 6:8], in0=cums[cp][:, 6:8],
                                    in1=colw[cp][:, 12:14], op=ALU.subtract))[-1])
                t_g = step(VEC, sv, [t_gb], lambda e, cp=cp:
                           e.tensor_tensor(out=cums[cp][:, 4:6], in0=cums[cp][:, 4:6],
                                           in1=smc[:, 2:4], op=ALU.mult))

                t_gh = step(VEC, sv, [t_g], lambda e, cp=cp:
                            e.tensor_copy(out=gsp[cp][:, 0:2], in_=cums[cp][:, 4:6]))
                t_gl = step(VEC, sv, [t_gh], lambda e, cp=cp:
                            e.tensor_tensor(out=gsp[cp][:, 8:10], in0=cums[cp][:, 4:6],
                                            in1=gsp[cp][:, 0:2], op=ALU.subtract))

                if SCAN_LVL < 12:
                    chunk_ten_end[cp] = (st, st.n)
                    continue
                t_cm = step(TEN, st, [t_gl], lambda e, cp=cp: (
                    e.matmul(P[5][:, 64:72], trb[:], gsp[cp][:, 0:8], start=True, stop=False),
                    e.matmul(P[5][:, 64:72], trb[:], gsp[cp][:, 8:16], start=False, stop=True)
                )[-1])
                t_cuma = step(VEC, sv, [t_cm, t_lnrk], lambda e, cp=cp:
                              e.tensor_copy(out=cums[cp][:, 0:2], in_=P[5][:, 64:66]))
                t_cum = step(VEC, sv, [t_cuma], lambda e, cp=cp: (
                    e.tensor_copy(out=colpA[cp][:, 0:1], in_=cums[cp][:, 0:1]),
                    e.tensor_copy(out=colpA[cp][:, 32:33], in_=cums[cp][:, 1:2]),
                    e.tensor_tensor(out=colpA[cp][:, 64:65], in0=colw[cp][:, 6:7],
                                    in1=cums[cp][:, 0:1], op=ALU.subtract),
                    e.tensor_tensor(out=colpB[cp][:, 0:1], in0=colw[cp][:, 6:7],
                                    in1=cums[cp][:, 1:2], op=ALU.subtract),
                    e.tensor_tensor(out=colpB[cp][:, 32:33], in0=cums[cp][:, 0:1],
                                    in1=cums[cp][:, 6:7], op=ALU.add),
                    e.tensor_tensor(out=colpB[cp][:, 64:65], in0=cums[cp][:, 1:2],
                                    in1=cums[cp][:, 7:8], op=ALU.add))[-1])
                t_cum2 = step(VEC, sv, [t_cum], lambda e, cp=cp: (
                    e.tensor_tensor(out=colpB[cp][:, 32:33], in0=colpB[cp][:, 32:33],
                                    in1=colw[cp][:, 6:7], op=ALU.add),
                    e.tensor_tensor(out=colpB[cp][:, 64:65], in0=colpB[cp][:, 64:65],
                                    in1=colw[cp][:, 6:7], op=ALU.add))[-1])

                if SCAN_LVL < 13:
                    chunk_ten_end[cp] = (st, st.n)
                    continue
                t_rpt = step(TEN, st, [t_cum2], lambda e, cp=cp: (
                    e.transpose(P[6][0:96, 0:128].bitcast(F32), colpA[cp][:], idf[:]),
                    e.transpose(P[6][0:96, 128:256].bitcast(F32), colpB[cp][:], idf[:]))[-1])
                t_rows = step(VEC, sv, [t_rpt], lambda e, cp=cp: (
                    e.tensor_copy(out=rows6[cp][0:1, 0:128], in_=P[6][0:1, 0:128]),
                    e.tensor_copy(out=rows6[cp][0:1, 128:256], in_=P[6][32:33, 0:128]),
                    e.tensor_copy(out=rows6[cp][0:1, 256:384], in_=P[6][64:65, 0:128]),
                    e.tensor_copy(out=rows6[cp][0:1, 384:512], in_=P[6][0:1, 128:256]),
                    e.tensor_copy(out=rows6[cp][0:1, 512:640], in_=P[6][32:33, 128:256]),
                    e.tensor_copy(out=rows6[cp][0:1, 640:768], in_=P[6][64:65, 128:256]))[-1])

                def rowv_(cp, name, v):
                    k6 = {("cum", 0): 0, ("cum", 1): 1, ("c2", 0): 2,
                          ("c2", 1): 3, ("c1", 0): 4, ("c1", 1): 5}[(name, v)]
                    return (rows6[cp][0:1, k6 * 128:(k6 + 1) * 128], 0)

                def bc_row(e, pslice, name, v, cp, start=True, stop=True):
                    r_, b_ = rowv_(cp, name, v)
                    return e.matmul(pslice, ones_t[b_:b_ + 1, :], r_, start=start, stop=stop)

                def bc_col(e, pslice, name, v, cp, start=True, stop=True):
                    r_, b_ = rowv_(cp, name, v)
                    return e.matmul(pslice, r_, ones_t[b_:b_ + 1, :], start=start, stop=stop)

                if SCAN_LVL < 14:
                    chunk_ten_end[cp] = (st, st.n)
                    continue
                t_cC = step(TEN, st, [t_rows], lambda e, cp=cp: (
                    bc_row(e, P[5][:, 128:256], "cum", 0, cp),
                    bc_row(e, P[5][:, 256:384], "cum", 1, cp))[-1])
                t_lamc = step(SCA, ss, [t_cC], lambda e, cp=cp: (
                    e.activation(out=lamc[cp][:, 0:1], in_=P[5][:, 255:256], func=AF.Exp),
                    e.activation(out=lamc[cp][:, 1:2], in_=P[5][:, 383:384], func=AF.Exp))[-1])
                t_cCc = step(VEC, sv, [t_cC], lambda e, cp=cp: (
                    e.tensor_copy(out=lamc[cp][:, 2:3], in_=P[5][:, 255:256]),
                    e.tensor_copy(out=lamc[cp][:, 3:4], in_=P[5][:, 383:384]))[-1])

                if SCAN_LVL < 15:
                    chunk_ten_end[cp] = (st, st.n)
                    continue
                t_kk = step(TEN, st, [tac, guard], lambda e, par=par, ccol=ccol: (
                    e.matmul(P[3][:, 0:128], act_k[par][:, ccol],
                             act_k[par][:, ccol], start=True, stop=True),
                    e.matmul(P[3][:, 128:256], act_k[par][:, ccol],
                             act_q[par][:, ccol], start=True, stop=True))[-1])
                t_qkb = step(VEC, sv, [t_kk], lambda e, cp=cp:
                             e.tensor_copy(out=qkb[cp][:], in_=P[3][:, 128:256]))

                t_lam = step(SCA, ss, [t_cum], lambda e, cp=cp:
                             e.activation(out=colw[cp][:, 16:18], in_=cums[cp][:, 0:2],
                                          func=AF.Exp))

                if SCAN_LVL < 16:
                    chunk_ten_end[cp] = (st, st.n)
                    continue
                per_v_sb = []
                o_cons_prev = None
                sn_prev = None
                ms_prev = None
                for v in range(VH):
                    t_bA = step(TEN, st, [t_rows, o_cons_prev], lambda e, cp=cp, v=v: (
                        bc_row(e, P[6][:, 128:256], "c2", v, cp),
                        bc_col(e, P[6][:, 256:384], "c1", v, cp))[-1])
                    t_dAa = step(VEC, sv, [t_bA], lambda e, cp=cp:
                        e.tensor_tensor(out=eT[cp][0][:], in0=P[6][:, 256:384], in1=mA[:],
                                        op=ALU.add))
                    t_dA = step(VEC, sv, [t_dAa], lambda e, cp=cp:
                        e.tensor_tensor(out=eT[cp][0][:], in0=P[6][:, 128:256],
                                        in1=eT[cp][0][:], op=ALU.add))
                    t_xA = step(SCA, ss, [t_dA], lambda e, cp=cp:
                                e.activation(out=eT[cp][0][:], in_=eT[cp][0][:], func=AF.Exp))
                    t_A = step(VEC, sv, [t_xA, t_kk], lambda e, cp=cp, v=v:
                               e.tensor_tensor(out=Amat[cp][v][:], in0=eT[cp][0][:],
                                               in1=P[3][:, 0:128], op=ALU.mult))

                    t_bR = step(TEN, st, [t_rows, t_dA], lambda e, cp=cp, v=v: (
                        bc_row(e, P[6][:, 128:256], "c1", v, cp),
                        bc_col(e, P[6][:, 256:384], "c2", v, cp))[-1])
                    t_dRa = step(VEC, sv, [t_bR], lambda e, cp=cp:
                        e.tensor_tensor(out=eT[cp][1][:], in0=P[6][:, 256:384], in1=mR[:],
                                        op=ALU.add))
                    t_dR = step(VEC, sv, [t_dRa], lambda e, cp=cp:
                        e.tensor_tensor(out=eT[cp][1][:], in0=P[6][:, 128:256],
                                        in1=eT[cp][1][:], op=ALU.add))
                    t_xR = step(SCA, ss, [t_dR], lambda e, cp=cp:
                                e.activation(out=eT[cp][1][:], in_=eT[cp][1][:], func=AF.Exp))
                    t_R = step(VEC, sv, [t_xR, t_kk], lambda e, cp=cp, v=v:
                               e.tensor_tensor(out=Rmat[cp][v][:], in0=eT[cp][1][:],
                                               in1=P[3][:, 0:128], op=ALU.mult))

                    t_bF = step(TEN, st, [t_rows, t_dR], lambda e, cp=cp, v=v: (
                        bc_row(e, P[6][:, 128:256], "cum", v, cp),
                        bc_col(e, P[6][:, 256:384], "c2", v, cp))[-1])
                    t_dFa = step(VEC, sv, [t_bF], lambda e, cp=cp:
                        e.tensor_tensor(out=eT[cp][2][:], in0=P[6][:, 256:384], in1=mFt[:],
                                        op=ALU.add))
                    t_dF = step(VEC, sv, [t_dFa], lambda e, cp=cp:
                        e.tensor_tensor(out=eT[cp][2][:], in0=P[6][:, 128:256],
                                        in1=eT[cp][2][:], op=ALU.add))
                    t_xF = step(SCA, ss, [t_dF], lambda e, cp=cp:
                                e.activation(out=eT[cp][2][:], in_=eT[cp][2][:], func=AF.Exp))
                    t_F = step(VEC, sv, [t_xF, t_qkb], lambda e, cp=cp, v=v:
                               e.tensor_tensor(out=Ftm[cp][v][:], in0=eT[cp][2][:],
                                               in1=qkb[cp][:], op=ALU.mult))

                    if SCAN_LVL < 17:
                        continue
                    t_rhsa = step(VEC, sv, [t_lam, t_bet, t_rn, t_trc, t_cCc, guard],
                                  lambda e, cp=cp, v=v: (
                        e.tensor_tensor(out=colw[cp][:, 18:19], in0=colw[cp][:, 14 + v:15 + v],
                                        in1=colw[cp][:, 16 + v:17 + v], op=ALU.mult),
                        e.tensor_scalar(out=rhs0[cp][v][:, DK:DK + DV],
                                        in0=v_t[cp][:, v * DV:(v + 1) * DV],
                                        scalar1=colw[cp][:, 14 + v:15 + v],
                                        scalar2=None, op0=ALU.mult),
                        e.tensor_tensor(out=colw[cp][:, 19:20], in0=lamc[cp][:, 2 + v:3 + v],
                                        in1=cums[cp][:, v:v + 1], op=ALU.subtract))[-1])
                    t_rhsb = step(VEC, sv, [t_rhsa], lambda e, cp=cp, v=v:
                        e.tensor_tensor(out=colw[cp][:, 18:19], in0=colw[cp][:, 18:19],
                                        in1=colw[cp][:, 5:6], op=ALU.mult))
                    t_rhs = step(VEC, sv, [t_rhsb], lambda e, cp=cp, v=v:
                        e.tensor_scalar(out=rhs0[cp][v][:, 0:DK], in0=kv_t[cp][:, 64:128],
                                        scalar1=colw[cp][:, 18:19], scalar2=None, op0=ALU.mult))
                    t_kbe = step(SCA, ss, [t_rhs], lambda e, cp=cp:
                                 e.activation(out=colw[cp][:, 20:21], in_=colw[cp][:, 19:20],
                                              func=AF.Exp))
                    t_kba = step(VEC, sv, [t_kbe], lambda e, cp=cp, v=v:
                        e.tensor_tensor(out=colw[cp][:, 21:22], in0=colw[cp][:, 20:21],
                                        in1=colw[cp][:, 5:6], op=ALU.mult))
                    t_kb = step(VEC, sv, [t_kba], lambda e, cp=cp, v=v:
                        e.tensor_scalar(out=kbar[cp][v][:], in0=kv_t[cp][:, 64:128],
                                        scalar1=colw[cp][:, 21:22], scalar2=None,
                                        op0=ALU.mult))

                    if SCAN_LVL < 18:
                        continue
                    xt_cur, tok_x = rhs0[cp][v], t_rhs
                    a_cur, r_cur = Amat[cp][v], Rmat[cp][v]
                    tok_a, tok_r = t_A, t_R
                    xi = 0
                    ai = 0
                    p = 1
                    sign = -1.0
                    while p < C:
                        t_mx = step(TEN, st, [tok_r, tok_x], lambda e, r_cur=r_cur, xt_cur=xt_cur:
                                    e.matmul(P[0][:, 0:192], r_cur[:], xt_cur[:],
                                             start=True, stop=True))
                        xt_new = Xs[cp][xi]
                        tok_x = step(VEC, sv, [t_mx], lambda e, xt_cur=xt_cur, xt_new=xt_new,
                                     sign=sign:
                                     e.scalar_tensor_tensor(out=xt_new[:], in0=P[0][:, 0:192],
                                                            scalar=sign, in1=xt_cur[:],
                                                            op0=ALU.mult, op1=ALU.add))
                        xt_cur = xt_new
                        xi ^= 1
                        sign = 1.0
                        if p * 2 < C:
                            t_sq2 = step(TEN, st, [tok_a, tok_r, t_mx],
                                         lambda e, a_cur=a_cur, r_cur=r_cur: (
                                e.matmul(P[1][:, 0:128], r_cur[:], a_cur[:],
                                         start=True, stop=True),
                                e.matmul(P[2][:, 0:128], a_cur[:], r_cur[:],
                                         start=True, stop=True))[-1])
                            a_new, r_new = Apow[cp][ai], Rpow[cp][ai]
                            tok_a = step(VEC, sv, [t_sq2], lambda e, a_new=a_new:
                                         e.tensor_copy(out=a_new[:], in_=P[1][:, 0:128]))
                            tok_r = step(VEC, sv, [t_sq2], lambda e, r_new=r_new:
                                         e.tensor_copy(out=r_new[:], in_=P[2][:, 0:128]))
                            a_cur, r_cur = a_new, r_new
                            ai ^= 1
                        p *= 2
                    Xfin, t_solve = xt_cur, tok_x

                    if SCAN_LVL < 19:
                        continue
                    t_wtt = step(TEN, st, [t_solve], lambda e, Xfin=Xfin:
                                 e.transpose(PT[0:64, 512:640], Xfin[:, 0:DK], idb[:]))
                    t_wtc = step(VEC, sv, [t_wtt], lambda e, cp=cp:
                                 e.tensor_copy(out=wtT[cp][:], in_=PT[0:64, 512:640]))
                    t_wst = step(TEN, st, [t_wtc, chunk_sb], lambda e, cp=cp, v=v:
                                 e.matmul(P[4][:, 0:128], wtT[cp][:], S_b[v][:],
                                          start=True, stop=True))
                    t_u = step(VEC, sv, [t_wst], lambda e, cp=cp, v=v, Xfin=Xfin:
                               e.scalar_tensor_tensor(out=uu[cp][v][:], in0=P[4][:, 0:128],
                                                      scalar=-1.0, in1=Xfin[:, DK:DK + DV],
                                                      op0=ALU.mult, op1=ALU.add))

                    t_omm = step(TEN, st, [t_u, t_F, chunk_sb, tac, o_cons_prev],
                                 lambda e, cp=cp, v=v, par=par, ccol=ccol: (
                        e.matmul(P[6][:, 0:128], act_q[par][:, ccol], S_b[v][:],
                                 start=True, stop=True),
                        e.matmul(P[6][:, 128:256], Ftm[cp][v][:], uu[cp][v][:],
                                 start=True, stop=True))[-1])

                    t_smm = step(TEN, st, [t_u, t_kb, t_omm, sn_prev], lambda e, cp=cp, v=v:
                                 e.matmul(P[4][0:64, 128:256], kbar[cp][v][:], uu[cp][v][:],
                                          start=True, stop=True))

                    Sf_old, Sf_new = S_f[v][cc % 2], S_f[v][(cc + 1) % 2]
                    t_sn = step(VEC, sv, [t_smm, t_lamc], lambda e, cp=cp, v=v,
                                Sf_old=Sf_old, Sf_new=Sf_new:
                                e.scalar_tensor_tensor(out=Sf_new[:], in0=Sf_old[:],
                                                       scalar=lamc[cp][0:64, v:v + 1],
                                                       in1=P[4][0:64, 128:256],
                                                       op0=ALU.mult, op1=ALU.add))
                    t_sb2 = step(VEC, sv, [t_sn, t_omm, t_wst], lambda e, v=v, Sf_new=Sf_new:
                                 e.tensor_copy(out=S_b[v][:], in_=Sf_new[:]))
                    per_v_sb.append(t_sb2)
                    sn_prev = t_sn

                    t_oa = step(VEC, sv, [t_omm, t_lam, t_rn], lambda e, cp=cp, v=v:
                        e.tensor_scalar(out=o_w[cp][:, v * DV:(v + 1) * DV],
                                        in0=P[6][:, 0:128],
                                        scalar1=colw[cp][:, 16 + v:17 + v],
                                        scalar2=None, op0=ALU.mult))
                    t_ob = step(VEC, sv, [t_oa], lambda e, cp=cp, v=v:
                        e.tensor_tensor(out=o_w[cp][:, v * DV:(v + 1) * DV],
                                        in0=P[6][:, 128:256],
                                        in1=o_w[cp][:, v * DV:(v + 1) * DV], op=ALU.add))
                    t_o = step(VEC, sv, [t_ob], lambda e, cp=cp, v=v:
                        e.tensor_scalar(out=o_w[cp][:, v * DV:(v + 1) * DV],
                                        in0=o_w[cp][:, v * DV:(v + 1) * DV],
                                        scalar1=colw[cp][:, 4:5], scalar2=SCALE,
                                        op0=ALU.mult, op1=ALU.mult))
                    o_cons_prev = t_o

                    t_osq = step(SCA, ss, [t_o, ms_prev], lambda e, cp=cp, v=v:
                                 e.activation(out=sqw[cp][:], in_=o_w[cp][:, v * DV:(v + 1) * DV],
                                              func=AF.Square))
                    t_ms2a = step(VEC, sv, [t_osq], lambda e, cp=cp:
                        e.reduce_sum(out=colw[cp][:, 22:23], in_=sqw[cp][:, 0:DV], axis=AX.X))
                    t_ms2 = step(VEC, sv, [t_ms2a], lambda e, cp=cp:
                        e.tensor_scalar(out=colw[cp][:, 22:23], in0=colw[cp][:, 22:23],
                                        scalar1=1.0 / DV, scalar2=1e-6,
                                        op0=ALU.mult, op1=ALU.add))
                    ms_prev = t_ms2
                    t_rmq = step(SCA, ss, [t_ms2], lambda e, cp=cp:
                                 e.activation(out=colw[cp][:, 23:24], in_=colw[cp][:, 22:23],
                                              func=AF.Sqrt))
                    t_rr = step(VEC, sv, [t_rmq], lambda e, cp=cp:
                                e.reciprocal(out=colw[cp][:, 22:23], in_=colw[cp][:, 23:24]))
                    t_sz = step(SCA, ss, [tmx, t_osq], lambda e, cp=cp, v=v, zr=zr:
                                e.activation(out=zs_t[cp][:, v * DV:(v + 1) * DV],
                                             in_=zr[:, v * DV:(v + 1) * DV], func=AF.Silu))
                    zsil_tok[cc % 8] = t_sz
                    t_gta = step(VEC, sv, [t_rr, t_sz, tok_nw], lambda e, cp=cp, v=v:
                        e.tensor_scalar(out=o_w[cp][:, v * DV:(v + 1) * DV],
                                        in0=o_w[cp][:, v * DV:(v + 1) * DV],
                                        scalar1=colw[cp][:, 22:23], scalar2=None, op0=ALU.mult))
                    t_gtb = step(VEC, sv, [t_gta], lambda e, cp=cp, v=v:
                        e.tensor_tensor(out=o_w[cp][:, v * DV:(v + 1) * DV],
                                        in0=o_w[cp][:, v * DV:(v + 1) * DV],
                                        in1=nw_t[:], op=ALU.mult))
                    t_gt = step(VEC, sv, [t_gtb], lambda e, cp=cp, v=v:
                        e.tensor_tensor(out=gat[cp][:, v * DV:(v + 1) * DV],
                                        in0=o_w[cp][:, v * DV:(v + 1) * DV],
                                        in1=zs_t[cp][:, v * DV:(v + 1) * DV],
                                        op=ALU.mult))

                    t_gtt = step(TEN, st, [t_gt, gs_dma[-1] if (cc % 16 == 0 and len(gs_dma))
                                           else None],
                                 lambda e, cp=cp, v=v:
                                 e.transpose(PT[:, 640:768], gat[cp][:, v * DV:(v + 1) * DV],
                                             idb[:]))
                    t_gsc = step(VEC, sv, [t_gtt], lambda e, cp=cp, v=v, cc=cc:
                                 e.tensor_copy(out=gstrip[v][:, (cc % 16) * 128:(cc % 16 + 1) * 128],
                                               in_=PT[:, 640:768]))
                    gs_cp[v] = t_gsc

                if per_v_sb:
                    chunk_sb = per_v_sb[-1]
                chunk_ten_end[cp] = (st, st.n)

                if stage == "B1" and cc == 0 and TAP == "canary":
                    tcan = step(VEC, sv, [(st, st.n), (ss, ss.n)], lambda e:
                                e.memset(eT[0][0][:], 3.25))
                    tap_tok = step(SYNC, sd, [tcan], lambda e:
                                   e.dma_start(out=y_seg[0:128, 0:128],
                                               in_=eT[0][0][:, 0:128]), k=16)
                    tap_tok2 = step(GPS, sg, [tcan], lambda e:
                                    e.dma_start(out=y_seg[128:256, 0:128],
                                                in_=eT[0][0][:, 0:128]), k=16)
                    chunk_ten_end[cp] = tap_tok
                    chunk_sb = tap_tok
                elif stage == "B1" and cc == 0 and TAP:
                    def tapf(e, cp=cp):
                        tiles = []
                        if TAP == "act":
                            tiles = [(act_q[0][:], 0), (act_k[0][:], 64),
                                     (act_v0[0][:, 0:512], 128), (act_v1[0][:, 0:512], 192)]
                        elif TAP == "vecs":
                            tiles = [(cums[cp][:], 0), (colw[cp][:], 8), (lamc[cp][:], 40),
                                     (kv_t[cp][:].bitcast(BF16), 48),
                                     (gsp[cp][:].bitcast(BF16), 192)]
                        elif TAP == "AR":
                            tiles = [(Amat[cp][0][:], 0), (Rmat[cp][0][:], 64),
                                     (Ftm[cp][0][:], 128), (qkb[cp][:], 192)]
                        elif TAP == "XU":
                            tiles = [(rhs0[cp][0][:], 0), (uu[cp][0][:], 96),
                                     (kbar[cp][0][:], 160), (S_b[0][:], 192)]
                        elif TAP == "cd":
                            tiles = [(cd_q[:], 0), (cd_k[:], 128)]
                        elif TAP == "mask":
                            tiles = [(trb[:], 0), (idb[:], 128)]
                        elif TAP == "ws":
                            tiles = [(wstage[0][:, 0:1024].bitcast(BF16), 0)]
                        elif TAP == "wq":
                            tiles = [(wqkvT[0][:], 0), (wzT[0][:], 128)]
                        elif TAP == "mix":
                            tiles = [(mix_qk[:, 0:512], 0)]
                        elif TAP == "O":
                            tiles = [(o_w[cp][:].bitcast(BF16), 0), (gat[cp][:], 64),
                                     (zs_t[cp][:], 128)]
                        last = None
                        for ap_, r0 in tiles:
                            pr, fr = ap_.shape[0], ap_.shape[1]
                            dst = dbg[r0:r0 + min(pr, 256 - r0), 0:fr]
                            if str(ap_.dtype) != "dt.bfloat16":
                                ap_ = ap_.bitcast(BF16)
                                dst = dbg[r0:r0 + min(pr, 256 - r0), 0:ap_.shape[1]]
                            last = e.dma_start(out=dst, in_=ap_).then_inc(sg.h, 16)
                        return None
                    ntap = {"act": 4, "vecs": 5, "AR": 4, "XU": 4, "O": 3,
                            "cd": 2, "mix": 1, "ws": 1, "wq": 2, "mask": 2}[TAP]
                    tap_tok = step(GPS, sg, [(sv, sv.n), (ss, ss.n), (st, st.n)],
                                   tapf, k=16 * ntap)
                    chunk_ten_end[cp] = tap_tok
                    chunk_sb = tap_tok

                if (cc % 16 == 15 or cc == NBLK * 4 - 1) and SCAN_LVL >= 19:
                    c0 = (cc // 16) * 16 * 128
                    wgt = (cc % 16 + 1) * 128
                    tgd = step(GPS, sg, [gs_cp[0], gs_cp[1]],
                               lambda e, c0=c0, wgt=wgt: (
                        e.dma_start(out=cc_gt_in[0:128, c0:c0 + wgt],
                                    in_=gstrip[0][:, 0:wgt]).then_inc(sg.h, 16),
                        e.dma_start(out=cc_gt_in[128:256, c0:c0 + wgt],
                                    in_=gstrip[1][:, 0:wgt]).then_inc(sg.h, 16),
                        None)[-1], k=32)
                    gs_dma.append(tgd)

        if stage in ("B1", "B"):
            step(SYNC, sd, [tok_ms], lambda e:
                 e.dma_start(out=y_seg[1920:2048, 0:128], in_=ones_t[:]), k=16)
            step(SYNC, sd, [tok_ag12], lambda e:
                 e.dma_start(out=y_seg[1920:2048, 128:256], in_=ones_t[:]), k=16)
            step(SYNC, sd, [ip_tok[0] or tok_ms], lambda e:
                 e.dma_start(out=y_seg[1920:2048, 256:384], in_=ones_t[:]), k=16)
            step(SYNC, sd, [act_tok[0] or tok_ms], lambda e:
                 e.dma_start(out=y_seg[1920:2048, 384:512], in_=ones_t[:]), k=16)
            step(SYNC, sd, [chunk_ten_end[0]], lambda e:
                 e.dma_start(out=y_seg[1920:2048, 512:640], in_=ones_t[:]), k=16)
            step(SYNC, sd, [(sv, sv.n)], lambda e:
                 e.dma_start(out=y_seg[1920:2048, 640:768], in_=ones_t[:]), k=16)
            step(SYNC, sd, [(ss, ss.n)], lambda e:
                 e.dma_start(out=y_seg[1920:2048, 768:896], in_=ones_t[:]), k=16)
            step(SYNC, sd, [(st, st.n)], lambda e:
                 e.dma_start(out=y_seg[1920:2048, 896:1024], in_=ones_t[:]), k=16)
            def dump(e):
                cpd = 1
                items = [
                    (cums[cpd][:], 256, 0, F32),
                    (colw[cpd][:], 256, 16, F32),
                    (lamc[cpd][:], 256, 64, F32),
                    (kv_t[cpd][:], 256, 100, BF16),
                    (v_t[cpd][:], 256, 200, BF16),
                    (Amat[cpd][0][:], 384, 0, BF16),
                    (Rmat[cpd][0][:], 384, 100, BF16),
                    (Ftm[cpd][0][:], 384, 200, BF16),
                    (rhs0[cpd][0][:], 512, 0, BF16),
                    (uu[cpd][0][:], 512, 200, BF16),
                    (kbar[cpd][0][:], 512, 400, BF16),
                    (o_w[cpd][:], 640, 0, F32),
                    (gat[cpd][:], 640, 300, BF16),
                    (S_b[0][:], 768, 0, BF16),
                    (gsp[cpd][:], 768, 200, BF16),
                    (ba_ring[3][:], 768, 300, F32),
                    (z_ring[3][:], 896, 0, BF16),
                    (smc[:], 896, 200, F32),
                    (dtb_t[:, 0:2] if False else smc[:, 0:2], 896, 220, F32),
                ]
                last = None
                for ap_, r0, c0, dt_ in items:
                    pr, fr = ap_.shape[0], ap_.shape[1]
                    if dt_ == F32:
                        dst = y_seg[r0:r0 + pr, c0:c0 + fr]
                    else:
                        dst = y_seg[r0:r0 + pr, c0:c0 + (fr + 1) // 2].bitcast(BF16)[:, 0:fr]
                    last = e.dma_start(out=dst, in_=ap_).then_inc(sd.h, 16)
                return None
            tre = step(VEC, sv, [(st, st.n), (ss, ss.n), (sg, sg.n)], lambda e:
                       e.tensor_tensor(out=colw[1][:, 8:10], in0=ba_ring[3][:, 2:4],
                                       in1=smc[:, 0:2], op=ALU.add))
            step(SYNC, sd, [(sv, sv.n), (ss, ss.n), (st, st.n), (sg, sg.n), tre],
                 dump, k=16 * 19)
            lastg = gs_dma[-1] if gs_dma else None
            tdbg = step(SYNC, sd, [lastg, gs_cp[0], gs_cp[1],
                                   (sv, sv.n), (ss, ss.n), (st, st.n)], lambda e:
                        e.dma_start(out=y_seg[0:256, 0:1024].bitcast(BF16)[:, 0:2048],
                                    in_=cc_gt_in[:, 0:2048]), k=16)
            GPS.append(lambda e: e.wait_ge(sg.h, sg.n))
            _finish(block, SYNC, VEC, SCA, TEN, GPS)
            return nc

        tok_ag3 = step(GPS, scc3, [gs_dma[-1]], lambda e:
                       e.collective_compute("AllGather", ALU.bypass,
                                            replica_groups=[list(range(NCORES))],
                                            ins=[cc_gt_in[:].opt()], outs=[cc_gt_out[:].opt()]))

        es_b.close()
        gt_t = [sb(f"gt_t{k}", [128, 128], BF16) for k in range(16)]
        xc_t = sb("xc_t", [128, D])
        at_t = sb("at_t", [128, D])
        x2_t = sb("x2_t", [128, D])
        f_t = sb("f_t", [128, D])
        d1qT = sb("d1qT", [128, D], FP8)
        wout_r = [sb(f"wout_r{k}", [128, D], BF16) for k in range(16)]
        q2_r = [sb(f"q2_r{k}", [128, D], FP8) for k in range(16)]

        tw = None
        for kt in range(16):
            tw = step(SYNC, sd, [tok_ag3 if kt == 0 else tw,
                                 ip_tok[0], ip_tok[1]], lambda e, kt=kt:
                      e.dma_start(out=wout_r[kt][:],
                                  in_=w_outT[kt * 128:(kt + 1) * 128, :]), k=16)
        tok_wout = tw

        def q2v(kt, half):
            return q2_r[kt][:].bitcast(FP8)[:, half * 1024:(half + 1) * 1024]
        tq2 = step(SYNC, sd, [tok_ag3, tok_q2ag, ip_tok[0], ip_tok[1]], lambda e: (
            [e.dma_start(out=q2_r[kt][:],
                         in_=cc_q2_out[kt * 128:(kt + 1) * 128, :]).then_inc(sd.h, 16)
             for kt in range(16)],
            None)[-1], k=16 * 16)

        y_dma = []
        prev_ffn = None
        prev_fe = None
        prev_tgl = None
        for i in range(16):
            tof = step(VEC, sv, [tok_cl, prev_tgl], lambda e, i=i: (
                [e.tensor_scalar(out=gt_o[kt][:], in0=gto[:, kt:kt + 1],
                                 scalar1=i * 128, scalar2=None, op0=ALU.add)
                 for kt in range(16)])[-1])
            tgl = step(GPS, sg, [tof, tok_ag3, y_dma[i - 1] if i else None,
                                 tok_wout, tq2], lambda e: (
                [e.indirect_dma_start(out=gt_t[kt][:], out_offset=None,
                                      in_=cc_gt_out[0:128, 0:128],
                                      in_offset=IndirectOffsetOnAxis(ap=gt_o[kt][:], axis=1)
                                      ).then_inc(sg.h, 16)
                 for kt in range(16)],
                None)[-1], k=16 * 16)
            prev_tgl = tgl
            txl = step(SYNC, sd, [tgl], lambda e, i=i:
                       e.dma_start(out=xc_t[:], in_=x_seg[i * 128:(i + 1) * 128, :]), k=16)

            t_op = step(TEN, st, [tgl, tok_wout, prev_ffn, prev_fe,
                                  (sv, sv.n) if i == 0 else None,
                                  (ss, ss.n) if i == 0 else None], lambda e: (
                [[e.matmul(P[j][:], gt_t[kt][:], wout_r[kt][:, j * 512:(j + 1) * 512],
                           start=(kt == 0), stop=(kt == 15)) for kt in range(16)]
                 for j in range(4)])[-1][-1])
            t_ope = step(VEC, sv, [t_op], lambda e: (
                [e.tensor_copy(out=at_t[:, j * 512:(j + 1) * 512], in_=P[j][:])
                 for j in range(4)])[-1])

            t1a = step(VEC, sv, [t_ope], lambda e:
                       e.reduce_sum(out=acol[:, 0:1], in_=at_t[:], axis=AX.X))
            t1 = step(VEC, sv, [t1a], lambda e:
                      e.tensor_scalar(out=acol[:, 1:2], in0=acol[:, 0:1], scalar1=1.0 / D,
                                      scalar2=None, op0=ALU.mult))
            t2 = step(VEC, sv, [t1], lambda e:
                      e.tensor_scalar(out=at_t[:], in0=at_t[:], scalar1=acol[:, 1:2],
                                      scalar2=None, op0=ALU.subtract))
            t3 = step(SCA, ss, [t2], lambda e:
                      e.activation(out=sqtile[:], in_=at_t[:], func=AF.Square))
            t4a = step(VEC, sv, [t3], lambda e:
                       e.reduce_sum(out=acol[:, 2:3], in_=sqtile[:], axis=AX.X))
            t4 = step(VEC, sv, [t4a], lambda e:
                      e.tensor_scalar(out=acol[:, 3:4], in0=acol[:, 2:3], scalar1=1.0 / D,
                                      scalar2=1e-5, op0=ALU.mult, op1=ALU.add))
            t5 = step(SCA, ss, [t4], lambda e:
                      e.activation(out=acol[:, 4:5], in_=acol[:, 3:4], func=AF.Sqrt))
            t6 = step(VEC, sv, [t5], lambda e:
                      e.reciprocal(out=acol[:, 5:6], in_=acol[:, 4:5]))
            t7a = step(VEC, sv, [t6, txl], lambda e:
                       e.tensor_scalar(out=at_t[:], in0=at_t[:], scalar1=acol[:, 5:6],
                                       scalar2=None, op0=ALU.mult))
            t7 = step(VEC, sv, [t7a], lambda e:
                      e.tensor_tensor(out=x2_t[:], in0=xc_t[:], in1=at_t[:], op=ALU.add))

            t8a = step(VEC, sv, [t7], lambda e:
                       e.reduce_sum(out=acol[:, 0:1], in_=x2_t[:], axis=AX.X))
            t8 = step(VEC, sv, [t8a], lambda e:
                      e.tensor_scalar(out=acol[:, 1:2], in0=acol[:, 0:1], scalar1=1.0 / D,
                                      scalar2=None, op0=ALU.mult))
            t9 = step(VEC, sv, [t8], lambda e:
                      e.tensor_scalar(out=at_t[:], in0=x2_t[:], scalar1=acol[:, 1:2],
                                      scalar2=None, op0=ALU.subtract))
            t10 = step(SCA, ss, [t9], lambda e:
                       e.activation(out=sqtile[:], in_=at_t[:], func=AF.Square))
            t11a = step(VEC, sv, [t10], lambda e:
                        e.reduce_sum(out=acol[:, 2:3], in_=sqtile[:], axis=AX.X))
            t11 = step(VEC, sv, [t11a], lambda e:
                       e.tensor_scalar(out=acol[:, 3:4], in0=acol[:, 2:3], scalar1=1.0 / D,
                                       scalar2=1e-5, op0=ALU.mult, op1=ALU.add))
            t12 = step(SCA, ss, [t11], lambda e:
                       e.activation(out=acol[:, 4:5], in_=acol[:, 3:4], func=AF.Sqrt))
            t13 = step(VEC, sv, [t12], lambda e:
                       e.reciprocal(out=acol[:, 5:6], in_=acol[:, 4:5]))
            t14 = step(VEC, sv, [t13], lambda e:
                       e.tensor_scalar(out=at_t[:], in0=at_t[:], scalar1=acol[:, 5:6],
                                       scalar2=None, op0=ALU.mult))

            tqz = t14
            for kt in range(16):
                tt = step(TEN, st, [tqz], lambda e, kt=kt:
                          e.transpose(P[4][:, 0:128].bitcast(F32),
                                      at_t[:, kt * 128:(kt + 1) * 128], idf[:]))
                tqz = step(VEC, sv, [tt, tok_smc2], lambda e, kt=kt:
                           e.tensor_scalar(out=d1qT[:, kt * 128:(kt + 1) * 128],
                                           in0=P[4][:, 0:128], scalar1=smc[:, 4:5],
                                           scalar2=None, op0=ALU.mult))

            t_fm = step(TEN, st, [tqz, tq2], lambda e: (
                [[e.matmul(P[j][:], d1qT[:, kt * 128:(kt + 1) * 128],
                           q2_r[kt][:, j * 512:(j + 1) * 512],
                           start=(kt == 0), stop=(kt == 15)) for kt in range(16)]
                 for j in range(4)])[-1][-1])
            prev_ffn = t_fm
            t_fe = step(VEC, sv, [t_fm], lambda e: (
                [e.tensor_scalar(out=f_t[:, j * 512:(j + 1) * 512], in0=P[j][:],
                                 scalar1=smc[:, 6:7], scalar2=None, op0=ALU.mult)
                 for j in range(4)])[-1])

            u1a = step(VEC, sv, [t_fe], lambda e:
                       e.reduce_sum(out=acol[:, 0:1], in_=f_t[:], axis=AX.X))
            u1 = step(VEC, sv, [u1a], lambda e:
                      e.tensor_scalar(out=acol[:, 1:2], in0=acol[:, 0:1], scalar1=1.0 / D,
                                      scalar2=None, op0=ALU.mult))
            u2 = step(VEC, sv, [u1], lambda e:
                      e.tensor_scalar(out=f_t[:], in0=f_t[:], scalar1=acol[:, 1:2],
                                      scalar2=None, op0=ALU.subtract))
            u3 = step(SCA, ss, [u2], lambda e:
                      e.activation(out=sqtile[:], in_=f_t[:], func=AF.Square))
            u4a = step(VEC, sv, [u3], lambda e:
                       e.reduce_sum(out=acol[:, 2:3], in_=sqtile[:], axis=AX.X))
            u4 = step(VEC, sv, [u4a], lambda e:
                      e.tensor_scalar(out=acol[:, 3:4], in0=acol[:, 2:3], scalar1=1.0 / D,
                                      scalar2=1e-5, op0=ALU.mult, op1=ALU.add))
            u5 = step(SCA, ss, [u4], lambda e:
                      e.activation(out=acol[:, 4:5], in_=acol[:, 3:4], func=AF.Sqrt))
            u6 = step(VEC, sv, [u5], lambda e:
                      e.reciprocal(out=acol[:, 5:6], in_=acol[:, 4:5]))
            u7a = step(VEC, sv, [u6], lambda e:
                       e.tensor_scalar(out=f_t[:], in0=f_t[:], scalar1=acol[:, 5:6],
                                       scalar2=None, op0=ALU.mult))
            u7 = step(VEC, sv, [u7a], lambda e:
                      e.tensor_tensor(out=f_t[:], in0=f_t[:], in1=x2_t[:], op=ALU.add))

            ty = step(GPS, sg, [u7], lambda e, i=i:
                      e.dma_start(out=y_seg[i * 128:(i + 1) * 128, :], in_=f_t[:]), k=16)
            y_dma.append(ty)
            prev_fe = t_fe

        GPS.append(lambda e: e.wait_ge(sg.h, sg.n))
        _finish(block, SYNC, VEC, SCA, TEN, GPS)
    return nc


def _finish(block, SYNC, VEC, SCA, TEN, GPS):
    @block.sync
    def _(e):
        for f in SYNC:
            f(e)

    @block.vector
    def _(e):
        for f in VEC:
            f(e)

    @block.scalar
    def _(e):
        for f in SCA:
            f(e)

    @block.tensor
    def _(e):
        for f in TEN:
            f(e)

    @block.gpsimd
    def _(e):
        for f in GPS:
            f(e)


_NC_CACHE = {}


def _get_nc():
    if STAGE not in _NC_CACHE:
        _NC_CACHE[STAGE] = build_nc(STAGE)
    return _NC_CACHE[STAGE]


def make_in_maps(inputs):
    x = np.ascontiguousarray(np.asarray(inputs["L0_attn_x"], np.float32).reshape(T, D))
    wip = np.asarray(inputs["L0_attn_in_proj_w"], np.float32)
    convw = np.asarray(inputs["L0_attn_conv_w"], np.float32)
    dt_bias = np.asarray(inputs["L0_attn_dt_bias"], np.float32)
    a_log = np.asarray(inputs["L0_attn_A_log"], np.float32)
    nw = np.asarray(inputs["L0_attn_norm_weight"], np.float32)
    wout = np.asarray(inputs["L0_attn_out_proj_w"], np.float32)
    w2 = np.asarray(inputs["L0_ffn_input2"], np.float32)
    scl = np.asarray(inputs["L0_ffn_scale"], np.float32).reshape(1, 1)
    scl1 = np.asarray(inputs["L0_ffn_scale1"], np.float32).reshape(1, 1)

    ii = np.arange(C)
    mA_ = np.where(ii[None, :] < ii[:, None], 0.0, NEG).astype(np.float32)
    mR_ = np.where(ii[None, :] > ii[:, None], 0.0, NEG).astype(np.float32)
    mF_ = np.where(ii[None, :] >= ii[:, None], 0.0, NEG).astype(np.float32)
    tri = (ii[:, None] <= ii[None, :]).astype(ml_dtypes.bfloat16)
    idb_ = np.eye(128, dtype=ml_dtypes.bfloat16)
    idf_ = np.eye(128, dtype=np.float32)

    in_maps = []
    for c_ in range(NCORES):
        kh = c_
        q_rows = wip[kh * DK:(kh + 1) * DK]
        k_rows = wip[512 + kh * DK:512 + (kh + 1) * DK]
        v_rows = wip[1024 + c_ * 256:1024 + (c_ + 1) * 256]
        z_rows = wip[3072 + c_ * 256:3072 + (c_ + 1) * 256]
        b_rows = wip[5120 + 2 * c_:5120 + 2 * c_ + 2]
        a_rows = wip[5136 + 2 * c_:5136 + 2 * c_ + 2]
        w_qkv = np.concatenate([q_rows, k_rows, v_rows], axis=0)
        w_ba = np.concatenate([b_rows, a_rows,
                               np.zeros((12, D), np.float32)], axis=0)

        cw_q = convw[kh * DK:(kh + 1) * DK]
        cw_k = convw[512 + kh * DK:512 + (kh + 1) * DK]
        cw_v = convw[1024 + c_ * 256:1024 + (c_ + 1) * 256]
        dq = np.zeros((128, KS * DK), np.float32)
        dk_ = np.zeros((128, KS * DK), np.float32)
        dv0 = np.zeros((128, KS * 128), np.float32)
        dv1 = np.zeros((128, KS * 128), np.float32)
        for tap in range(KS):
            dq[0:DK, tap * DK:(tap + 1) * DK] = np.diag(cw_q[:, tap])
            dk_[DK:128, tap * DK:(tap + 1) * DK] = np.diag(cw_k[:, tap])
            dv0[:, tap * 128:(tap + 1) * 128] = np.diag(cw_v[0:128, tap])
            dv1[:, tap * 128:(tap + 1) * 128] = np.diag(cw_v[128:256, tap])

        smalls = np.zeros((1, 192), np.float32)
        smalls[0, 0:2] = dt_bias[2 * c_:2 * c_ + 2]
        smalls[0, 2:4] = a_log[2 * c_:2 * c_ + 2]
        smalls[0, 4:5] = scl[0]
        smalls[0, 5:6] = scl1[0]
        smalls[0, 64:64 + DV] = nw

        in_maps.append({
            "x_seg": np.ascontiguousarray(x[c_ * SEG:(c_ + 1) * SEG]),
            "w_qkvT": np.ascontiguousarray(w_qkv.T).astype(ml_dtypes.bfloat16),
            "w_zT": np.ascontiguousarray(z_rows.T).astype(ml_dtypes.bfloat16),
            "w_baT": np.ascontiguousarray(w_ba.T).astype(ml_dtypes.bfloat16),
            "w_outT": np.ascontiguousarray(wout.T).astype(ml_dtypes.bfloat16),
            "w2_slice": np.ascontiguousarray(w2[c_ * 256:(c_ + 1) * 256]),
            "convd_q": dq.astype(ml_dtypes.bfloat16),
            "convd_k": dk_.astype(ml_dtypes.bfloat16),
            "convd_v0": dv0.astype(ml_dtypes.bfloat16),
            "convd_v1": dv1.astype(ml_dtypes.bfloat16),
            "smalls": smalls,
            "maskA": mA_, "maskR": mR_, "maskFt": mF_,
            "triu_b": tri, "ident_b": idb_, "ident_f": idf_,
            "gtoff": (np.arange(16)[None, :] * 128 * T
                      + np.arange(128)[:, None] * T + 2048 * c_).astype(np.int32),
        })
    return in_maps


def kernel(**inputs):
    nc = _get_nc()
    in_maps = make_in_maps(inputs)
    res = run_bass_kernel_spmd(nc, in_maps, core_ids=list(range(NCORES)))
    out = np.concatenate([res.results[c_]["y_seg"] for c_ in range(NCORES)], axis=0)
    return out.reshape(1, T, D)


# version 04 (final)
# speedup vs baseline: 1.0097x; 1.0020x over previous
import numpy as np
import ml_dtypes
from contextlib import ExitStack

import concourse.bass as bass
import concourse.mybir as mybir
from concourse.bass_utils import run_bass_kernel_spmd
from concourse.bass_core_rust import IndirectOffsetOnAxis

F32 = mybir.dt.float32
BF16 = mybir.dt.bfloat16
FP8 = mybir.dt.float8e4
AF = mybir.ActivationFunctionType
ALU = mybir.AluOpType
AX = mybir.AxisListType

NCORES = 8
T = 16384
D = 2048
SEG = T // NCORES
C = 128
DK = 64
DV = 128
VH = 2
KS = 4
TB = 512
NTB = T // TB
SCALE = float(DK) ** -0.5
NEG = -1.0e30

STAGE = "ALL"
import os as _os
SCAN_LVL = int(_os.environ.get("SCAN_LVL", "99"))
TAP = _os.environ.get("TAP", "")


def build_nc(stage="ALL"):
    nc = bass.Bass("TRN2", target_bir_lowering=False)

    def din(name, shape, dt=F32):
        return nc.dram_tensor(name, shape, dt, kind="ExternalInput").ap()

    x_seg = din("x_seg", [SEG, D])
    w_qkvT = din("w_qkvT", [D, 384], BF16)
    w_zT = din("w_zT", [D, 256], BF16)
    w_baT = din("w_baT", [D, 16], BF16)
    w_outT = din("w_outT", [D, D], BF16)
    w2_slice = din("w2_slice", [D // NCORES, D])
    convd_q = din("convd_q", [128, KS * DK], BF16)
    convd_k = din("convd_k", [128, KS * DK], BF16)
    convd_v0 = din("convd_v0", [128, KS * 128], BF16)
    convd_v1 = din("convd_v1", [128, KS * 128], BF16)
    smalls = din("smalls", [1, 192])
    maskA = din("maskA", [C, C])
    maskR = din("maskR", [C, C])
    maskFt = din("maskFt", [C, C])
    triu_b = din("triu_b", [C, C], BF16)
    ident_b = din("ident_b", [128, 128], BF16)
    ident_f = din("ident_f", [128, 128], F32)
    gtoff = din("gtoff", [128, 16], mybir.dt.int32)

    y_seg = nc.dram_tensor("y_seg", [SEG, D], F32, kind="ExternalOutput").ap()

    cc_ln_in = nc.dram_tensor("cc_ln_in", [D, SEG], BF16, kind="Internal").ap()
    cc_ln_out = nc.dram_tensor("cc_ln_out", [NCORES * D, SEG], BF16, kind="Internal").ap()
    cc_q2_in = nc.dram_tensor("cc_q2_in", [D // NCORES, D], FP8, kind="Internal").ap()
    cc_q2_out = nc.dram_tensor("cc_q2_out", [D, D], FP8, kind="Internal").ap()
    cc_gt_in = nc.dram_tensor("cc_gt_in", [VH * DV, T], BF16, kind="Internal").ap()
    cc_gt_out = nc.dram_tensor("cc_gt_out", [NCORES * VH * DV, T], BF16, kind="Internal").ap()
    dbg = nc.dram_tensor("dbg", [256, 2048], BF16, kind="ExternalOutput").ap() \
        if stage in ("A", "B1", "B") else None

    es = ExitStack()

    def sb(name, shape, dt=F32):
        return es.enter_context(nc.sbuf_tensor(name, shape, dt))

    def pt(name, shape, dt=F32):
        return es.enter_context(nc.psum_tensor(name, shape, dt))

    SYNC, VEC, SCA, TEN, GPS = [], [], [], [], []
    STEPLOG = []
    nc._steplog = STEPLOG

    class S:
        def __init__(self, name):
            self.h = es.enter_context(nc.semaphore(name))
            self.n = 0

    def step(lst, sem, waits, fn, k=1):
        wl = {}
        for t in waits:
            if t is None:
                continue
            s_, n_ = t
            if n_ > wl.get(s_, -1):
                wl[s_] = n_
        tgt = None
        if sem is not None:
            sem.n += k
            tgt = sem.n

        def closure(e, wl=wl, fn=fn, sem=sem, k=k):
            for s_, n_ in wl.items():
                if n_ > 0:
                    e.wait_ge(s_.h, n_)
            last = fn(e)
            if sem is not None and last is not None:
                last.then_inc(sem.h, k)

        STEPLOG.append((id(lst), {id(s_): n_ for s_, n_ in wl.items()},
                        id(sem) if sem else None, tgt, len(lst)))
        lst.append(closure)
        return (sem, tgt) if sem is not None else None

    with es:
        ones_r = sb("ones_r", [1, 128])
        ones_t = sb("ones_t", [128, 128])
        one_col = sb("one_col", [128, 1])
        mA = sb("mA", [C, C]); mR = sb("mR", [C, C]); mFt = sb("mFt", [C, C])
        trb = sb("trb", [C, C], BF16)
        idb = sb("idb", [128, 128], BF16)
        idf = sb("idf", [128, 128], F32)
        small_row = sb("small_row", [1, 192])
        pbc = sb("pbc", [128, 192])
        nw_t = sb("nw_t", [128, DV])
        smc = sb("smc", [128, 8])

        wstage = [sb(f"wstage{i}", [128, D]) for i in range(2)]
        atile = sb("atile", [128, D])
        sqtile = sb("sqtile", [128, D])
        acol = sb("acol", [128, 8])
        gto = sb("gto", [128, 16], mybir.dt.int32)
        gt_o = [sb(f"gt_o{k}", [128, 1], mybir.dt.int32) for k in range(16)]

        es_b = es.enter_context(ExitStack())

        def sbb(name, shape, dt=F32):
            return es_b.enter_context(nc.sbuf_tensor(name, shape, dt))

        wqkvT = [sbb(f"wqkvT{k}", [128, 384], BF16) for k in range(16)]
        wzT = [sbb(f"wzT{k}", [128, 256], BF16) for k in range(16)]
        wbaT = [sbb(f"wbaT{k}", [128, 16], BF16) for k in range(16)]
        cd_q = sbb("cd_q", [128, KS * DK], BF16)
        cd_k = sbb("cd_k", [128, KS * DK], BF16)
        cd_v0 = sbb("cd_v0", [128, KS * 128], BF16)
        cd_v1 = sbb("cd_v1", [128, KS * 128], BF16)

        lnbf = [sbb(f"lnbf{i}", [128, D], BF16) for i in range(2)]
        tstg = [sbb(f"tstg{i}", [128, 128], BF16) for i in range(2)]

        lnx = [[sbb(f"lnx{i}_{k}", [128, TB], BF16) for k in range(16)] for i in range(2)]
        mix_qk = sbb("mix_qk", [128, 3 + TB], BF16)
        mix_v0 = sbb("mix_v0", [128, 3 + TB], BF16)
        mix_v1 = sbb("mix_v1", [128, 3 + TB], BF16)
        act_q = [sbb(f"act_q{i}", [64, TB], BF16) for i in range(2)]
        act_k = [sbb(f"act_k{i}", [64, TB], BF16) for i in range(2)]
        act_v0 = [sbb(f"act_v0{i}", [128, TB], BF16) for i in range(2)]
        act_v1 = [sbb(f"act_v1{i}", [128, TB], BF16) for i in range(2)]
        z_ring = [sbb(f"z_ring{i}", [128, 256], BF16) for i in range(8)]
        ba_ring = [sbb(f"ba_ring{i}", [128, 4]) for i in range(8)]

        kv_t = [sbb(f"kv_t{i}", [128, 128], BF16) for i in range(2)]
        v_t = [sbb(f"v_t{i}", [128, 256], BF16) for i in range(2)]
        sqw = [sbb(f"sqw{i}", [128, 128]) for i in range(2)]
        eT = [[sbb(f"eT{i}_{j}", [128, 128]) for j in range(3)] for i in range(2)]
        gsp = [sbb(f"gsp{i}", [128, 16], BF16) for i in range(2)]
        colpA = [sbb(f"colpA{i}", [128, 96]) for i in range(2)]
        colpB = [sbb(f"colpB{i}", [128, 96]) for i in range(2)]
        rows6 = [sbb(f"rows6_{i}", [1, 768]) for i in range(2)]
        cums = [sbb(f"cums{i}", [128, 8]) for i in range(2)]
        colw = [sbb(f"colw{i}", [128, 24]) for i in range(2)]
        lamc = [sbb(f"lamc{i}", [128, 8]) for i in range(2)]
        qkb = [sbb(f"qkb{i}", [128, 128], BF16) for i in range(2)]
        Amat = [[sbb(f"Amat{i}_{v}", [C, C], BF16) for v in range(VH)] for i in range(2)]
        Rmat = [[sbb(f"Rmat{i}_{v}", [C, C], BF16) for v in range(VH)] for i in range(2)]
        Apow = [[sbb(f"Apow{i}_{j}", [C, C], BF16) for j in range(2)] for i in range(2)]
        Rpow = [[sbb(f"Rpow{i}_{j}", [C, C], BF16) for j in range(2)] for i in range(2)]
        Ftm = [[sbb(f"Ftm{i}_{v}", [C, C], BF16) for v in range(VH)] for i in range(2)]
        Xs = [[sbb(f"Xs{i}_{j}", [C, DK + DV], BF16) for j in range(2)] for i in range(2)]
        rhs0 = [[sbb(f"rhs0{i}_{v}", [C, DK + DV], BF16) for v in range(VH)] for i in range(2)]
        kbar = [[sbb(f"kbar{i}_{v}", [C, DK], BF16) for v in range(VH)] for i in range(2)]
        wtT = [sbb(f"wtT{i}", [DK, C], BF16) for i in range(2)]
        uu = [[sbb(f"uu{i}_{v}", [C, DV], BF16) for v in range(VH)] for i in range(2)]
        S_f = [[sbb(f"S_f{v}_{j}", [DK, DV]) for j in range(2)] for v in range(VH)]
        S_b = [sbb(f"S_b{v}", [DK, DV], BF16) for v in range(VH)]
        o_w = [sbb(f"o_w{i}", [128, 256]) for i in range(2)]
        zs_t = [sbb(f"zs_t{i}", [128, 256], BF16) for i in range(2)]
        gat = [sbb(f"gat{i}", [128, 256], BF16) for i in range(2)]
        gstrip = [sbb(f"gstrip{v}", [128, 2048], BF16) for v in range(VH)]


        P = [pt(f"P{i}", [128, 512]) for i in range(7)]
        PT = pt("PT", [128, 1024], BF16)

        sd = S("sd"); sg = S("sg"); sv = S("sv"); ss = S("ss"); st = S("st"); scc = S("scc")
        scc1 = S("scc1"); scc2 = S("scc2"); scc3 = S("scc3")

        block = es.enter_context(nc.Block())

        tok_cl = step(SYNC, sd, [], lambda e: (
            e.dma_start(out=mA[:], in_=maskA[:]).then_inc(sd.h, 16),
            e.dma_start(out=mR[:], in_=maskR[:]).then_inc(sd.h, 16),
            e.dma_start(out=mFt[:], in_=maskFt[:]).then_inc(sd.h, 16),
            e.dma_start(out=trb[:], in_=triu_b[:]).then_inc(sd.h, 16),
            e.dma_start(out=idb[:], in_=ident_b[:]).then_inc(sd.h, 16),
            e.dma_start(out=idf[:], in_=ident_f[:]).then_inc(sd.h, 16),
            e.dma_start(out=gto[:], in_=gtoff[:]).then_inc(sd.h, 16),
            e.dma_start(out=small_row[:], in_=smalls[:]).then_inc(sd.h, 16),
            None)[-1], k=16 * 8)

        tok_ms = step(VEC, sv, [], lambda e: (
            e.memset(ones_r[:], 1.0),
            e.memset(ones_t[:], 1.0),
            e.memset(one_col[:], 1.0),
            e.memset(gsp[0][:], 0.0),
            e.memset(gsp[1][:], 0.0),
            e.memset(colpA[0][:], 0.0),
            e.memset(colpA[1][:], 0.0),
            e.memset(colpB[0][:], 0.0),
            e.memset(colpB[1][:], 0.0),
            e.memset(mix_qk[:, 0:3], 0.0),
            e.memset(mix_v0[:, 0:3], 0.0),
            e.memset(mix_v1[:, 0:3], 0.0))[-1])

        tok_s0 = step(VEC, sv, [], lambda e: (
            [e.memset(S_f[v][0][:], 0.0) for v in range(VH)],
            [e.memset(S_b[v][:], 0.0) for v in range(VH)])[-1][-1])

        tok_bmm = step(TEN, st, [tok_cl, tok_ms], lambda e:
                       e.matmul(P[0][:, 0:192], ones_r[:], small_row[:],
                                start=True, stop=True))
        tok_bcp = step(VEC, sv, [tok_bmm], lambda e: (
            e.tensor_copy(out=pbc[:], in_=P[0][:, 0:192])))
        tok_nw = step(VEC, sv, [tok_bcp], lambda e:
                      e.tensor_copy(out=nw_t[:], in_=pbc[:, 64:64 + DV]))
        tok_smc1 = step(VEC, sv, [tok_bcp], lambda e: (
            e.tensor_copy(out=smc[:, 0:2], in_=pbc[:, 0:2]),
            e.tensor_copy(out=smc[:, 4:6], in_=pbc[:, 4:6]))[-1])
        tok_eA = step(SCA, ss, [tok_smc1], lambda e:
                      e.activation(out=smc[:, 2:4], in_=pbc[:, 2:4], func=AF.Exp))
        tok_smc = step(VEC, sv, [tok_eA], lambda e: (
            e.tensor_scalar(out=smc[:, 2:4], in0=smc[:, 2:4], scalar1=-1.0,
                            scalar2=None, op0=ALU.mult),
            e.tensor_tensor(out=smc[:, 7:8], in0=smc[:, 4:5], in1=smc[:, 5:6], op=ALU.mult))[-1])
        tok_smc2 = step(VEC, sv, [tok_smc], lambda e:
                        e.reciprocal(out=smc[:, 6:7], in_=smc[:, 7:8]))

        tok_w = None
        for kt in range(16):
            tok_w = step(SYNC, sd, [], lambda e, kt=kt: (
                e.dma_start(out=wqkvT[kt][:],
                            in_=w_qkvT[kt * 128:(kt + 1) * 128, :]).then_inc(sd.h, 16),
                e.dma_start(out=wzT[kt][:],
                            in_=w_zT[kt * 128:(kt + 1) * 128, :]).then_inc(sd.h, 16),
                e.dma_start(out=wbaT[kt][:],
                            in_=w_baT[kt * 128:(kt + 1) * 128, :]).then_inc(sd.h, 16),
                None)[-1], k=48)

        tok_cd = step(SYNC, sd, [], lambda e: (
            e.dma_start(out=cd_q[:], in_=convd_q[:]).then_inc(sd.h, 16),
            e.dma_start(out=cd_k[:], in_=convd_k[:]).then_inc(sd.h, 16),
            e.dma_start(out=cd_v0[:], in_=convd_v0[:]).then_inc(sd.h, 16),
            e.dma_start(out=cd_v1[:], in_=convd_v1[:]).then_inc(sd.h, 16),
            None)[-1], k=64)

        ln_done = [None, None]
        piece_dma = [None, None]
        last_piece_dma = None
        tr_prev = None
        for i in range(16):
            lp = i % 2
            tx = step(SYNC, sd, [ln_done[lp], tok_cd], lambda e, i=i:
                      e.dma_start(out=atile[:], in_=x_seg[i * 128:(i + 1) * 128, :]), k=16)
            t1a = step(VEC, sv, [tx], lambda e:
                       e.reduce_sum(out=acol[:, 0:1], in_=atile[:], axis=AX.X))
            t1 = step(VEC, sv, [t1a], lambda e:
                      e.tensor_scalar(out=acol[:, 1:2], in0=acol[:, 0:1], scalar1=1.0 / D,
                                      scalar2=None, op0=ALU.mult))
            t2 = step(VEC, sv, [t1], lambda e:
                      e.tensor_scalar(out=atile[:], in0=atile[:], scalar1=acol[:, 1:2],
                                      scalar2=None, op0=ALU.subtract))
            t3 = step(SCA, ss, [t2], lambda e:
                      e.activation(out=sqtile[:], in_=atile[:], func=AF.Square))
            t4a = step(VEC, sv, [t3], lambda e:
                       e.reduce_sum(out=acol[:, 2:3], in_=sqtile[:], axis=AX.X))
            t4 = step(VEC, sv, [t4a], lambda e:
                      e.tensor_scalar(out=acol[:, 3:4], in0=acol[:, 2:3], scalar1=1.0 / D,
                                      scalar2=1e-5, op0=ALU.mult, op1=ALU.add))
            t5 = step(SCA, ss, [t4], lambda e:
                      e.activation(out=acol[:, 4:5], in_=acol[:, 3:4], func=AF.Sqrt))
            t6 = step(VEC, sv, [t5], lambda e:
                      e.reciprocal(out=acol[:, 5:6], in_=acol[:, 4:5]))
            t7 = step(VEC, sv, [t6], lambda e, lp=lp:
                      e.tensor_scalar(out=lnbf[lp][:], in0=atile[:], scalar1=acol[:, 5:6],
                                      scalar2=None, op0=ALU.mult))
            for fb in range(16):
                pp = fb % 2
                trp = step(TEN, st, [t7, piece_dma[pp], tr_prev], lambda e, lp=lp, fb=fb:
                           e.transpose(PT[:, (fb % 4) * 128:(fb % 4 + 1) * 128],
                                       lnbf[lp][:, fb * 128:(fb + 1) * 128], idb[:]))
                tcp = step(VEC, sv, [trp], lambda e, pp=pp, fb=fb:
                           e.tensor_copy(out=tstg[pp][:],
                                         in_=PT[:, (fb % 4) * 128:(fb % 4 + 1) * 128]))
                tr_prev = tcp
                tdm = step(SYNC, sd, [tcp], lambda e, pp=pp, fb=fb, i=i:
                           e.dma_start(out=cc_ln_in[fb * 128:(fb + 1) * 128,
                                                    i * 128:(i + 1) * 128],
                                       in_=tstg[pp][:]), k=16)
                piece_dma[pp] = tdm
                last_piece_dma = tdm
            ln_done[lp] = t7
        strip_dma = [last_piece_dma]

        q2_dma = []
        for i in range(2):
            tw = step(SYNC, sd, [tok_cd, q2_dma[-1] if q2_dma else None],
                      lambda e, i=i: e.dma_start(out=wstage[1][:, 0:D],
                                                 in_=w2_slice[i * 128:(i + 1) * 128, :]), k=16)
            tq = step(VEC, sv, [tw, tok_smc2, ln_done[0], ln_done[1]], lambda e:
                      e.tensor_scalar(out=atile[:].bitcast(FP8)[:, 0:D], in0=wstage[1][:, 0:D],
                                      scalar1=smc[:, 5:6], scalar2=None, op0=ALU.mult))
            td = step(SYNC, sd, [tq], lambda e, i=i:
                      e.dma_start(out=cc_q2_in[i * 128:(i + 1) * 128, :],
                                  in_=atile[:].bitcast(FP8)[:, 0:D]), k=16)
            q2_dma.append(td)

        step(GPS, None, [strip_dma[-1], q2_dma[-1]], lambda e: (
            e.collective_compute("AllGather", ALU.bypass,
                                 replica_groups=[list(range(NCORES))],
                                 ins=[cc_ln_in[:].opt()],
                                 outs=[cc_ln_out[:].opt()]).then_inc(scc1.h, 1),
            e.collective_compute("AllGather", ALU.bypass,
                                 replica_groups=[list(range(NCORES))],
                                 ins=[cc_q2_in[:].opt()],
                                 outs=[cc_q2_out[:].opt()]).then_inc(scc2.h, 1),
            None)[-1])
        scc1.n = 1; scc2.n = 1
        tok_ag12 = (scc1, 1)
        tok_q2ag = (scc2, 1)

        if stage == "A":
            tdbg = step(GPS, sg, [tok_ag12], lambda e: (
                [e.dma_start(out=dbg[r * 16:(r + 1) * 16, :],
                             in_=cc_ln_out[r * 1024 + 0:r * 1024 + 16, :]).then_inc(sg.h, 16)
                 for r in range(16)],
                None)[-1], k=16 * 16)
            GPS.append(lambda e: e.wait_ge(sg.h, sg.n))
            _finish(block, SYNC, VEC, SCA, TEN, GPS)
            return nc

        ip_tok = [None, None]
        act_tok = [None, None]
        conv_tok = [None, None]
        chunk_ten_end = [tok_s0, tok_s0]
        chunk_sb = tok_s0
        zsil_tok = [None] * 8
        ba_tok = [None] * 8
        gs_dma = []
        gs_cp = [None, None]

        NBLK = 1 if stage == "B1" else NTB

        for b in range(NBLK):
            par = b % 2
            seg_i = (b * TB) // SEG
            off = (b * TB) % SEG

            tlnx = step(SYNC, sd, [tok_ag12, ip_tok[par]], lambda e, par=par, seg_i=seg_i, off=off: (
                [e.dma_start(out=lnx[par][kt][:],
                             in_=cc_ln_out[seg_i * D + kt * 128:seg_i * D + (kt + 1) * 128,
                                           off:off + TB]).then_inc(sd.h, 16)
                 for kt in range(16)],
                None)[-1], k=16 * 16)

            tip = step(TEN, st, [tlnx, tok_w, conv_tok[par], act_tok[par],
                                 (sv, sv.n), (ss, ss.n)], lambda e, par=par: (
                [(e.matmul(P[0][:], wqkvT[kt][:, 0:128], lnx[par][kt][:],
                           start=(kt == 0), stop=(kt == 15)),
                  e.matmul(P[1][:], wqkvT[kt][:, 128:256], lnx[par][kt][:],
                           start=(kt == 0), stop=(kt == 15)),
                  e.matmul(P[2][:], wqkvT[kt][:, 256:384], lnx[par][kt][:],
                           start=(kt == 0), stop=(kt == 15))) for kt in range(16)],
                [[e.matmul((P[3] if sq < 2 else P[4])[:, (sq % 2) * 256:(sq % 2) * 256 + 256],
                           lnx[par][kt][:, sq * 128:(sq + 1) * 128], wzT[kt][:],
                           start=(kt == 0), stop=(kt == 15)) for kt in range(16)]
                 for sq in range(4)],
                [[e.matmul(P[5][:, sq * 16:(sq + 1) * 16],
                           lnx[par][kt][:, sq * 128:(sq + 1) * 128], wbaT[kt][:],
                           start=(kt == 0), stop=(kt == 15)) for kt in range(16)]
                 for sq in range(4)])[-1][-1][-1])
            ip_tok[par] = tip

            def mk_evict(b, par, tip):
                def f(e):
                    if b > 0:
                        e.tensor_copy(out=mix_qk[:, 0:3], in_=mix_qk[:, TB:TB + 3])
                        e.tensor_copy(out=mix_v0[:, 0:3], in_=mix_v0[:, TB:TB + 3])
                        e.tensor_copy(out=mix_v1[:, 0:3], in_=mix_v1[:, TB:TB + 3])
                    e.tensor_copy(out=mix_qk[:, 3:3 + TB], in_=P[0][:])
                    e.tensor_copy(out=mix_v0[:, 3:3 + TB], in_=P[1][:])
                    e.tensor_copy(out=mix_v1[:, 3:3 + TB], in_=P[2][:])
                    last = None
                    for sq in range(4):
                        pz = P[3] if sq < 2 else P[4]
                        e.tensor_copy(out=z_ring[(b * 4 + sq) % 8][:],
                                      in_=pz[:, (sq % 2) * 256:(sq % 2) * 256 + 256])
                        last = e.tensor_copy(out=ba_ring[(b * 4 + sq) % 8][:],
                                             in_=P[5][:, sq * 16:sq * 16 + 4])
                    return last
                return f
            zguards = [zsil_tok[(b * 4 + sq) % 8] for sq in range(4)] + \
                      [ba_tok[(b * 4 + sq) % 8] for sq in range(4)]
            tmx = step(VEC, sv, [tip, conv_tok[par]] + zguards, mk_evict(b, par, tip))

            tcv = step(TEN, st, [tmx, act_tok[par], chunk_ten_end[0], chunk_ten_end[1],
                                 (sv, sv.n), (ss, ss.n)],
                       lambda e, par=par: (
                [e.matmul(P[0][0:64, :], cd_q[:, tap * DK:(tap + 1) * DK],
                          mix_qk[:, tap:tap + TB],
                          start=(tap == 0), stop=(tap == KS - 1)) for tap in range(KS)],
                [e.matmul(P[6][0:64, 0:512], cd_k[:, tap * DK:(tap + 1) * DK],
                          mix_qk[:, tap:tap + TB],
                          start=(tap == 0), stop=(tap == KS - 1)) for tap in range(KS)],
                [e.matmul(P[1][:], cd_v0[:, tap * 128:(tap + 1) * 128], mix_v0[:, tap:tap + TB],
                          start=(tap == 0), stop=(tap == KS - 1)) for tap in range(KS)],
                [e.matmul(P[2][:], cd_v1[:, tap * 128:(tap + 1) * 128], mix_v1[:, tap:tap + TB],
                          start=(tap == 0), stop=(tap == KS - 1)) for tap in range(KS)])[-1][-1])
            conv_tok[par] = tcv

            tac = step(SCA, ss, [tcv], lambda e, par=par: (
                e.activation(out=act_q[par][:], in_=P[0][0:64, :], func=AF.Silu),
                e.activation(out=act_k[par][:], in_=P[6][0:64, 0:512], func=AF.Silu),
                e.activation(out=act_v0[par][:], in_=P[1][:], func=AF.Silu),
                e.activation(out=act_v1[par][:], in_=P[2][:], func=AF.Silu))[-1])
            act_tok[par] = tac

            for cl in range(4):
                if SCAN_LVL == 0:
                    break
                cc = b * 4 + cl
                cp = cc % 2
                ccol = slice(cl * 128, (cl + 1) * 128)
                zr = z_ring[cc % 8]
                bar = ba_ring[cc % 8]
                guard = chunk_ten_end[cp]

                t_tr = step(TEN, st, [tac, guard, (sv, sv.n), (ss, ss.n)],
                            lambda e, par=par, ccol=ccol: (
                    e.transpose(PT[:, 0:64], act_q[par][:, ccol], idb[0:64, 0:64]),
                    e.transpose(PT[:, 64:128], act_k[par][:, ccol], idb[0:64, 0:64]),
                    e.transpose(PT[:, 128:256], act_v0[par][:, ccol], idb[:]),
                    e.transpose(PT[:, 256:384], act_v1[par][:, ccol], idb[:]))[-1])
                t_trc = step(VEC, sv, [t_tr], lambda e, cp=cp: (
                    e.tensor_copy(out=kv_t[cp][:], in_=PT[:, 0:128]),
                    e.tensor_copy(out=v_t[cp][:], in_=PT[:, 128:384]))[-1])

                t_sq = step(SCA, ss, [t_trc], lambda e, cp=cp:
                            e.activation(out=sqw[cp][:], in_=kv_t[cp][:], func=AF.Square))
                t_ssq = step(VEC, sv, [t_sq], lambda e, cp=cp: (
                    e.reduce_sum(out=colw[cp][:, 0:1], in_=sqw[cp][:, 0:64], axis=AX.X),
                    e.reduce_sum(out=colw[cp][:, 1:2], in_=sqw[cp][:, 64:128], axis=AX.X))[-1])
                t_nr = step(SCA, ss, [t_ssq], lambda e, cp=cp:
                            e.activation(out=colw[cp][:, 2:4], in_=colw[cp][:, 0:2], func=AF.Sqrt))
                t_rna = step(VEC, sv, [t_nr], lambda e, cp=cp:
                             e.tensor_scalar(out=colw[cp][:, 2:4], in0=colw[cp][:, 2:4],
                                             scalar1=1e-12, scalar2=None, op0=ALU.max))
                t_rn = step(VEC, sv, [t_rna], lambda e, cp=cp:
                            e.reciprocal(out=colw[cp][:, 4:6], in_=colw[cp][:, 2:4]))
                t_lnrk = step(SCA, ss, [t_rn], lambda e, cp=cp:
                              e.activation(out=colw[cp][:, 6:7], in_=colw[cp][:, 5:6], func=AF.Ln))

                if SCAN_LVL < 11:
                    chunk_ten_end[cp] = (st, st.n)
                    continue
                t_apd = step(VEC, sv, [tmx, tok_smc2, guard], lambda e, cp=cp, bar=bar:
                             e.tensor_tensor(out=colw[cp][:, 8:10], in0=bar[:, 2:4],
                                             in1=smc[:, 0:2], op=ALU.add))
                t_abs = step(SCA, ss, [t_apd], lambda e, cp=cp, bar=bar: (
                    e.activation(out=colw[cp][:, 10:12], in_=colw[cp][:, 8:10], func=AF.Abs),
                    e.activation(out=colw[cp][:, 12:14], in_=bar[:, 0:2], func=AF.Abs))[-1])
                t_ex = step(SCA, ss, [t_abs], lambda e, cp=cp: (
                    e.activation(out=colw[cp][:, 10:12], in_=colw[cp][:, 10:12],
                                 func=AF.Exp, scale=-1.0),
                    e.activation(out=colw[cp][:, 12:14], in_=colw[cp][:, 12:14],
                                 func=AF.Exp, scale=-1.0))[-1])
                t_l1p = step(SCA, ss, [t_ex], lambda e, cp=cp: (
                    e.activation(out=colw[cp][:, 10:12], in_=colw[cp][:, 10:12],
                                 func=AF.Ln, bias=one_col[:]),
                    e.activation(out=colw[cp][:, 12:14], in_=colw[cp][:, 12:14],
                                 func=AF.Ln, bias=one_col[:]))[-1])
                t_bet = step(SCA, ss, [t_l1p], lambda e, cp=cp, bar=bar:
                             e.activation(out=colw[cp][:, 14:16], in_=bar[:, 0:2], func=AF.Sigmoid))
                ba_tok[cc % 8] = t_bet
                t_ga = step(VEC, sv, [t_l1p, t_bet], lambda e, cp=cp, bar=bar: (
                    e.tensor_scalar(out=cums[cp][:, 4:6], in0=colw[cp][:, 8:10],
                                    scalar1=0.0, scalar2=None, op0=ALU.max),
                    e.tensor_scalar(out=cums[cp][:, 6:8], in0=bar[:, 0:2],
                                    scalar1=0.0, scalar2=None, op0=ALU.min))[-1])
                t_gb = step(VEC, sv, [t_ga], lambda e, cp=cp: (
                    e.tensor_tensor(out=cums[cp][:, 4:6], in0=cums[cp][:, 4:6],
                                    in1=colw[cp][:, 10:12], op=ALU.add),
                    e.tensor_tensor(out=cums[cp][:, 6:8], in0=cums[cp][:, 6:8],
                                    in1=colw[cp][:, 12:14], op=ALU.subtract))[-1])
                t_g = step(VEC, sv, [t_gb], lambda e, cp=cp:
                           e.tensor_tensor(out=cums[cp][:, 4:6], in0=cums[cp][:, 4:6],
                                           in1=smc[:, 2:4], op=ALU.mult))

                t_gh = step(VEC, sv, [t_g], lambda e, cp=cp:
                            e.tensor_copy(out=gsp[cp][:, 0:2], in_=cums[cp][:, 4:6]))
                t_gl = step(VEC, sv, [t_gh], lambda e, cp=cp:
                            e.tensor_tensor(out=gsp[cp][:, 8:10], in0=cums[cp][:, 4:6],
                                            in1=gsp[cp][:, 0:2], op=ALU.subtract))

                if SCAN_LVL < 12:
                    chunk_ten_end[cp] = (st, st.n)
                    continue
                t_cm = step(TEN, st, [t_gl], lambda e, cp=cp: (
                    e.matmul(P[5][:, 64:72], trb[:], gsp[cp][:, 0:8], start=True, stop=False),
                    e.matmul(P[5][:, 64:72], trb[:], gsp[cp][:, 8:16], start=False, stop=True)
                )[-1])
                t_cuma = step(VEC, sv, [t_cm, t_lnrk], lambda e, cp=cp:
                              e.tensor_copy(out=cums[cp][:, 0:2], in_=P[5][:, 64:66]))
                t_cum = step(VEC, sv, [t_cuma], lambda e, cp=cp: (
                    e.tensor_copy(out=colpA[cp][:, 0:1], in_=cums[cp][:, 0:1]),
                    e.tensor_copy(out=colpA[cp][:, 32:33], in_=cums[cp][:, 1:2]),
                    e.tensor_tensor(out=colpA[cp][:, 64:65], in0=colw[cp][:, 6:7],
                                    in1=cums[cp][:, 0:1], op=ALU.subtract),
                    e.tensor_tensor(out=colpB[cp][:, 0:1], in0=colw[cp][:, 6:7],
                                    in1=cums[cp][:, 1:2], op=ALU.subtract),
                    e.tensor_tensor(out=colpB[cp][:, 32:33], in0=cums[cp][:, 0:1],
                                    in1=cums[cp][:, 6:7], op=ALU.add),
                    e.tensor_tensor(out=colpB[cp][:, 64:65], in0=cums[cp][:, 1:2],
                                    in1=cums[cp][:, 7:8], op=ALU.add))[-1])
                t_cum2 = step(VEC, sv, [t_cum], lambda e, cp=cp: (
                    e.tensor_tensor(out=colpB[cp][:, 32:33], in0=colpB[cp][:, 32:33],
                                    in1=colw[cp][:, 6:7], op=ALU.add),
                    e.tensor_tensor(out=colpB[cp][:, 64:65], in0=colpB[cp][:, 64:65],
                                    in1=colw[cp][:, 6:7], op=ALU.add))[-1])

                if SCAN_LVL < 13:
                    chunk_ten_end[cp] = (st, st.n)
                    continue
                t_rpt = step(TEN, st, [t_cum2], lambda e, cp=cp: (
                    e.transpose(P[6][0:96, 0:128].bitcast(F32), colpA[cp][:], idf[:]),
                    e.transpose(P[6][0:96, 128:256].bitcast(F32), colpB[cp][:], idf[:]))[-1])
                t_rows = step(VEC, sv, [t_rpt], lambda e, cp=cp: (
                    e.tensor_copy(out=rows6[cp][0:1, 0:128], in_=P[6][0:1, 0:128]),
                    e.tensor_copy(out=rows6[cp][0:1, 128:256], in_=P[6][32:33, 0:128]),
                    e.tensor_copy(out=rows6[cp][0:1, 256:384], in_=P[6][64:65, 0:128]),
                    e.tensor_copy(out=rows6[cp][0:1, 384:512], in_=P[6][0:1, 128:256]),
                    e.tensor_copy(out=rows6[cp][0:1, 512:640], in_=P[6][32:33, 128:256]),
                    e.tensor_copy(out=rows6[cp][0:1, 640:768], in_=P[6][64:65, 128:256]))[-1])

                def rowv_(cp, name, v):
                    k6 = {("cum", 0): 0, ("cum", 1): 1, ("c2", 0): 2,
                          ("c2", 1): 3, ("c1", 0): 4, ("c1", 1): 5}[(name, v)]
                    return (rows6[cp][0:1, k6 * 128:(k6 + 1) * 128], 0)

                def bc_row(e, pslice, name, v, cp, start=True, stop=True):
                    r_, b_ = rowv_(cp, name, v)
                    return e.matmul(pslice, ones_t[b_:b_ + 1, :], r_, start=start, stop=stop)

                def bc_col(e, pslice, name, v, cp, start=True, stop=True):
                    r_, b_ = rowv_(cp, name, v)
                    return e.matmul(pslice, r_, ones_t[b_:b_ + 1, :], start=start, stop=stop)

                if SCAN_LVL < 14:
                    chunk_ten_end[cp] = (st, st.n)
                    continue
                t_cC = step(TEN, st, [t_rows], lambda e, cp=cp: (
                    bc_row(e, P[5][:, 128:256], "cum", 0, cp),
                    bc_row(e, P[5][:, 256:384], "cum", 1, cp))[-1])
                t_lamc = step(SCA, ss, [t_cC], lambda e, cp=cp: (
                    e.activation(out=lamc[cp][:, 0:1], in_=P[5][:, 255:256], func=AF.Exp),
                    e.activation(out=lamc[cp][:, 1:2], in_=P[5][:, 383:384], func=AF.Exp))[-1])
                t_cCc = step(VEC, sv, [t_cC], lambda e, cp=cp: (
                    e.tensor_copy(out=lamc[cp][:, 2:3], in_=P[5][:, 255:256]),
                    e.tensor_copy(out=lamc[cp][:, 3:4], in_=P[5][:, 383:384]))[-1])

                if SCAN_LVL < 15:
                    chunk_ten_end[cp] = (st, st.n)
                    continue
                t_kk = step(TEN, st, [tac, guard], lambda e, par=par, ccol=ccol: (
                    e.matmul(P[3][:, 0:128], act_k[par][:, ccol],
                             act_k[par][:, ccol], start=True, stop=True),
                    e.matmul(P[3][:, 128:256], act_k[par][:, ccol],
                             act_q[par][:, ccol], start=True, stop=True))[-1])
                t_qkb = step(VEC, sv, [t_kk], lambda e, cp=cp:
                             e.tensor_copy(out=qkb[cp][:], in_=P[3][:, 128:256]))

                t_lam = step(SCA, ss, [t_cum], lambda e, cp=cp:
                             e.activation(out=colw[cp][:, 16:18], in_=cums[cp][:, 0:2],
                                          func=AF.Exp))

                if SCAN_LVL < 16:
                    chunk_ten_end[cp] = (st, st.n)
                    continue
                per_v_sb = []
                o_cons_prev = None
                sn_prev = None
                ms_prev = None
                for v in range(VH):
                    t_bA = step(TEN, st, [t_rows, o_cons_prev], lambda e, cp=cp, v=v: (
                        bc_row(e, P[6][:, 128:256], "c2", v, cp),
                        bc_col(e, P[6][:, 256:384], "c1", v, cp))[-1])
                    t_dAa = step(VEC, sv, [t_bA], lambda e, cp=cp:
                        e.tensor_tensor(out=eT[cp][0][:], in0=P[6][:, 256:384], in1=mA[:],
                                        op=ALU.add))
                    t_dA = step(VEC, sv, [t_dAa], lambda e, cp=cp:
                        e.tensor_tensor(out=eT[cp][0][:], in0=P[6][:, 128:256],
                                        in1=eT[cp][0][:], op=ALU.add))
                    t_xA = step(SCA, ss, [t_dA], lambda e, cp=cp:
                                e.activation(out=eT[cp][0][:], in_=eT[cp][0][:], func=AF.Exp))
                    t_A = step(VEC, sv, [t_xA, t_kk], lambda e, cp=cp, v=v:
                               e.tensor_tensor(out=Amat[cp][v][:], in0=eT[cp][0][:],
                                               in1=P[3][:, 0:128], op=ALU.mult))

                    t_bR = step(TEN, st, [t_rows, t_dA], lambda e, cp=cp, v=v: (
                        bc_row(e, P[6][:, 128:256], "c1", v, cp),
                        bc_col(e, P[6][:, 256:384], "c2", v, cp))[-1])
                    t_dRa = step(VEC, sv, [t_bR], lambda e, cp=cp:
                        e.tensor_tensor(out=eT[cp][1][:], in0=P[6][:, 256:384], in1=mR[:],
                                        op=ALU.add))
                    t_dR = step(VEC, sv, [t_dRa], lambda e, cp=cp:
                        e.tensor_tensor(out=eT[cp][1][:], in0=P[6][:, 128:256],
                                        in1=eT[cp][1][:], op=ALU.add))
                    t_xR = step(SCA, ss, [t_dR], lambda e, cp=cp:
                                e.activation(out=eT[cp][1][:], in_=eT[cp][1][:], func=AF.Exp))
                    t_R = step(VEC, sv, [t_xR, t_kk], lambda e, cp=cp, v=v:
                               e.tensor_tensor(out=Rmat[cp][v][:], in0=eT[cp][1][:],
                                               in1=P[3][:, 0:128], op=ALU.mult))

                    t_bF = step(TEN, st, [t_rows, t_dR], lambda e, cp=cp, v=v: (
                        bc_row(e, P[6][:, 128:256], "cum", v, cp),
                        bc_col(e, P[6][:, 256:384], "c2", v, cp))[-1])
                    t_dFa = step(VEC, sv, [t_bF], lambda e, cp=cp:
                        e.tensor_tensor(out=eT[cp][2][:], in0=P[6][:, 256:384], in1=mFt[:],
                                        op=ALU.add))
                    t_dF = step(VEC, sv, [t_dFa], lambda e, cp=cp:
                        e.tensor_tensor(out=eT[cp][2][:], in0=P[6][:, 128:256],
                                        in1=eT[cp][2][:], op=ALU.add))
                    t_xF = step(SCA, ss, [t_dF], lambda e, cp=cp:
                                e.activation(out=eT[cp][2][:], in_=eT[cp][2][:], func=AF.Exp))
                    t_F = step(VEC, sv, [t_xF, t_qkb], lambda e, cp=cp, v=v:
                               e.tensor_tensor(out=Ftm[cp][v][:], in0=eT[cp][2][:],
                                               in1=qkb[cp][:], op=ALU.mult))

                    if SCAN_LVL < 17:
                        continue
                    t_rhsa = step(VEC, sv, [t_lam, t_bet, t_rn, t_trc, t_cCc, guard],
                                  lambda e, cp=cp, v=v: (
                        e.tensor_tensor(out=colw[cp][:, 18:19], in0=colw[cp][:, 14 + v:15 + v],
                                        in1=colw[cp][:, 16 + v:17 + v], op=ALU.mult),
                        e.tensor_scalar(out=rhs0[cp][v][:, DK:DK + DV],
                                        in0=v_t[cp][:, v * DV:(v + 1) * DV],
                                        scalar1=colw[cp][:, 14 + v:15 + v],
                                        scalar2=None, op0=ALU.mult),
                        e.tensor_tensor(out=colw[cp][:, 19:20], in0=lamc[cp][:, 2 + v:3 + v],
                                        in1=cums[cp][:, v:v + 1], op=ALU.subtract))[-1])
                    t_rhsb = step(VEC, sv, [t_rhsa], lambda e, cp=cp, v=v:
                        e.tensor_tensor(out=colw[cp][:, 18:19], in0=colw[cp][:, 18:19],
                                        in1=colw[cp][:, 5:6], op=ALU.mult))
                    t_rhs = step(VEC, sv, [t_rhsb], lambda e, cp=cp, v=v:
                        e.tensor_scalar(out=rhs0[cp][v][:, 0:DK], in0=kv_t[cp][:, 64:128],
                                        scalar1=colw[cp][:, 18:19], scalar2=None, op0=ALU.mult))
                    t_kbe = step(SCA, ss, [t_rhs], lambda e, cp=cp:
                                 e.activation(out=colw[cp][:, 20:21], in_=colw[cp][:, 19:20],
                                              func=AF.Exp))
                    t_kba = step(VEC, sv, [t_kbe], lambda e, cp=cp, v=v:
                        e.tensor_tensor(out=colw[cp][:, 21:22], in0=colw[cp][:, 20:21],
                                        in1=colw[cp][:, 5:6], op=ALU.mult))
                    t_kb = step(VEC, sv, [t_kba], lambda e, cp=cp, v=v:
                        e.tensor_scalar(out=kbar[cp][v][:], in0=kv_t[cp][:, 64:128],
                                        scalar1=colw[cp][:, 21:22], scalar2=None,
                                        op0=ALU.mult))

                    if SCAN_LVL < 18:
                        continue
                    xt_cur, tok_x = rhs0[cp][v], t_rhs
                    a_cur, r_cur = Amat[cp][v], Rmat[cp][v]
                    tok_a, tok_r = t_A, t_R
                    xi = 0
                    ai = 0
                    p = 1
                    sign = -1.0
                    while p < C:
                        t_mx = step(TEN, st, [tok_r, tok_x], lambda e, r_cur=r_cur, xt_cur=xt_cur:
                                    e.matmul(P[0][:, 0:192], r_cur[:], xt_cur[:],
                                             start=True, stop=True))
                        xt_new = Xs[cp][xi]
                        tok_x = step(VEC, sv, [t_mx], lambda e, xt_cur=xt_cur, xt_new=xt_new,
                                     sign=sign:
                                     e.scalar_tensor_tensor(out=xt_new[:], in0=P[0][:, 0:192],
                                                            scalar=sign, in1=xt_cur[:],
                                                            op0=ALU.mult, op1=ALU.add))
                        xt_cur = xt_new
                        xi ^= 1
                        sign = 1.0
                        if p * 2 < C:
                            t_sq2 = step(TEN, st, [tok_a, tok_r, t_mx],
                                         lambda e, a_cur=a_cur, r_cur=r_cur: (
                                e.matmul(P[1][:, 0:128], r_cur[:], a_cur[:],
                                         start=True, stop=True),
                                e.matmul(P[2][:, 0:128], a_cur[:], r_cur[:],
                                         start=True, stop=True))[-1])
                            a_new, r_new = Apow[cp][ai], Rpow[cp][ai]
                            tok_a = step(VEC, sv, [t_sq2], lambda e, a_new=a_new:
                                         e.tensor_copy(out=a_new[:], in_=P[1][:, 0:128]))
                            tok_r = step(VEC, sv, [t_sq2], lambda e, r_new=r_new:
                                         e.tensor_copy(out=r_new[:], in_=P[2][:, 0:128]))
                            a_cur, r_cur = a_new, r_new
                            ai ^= 1
                        p *= 2
                    Xfin, t_solve = xt_cur, tok_x

                    if SCAN_LVL < 19:
                        continue
                    t_wtt = step(TEN, st, [t_solve], lambda e, Xfin=Xfin:
                                 e.transpose(PT[0:64, 512:640], Xfin[:, 0:DK], idb[:]))
                    t_wtc = step(VEC, sv, [t_wtt], lambda e, cp=cp:
                                 e.tensor_copy(out=wtT[cp][:], in_=PT[0:64, 512:640]))
                    t_wst = step(TEN, st, [t_wtc, chunk_sb], lambda e, cp=cp, v=v:
                                 e.matmul(P[4][:, 0:128], wtT[cp][:], S_b[v][:],
                                          start=True, stop=True))
                    t_u = step(VEC, sv, [t_wst], lambda e, cp=cp, v=v, Xfin=Xfin:
                               e.scalar_tensor_tensor(out=uu[cp][v][:], in0=P[4][:, 0:128],
                                                      scalar=-1.0, in1=Xfin[:, DK:DK + DV],
                                                      op0=ALU.mult, op1=ALU.add))

                    t_omm = step(TEN, st, [t_u, t_F, chunk_sb, tac, o_cons_prev],
                                 lambda e, cp=cp, v=v, par=par, ccol=ccol: (
                        e.matmul(P[6][:, 0:128], act_q[par][:, ccol], S_b[v][:],
                                 start=True, stop=True),
                        e.matmul(P[6][:, 128:256], Ftm[cp][v][:], uu[cp][v][:],
                                 start=True, stop=True))[-1])

                    t_smm = step(TEN, st, [t_u, t_kb, t_omm, sn_prev], lambda e, cp=cp, v=v:
                                 e.matmul(P[4][0:64, 128:256], kbar[cp][v][:], uu[cp][v][:],
                                          start=True, stop=True))

                    Sf_old, Sf_new = S_f[v][cc % 2], S_f[v][(cc + 1) % 2]
                    t_sn = step(VEC, sv, [t_smm, t_lamc], lambda e, cp=cp, v=v,
                                Sf_old=Sf_old, Sf_new=Sf_new:
                                e.scalar_tensor_tensor(out=Sf_new[:], in0=Sf_old[:],
                                                       scalar=lamc[cp][0:64, v:v + 1],
                                                       in1=P[4][0:64, 128:256],
                                                       op0=ALU.mult, op1=ALU.add))
                    t_sb2 = step(VEC, sv, [t_sn, t_omm, t_wst], lambda e, v=v, Sf_new=Sf_new:
                                 e.tensor_copy(out=S_b[v][:], in_=Sf_new[:]))
                    per_v_sb.append(t_sb2)
                    sn_prev = t_sn

                    t_oa = step(VEC, sv, [t_omm, t_lam, t_rn], lambda e, cp=cp, v=v:
                        e.tensor_scalar(out=o_w[cp][:, v * DV:(v + 1) * DV],
                                        in0=P[6][:, 0:128],
                                        scalar1=colw[cp][:, 16 + v:17 + v],
                                        scalar2=None, op0=ALU.mult))
                    t_ob = step(VEC, sv, [t_oa], lambda e, cp=cp, v=v:
                        e.tensor_tensor(out=o_w[cp][:, v * DV:(v + 1) * DV],
                                        in0=P[6][:, 128:256],
                                        in1=o_w[cp][:, v * DV:(v + 1) * DV], op=ALU.add))
                    t_o = step(VEC, sv, [t_ob], lambda e, cp=cp, v=v:
                        e.tensor_scalar(out=o_w[cp][:, v * DV:(v + 1) * DV],
                                        in0=o_w[cp][:, v * DV:(v + 1) * DV],
                                        scalar1=colw[cp][:, 4:5], scalar2=SCALE,
                                        op0=ALU.mult, op1=ALU.mult))
                    o_cons_prev = t_o

                    t_osq = step(SCA, ss, [t_o, ms_prev], lambda e, cp=cp, v=v:
                                 e.activation(out=sqw[cp][:], in_=o_w[cp][:, v * DV:(v + 1) * DV],
                                              func=AF.Square))
                    t_ms2a = step(VEC, sv, [t_osq], lambda e, cp=cp:
                        e.reduce_sum(out=colw[cp][:, 22:23], in_=sqw[cp][:, 0:DV], axis=AX.X))
                    t_ms2 = step(VEC, sv, [t_ms2a], lambda e, cp=cp:
                        e.tensor_scalar(out=colw[cp][:, 22:23], in0=colw[cp][:, 22:23],
                                        scalar1=1.0 / DV, scalar2=1e-6,
                                        op0=ALU.mult, op1=ALU.add))
                    ms_prev = t_ms2
                    t_rmq = step(SCA, ss, [t_ms2], lambda e, cp=cp:
                                 e.activation(out=colw[cp][:, 23:24], in_=colw[cp][:, 22:23],
                                              func=AF.Sqrt))
                    t_rr = step(VEC, sv, [t_rmq], lambda e, cp=cp:
                                e.reciprocal(out=colw[cp][:, 22:23], in_=colw[cp][:, 23:24]))
                    t_sz = step(SCA, ss, [tmx, t_osq], lambda e, cp=cp, v=v, zr=zr:
                                e.activation(out=zs_t[cp][:, v * DV:(v + 1) * DV],
                                             in_=zr[:, v * DV:(v + 1) * DV], func=AF.Silu))
                    zsil_tok[cc % 8] = t_sz
                    t_gta = step(VEC, sv, [t_rr, t_sz, tok_nw], lambda e, cp=cp, v=v:
                        e.tensor_scalar(out=o_w[cp][:, v * DV:(v + 1) * DV],
                                        in0=o_w[cp][:, v * DV:(v + 1) * DV],
                                        scalar1=colw[cp][:, 22:23], scalar2=None, op0=ALU.mult))
                    t_gtb = step(VEC, sv, [t_gta], lambda e, cp=cp, v=v:
                        e.tensor_tensor(out=o_w[cp][:, v * DV:(v + 1) * DV],
                                        in0=o_w[cp][:, v * DV:(v + 1) * DV],
                                        in1=nw_t[:], op=ALU.mult))
                    t_gt = step(VEC, sv, [t_gtb], lambda e, cp=cp, v=v:
                        e.tensor_tensor(out=gat[cp][:, v * DV:(v + 1) * DV],
                                        in0=o_w[cp][:, v * DV:(v + 1) * DV],
                                        in1=zs_t[cp][:, v * DV:(v + 1) * DV],
                                        op=ALU.mult))

                    t_gtt = step(TEN, st, [t_gt, gs_dma[-1] if (cc % 16 == 0 and len(gs_dma))
                                           else None],
                                 lambda e, cp=cp, v=v:
                                 e.transpose(PT[:, 640:768], gat[cp][:, v * DV:(v + 1) * DV],
                                             idb[:]))
                    t_gsc = step(VEC, sv, [t_gtt], lambda e, cp=cp, v=v, cc=cc:
                                 e.tensor_copy(out=gstrip[v][:, (cc % 16) * 128:(cc % 16 + 1) * 128],
                                               in_=PT[:, 640:768]))
                    gs_cp[v] = t_gsc

                if per_v_sb:
                    chunk_sb = per_v_sb[-1]
                chunk_ten_end[cp] = (st, st.n)

                if stage == "B1" and cc == 0 and TAP == "canary":
                    tcan = step(VEC, sv, [(st, st.n), (ss, ss.n)], lambda e:
                                e.memset(eT[0][0][:], 3.25))
                    tap_tok = step(SYNC, sd, [tcan], lambda e:
                                   e.dma_start(out=y_seg[0:128, 0:128],
                                               in_=eT[0][0][:, 0:128]), k=16)
                    tap_tok2 = step(GPS, sg, [tcan], lambda e:
                                    e.dma_start(out=y_seg[128:256, 0:128],
                                                in_=eT[0][0][:, 0:128]), k=16)
                    chunk_ten_end[cp] = tap_tok
                    chunk_sb = tap_tok
                elif stage == "B1" and cc == 0 and TAP:
                    def tapf(e, cp=cp):
                        tiles = []
                        if TAP == "act":
                            tiles = [(act_q[0][:], 0), (act_k[0][:], 64),
                                     (act_v0[0][:, 0:512], 128), (act_v1[0][:, 0:512], 192)]
                        elif TAP == "vecs":
                            tiles = [(cums[cp][:], 0), (colw[cp][:], 8), (lamc[cp][:], 40),
                                     (kv_t[cp][:].bitcast(BF16), 48),
                                     (gsp[cp][:].bitcast(BF16), 192)]
                        elif TAP == "AR":
                            tiles = [(Amat[cp][0][:], 0), (Rmat[cp][0][:], 64),
                                     (Ftm[cp][0][:], 128), (qkb[cp][:], 192)]
                        elif TAP == "XU":
                            tiles = [(rhs0[cp][0][:], 0), (uu[cp][0][:], 96),
                                     (kbar[cp][0][:], 160), (S_b[0][:], 192)]
                        elif TAP == "cd":
                            tiles = [(cd_q[:], 0), (cd_k[:], 128)]
                        elif TAP == "mask":
                            tiles = [(trb[:], 0), (idb[:], 128)]
                        elif TAP == "ws":
                            tiles = [(wstage[0][:, 0:1024].bitcast(BF16), 0)]
                        elif TAP == "wq":
                            tiles = [(wqkvT[0][:], 0), (wzT[0][:], 128)]
                        elif TAP == "mix":
                            tiles = [(mix_qk[:, 0:512], 0)]
                        elif TAP == "O":
                            tiles = [(o_w[cp][:].bitcast(BF16), 0), (gat[cp][:], 64),
                                     (zs_t[cp][:], 128)]
                        last = None
                        for ap_, r0 in tiles:
                            pr, fr = ap_.shape[0], ap_.shape[1]
                            dst = dbg[r0:r0 + min(pr, 256 - r0), 0:fr]
                            if str(ap_.dtype) != "dt.bfloat16":
                                ap_ = ap_.bitcast(BF16)
                                dst = dbg[r0:r0 + min(pr, 256 - r0), 0:ap_.shape[1]]
                            last = e.dma_start(out=dst, in_=ap_).then_inc(sg.h, 16)
                        return None
                    ntap = {"act": 4, "vecs": 5, "AR": 4, "XU": 4, "O": 3,
                            "cd": 2, "mix": 1, "ws": 1, "wq": 2, "mask": 2}[TAP]
                    tap_tok = step(GPS, sg, [(sv, sv.n), (ss, ss.n), (st, st.n)],
                                   tapf, k=16 * ntap)
                    chunk_ten_end[cp] = tap_tok
                    chunk_sb = tap_tok

                if (cc % 16 == 15 or cc == NBLK * 4 - 1) and SCAN_LVL >= 19:
                    c0 = (cc // 16) * 16 * 128
                    wgt = (cc % 16 + 1) * 128
                    tgd = step(GPS, sg, [gs_cp[0], gs_cp[1]],
                               lambda e, c0=c0, wgt=wgt: (
                        e.dma_start(out=cc_gt_in[0:128, c0:c0 + wgt],
                                    in_=gstrip[0][:, 0:wgt]).then_inc(sg.h, 16),
                        e.dma_start(out=cc_gt_in[128:256, c0:c0 + wgt],
                                    in_=gstrip[1][:, 0:wgt]).then_inc(sg.h, 16),
                        None)[-1], k=32)
                    gs_dma.append(tgd)

        if stage in ("B1", "B"):
            step(SYNC, sd, [tok_ms], lambda e:
                 e.dma_start(out=y_seg[1920:2048, 0:128], in_=ones_t[:]), k=16)
            step(SYNC, sd, [tok_ag12], lambda e:
                 e.dma_start(out=y_seg[1920:2048, 128:256], in_=ones_t[:]), k=16)
            step(SYNC, sd, [ip_tok[0] or tok_ms], lambda e:
                 e.dma_start(out=y_seg[1920:2048, 256:384], in_=ones_t[:]), k=16)
            step(SYNC, sd, [act_tok[0] or tok_ms], lambda e:
                 e.dma_start(out=y_seg[1920:2048, 384:512], in_=ones_t[:]), k=16)
            step(SYNC, sd, [chunk_ten_end[0]], lambda e:
                 e.dma_start(out=y_seg[1920:2048, 512:640], in_=ones_t[:]), k=16)
            step(SYNC, sd, [(sv, sv.n)], lambda e:
                 e.dma_start(out=y_seg[1920:2048, 640:768], in_=ones_t[:]), k=16)
            step(SYNC, sd, [(ss, ss.n)], lambda e:
                 e.dma_start(out=y_seg[1920:2048, 768:896], in_=ones_t[:]), k=16)
            step(SYNC, sd, [(st, st.n)], lambda e:
                 e.dma_start(out=y_seg[1920:2048, 896:1024], in_=ones_t[:]), k=16)
            def dump(e):
                cpd = 1
                items = [
                    (cums[cpd][:], 256, 0, F32),
                    (colw[cpd][:], 256, 16, F32),
                    (lamc[cpd][:], 256, 64, F32),
                    (kv_t[cpd][:], 256, 100, BF16),
                    (v_t[cpd][:], 256, 200, BF16),
                    (Amat[cpd][0][:], 384, 0, BF16),
                    (Rmat[cpd][0][:], 384, 100, BF16),
                    (Ftm[cpd][0][:], 384, 200, BF16),
                    (rhs0[cpd][0][:], 512, 0, BF16),
                    (uu[cpd][0][:], 512, 200, BF16),
                    (kbar[cpd][0][:], 512, 400, BF16),
                    (o_w[cpd][:], 640, 0, F32),
                    (gat[cpd][:], 640, 300, BF16),
                    (S_b[0][:], 768, 0, BF16),
                    (gsp[cpd][:], 768, 200, BF16),
                    (ba_ring[3][:], 768, 300, F32),
                    (z_ring[3][:], 896, 0, BF16),
                    (smc[:], 896, 200, F32),
                    (dtb_t[:, 0:2] if False else smc[:, 0:2], 896, 220, F32),
                ]
                last = None
                for ap_, r0, c0, dt_ in items:
                    pr, fr = ap_.shape[0], ap_.shape[1]
                    if dt_ == F32:
                        dst = y_seg[r0:r0 + pr, c0:c0 + fr]
                    else:
                        dst = y_seg[r0:r0 + pr, c0:c0 + (fr + 1) // 2].bitcast(BF16)[:, 0:fr]
                    last = e.dma_start(out=dst, in_=ap_).then_inc(sd.h, 16)
                return None
            tre = step(VEC, sv, [(st, st.n), (ss, ss.n), (sg, sg.n)], lambda e:
                       e.tensor_tensor(out=colw[1][:, 8:10], in0=ba_ring[3][:, 2:4],
                                       in1=smc[:, 0:2], op=ALU.add))
            step(SYNC, sd, [(sv, sv.n), (ss, ss.n), (st, st.n), (sg, sg.n), tre],
                 dump, k=16 * 19)
            lastg = gs_dma[-1] if gs_dma else None
            tdbg = step(SYNC, sd, [lastg, gs_cp[0], gs_cp[1],
                                   (sv, sv.n), (ss, ss.n), (st, st.n)], lambda e:
                        e.dma_start(out=y_seg[0:256, 0:1024].bitcast(BF16)[:, 0:2048],
                                    in_=cc_gt_in[:, 0:2048]), k=16)
            GPS.append(lambda e: e.wait_ge(sg.h, sg.n))
            _finish(block, SYNC, VEC, SCA, TEN, GPS)
            return nc

        tok_ag3 = step(GPS, scc3, [gs_dma[-1]], lambda e:
                       e.collective_compute("AllGather", ALU.bypass,
                                            replica_groups=[list(range(NCORES))],
                                            ins=[cc_gt_in[:].opt()], outs=[cc_gt_out[:].opt()]))

        es_b.close()
        gt_t = [sb(f"gt_t{k}", [128, 128], BF16) for k in range(16)]
        xc_t = sb("xc_t", [128, D])
        at_t = sb("at_t", [128, D])
        x2_t = sb("x2_t", [128, D])
        f_t = sb("f_t", [128, D])
        d1qT = sb("d1qT", [128, D], FP8)
        wout_r = [sb(f"wout_r{k}", [128, D], BF16) for k in range(16)]
        q2_r = [sb(f"q2_r{k}", [128, D], FP8) for k in range(16)]

        tw = None
        for kt in range(16):
            tw = step(SYNC, sd, [tok_ag3, ip_tok[0], ip_tok[1]], lambda e, kt=kt:
                      e.dma_start(out=wout_r[kt][:],
                                  in_=w_outT[kt * 128:(kt + 1) * 128, :]), k=16)
        tok_wout = tw

        def q2v(kt, half):
            return q2_r[kt][:].bitcast(FP8)[:, half * 1024:(half + 1) * 1024]
        tq2 = step(SYNC, sd, [tok_ag3, tok_q2ag, ip_tok[0], ip_tok[1]], lambda e: (
            [e.dma_start(out=q2_r[kt][:],
                         in_=cc_q2_out[kt * 128:(kt + 1) * 128, :]).then_inc(sd.h, 16)
             for kt in range(16)],
            None)[-1], k=16 * 16)

        y_dma = []
        prev_ffn = None
        prev_fe = None
        prev_tgl = None
        for i in range(16):
            tof = step(VEC, sv, [tok_cl, prev_tgl], lambda e, i=i: (
                [e.tensor_scalar(out=gt_o[kt][:], in0=gto[:, kt:kt + 1],
                                 scalar1=i * 128, scalar2=None, op0=ALU.add)
                 for kt in range(16)])[-1])
            tgl = step(GPS, sg, [tof, tok_ag3, y_dma[i - 1] if i else None,
                                 tok_wout, tq2], lambda e: (
                [e.indirect_dma_start(out=gt_t[kt][:], out_offset=None,
                                      in_=cc_gt_out[0:128, 0:128],
                                      in_offset=IndirectOffsetOnAxis(ap=gt_o[kt][:], axis=1)
                                      ).then_inc(sg.h, 16)
                 for kt in range(16)],
                None)[-1], k=16 * 16)
            prev_tgl = tgl
            txl = step(SYNC, sd, [tgl], lambda e, i=i:
                       e.dma_start(out=xc_t[:], in_=x_seg[i * 128:(i + 1) * 128, :]), k=16)

            t_op = step(TEN, st, [tgl, tok_wout, prev_ffn, prev_fe,
                                  (sv, sv.n) if i == 0 else None,
                                  (ss, ss.n) if i == 0 else None], lambda e: (
                [[e.matmul(P[j][:], gt_t[kt][:], wout_r[kt][:, j * 512:(j + 1) * 512],
                           start=(kt == 0), stop=(kt == 15)) for kt in range(16)]
                 for j in range(4)])[-1][-1])
            t_ope = step(VEC, sv, [t_op], lambda e: (
                [e.tensor_copy(out=at_t[:, j * 512:(j + 1) * 512], in_=P[j][:])
                 for j in range(4)])[-1])

            t1a = step(VEC, sv, [t_ope], lambda e:
                       e.reduce_sum(out=acol[:, 0:1], in_=at_t[:], axis=AX.X))
            t1 = step(VEC, sv, [t1a], lambda e:
                      e.tensor_scalar(out=acol[:, 1:2], in0=acol[:, 0:1], scalar1=1.0 / D,
                                      scalar2=None, op0=ALU.mult))
            t2 = step(VEC, sv, [t1], lambda e:
                      e.tensor_scalar(out=at_t[:], in0=at_t[:], scalar1=acol[:, 1:2],
                                      scalar2=None, op0=ALU.subtract))
            t3 = step(SCA, ss, [t2], lambda e:
                      e.activation(out=sqtile[:], in_=at_t[:], func=AF.Square))
            t4a = step(VEC, sv, [t3], lambda e:
                       e.reduce_sum(out=acol[:, 2:3], in_=sqtile[:], axis=AX.X))
            t4 = step(VEC, sv, [t4a], lambda e:
                      e.tensor_scalar(out=acol[:, 3:4], in0=acol[:, 2:3], scalar1=1.0 / D,
                                      scalar2=1e-5, op0=ALU.mult, op1=ALU.add))
            t5 = step(SCA, ss, [t4], lambda e:
                      e.activation(out=acol[:, 4:5], in_=acol[:, 3:4], func=AF.Sqrt))
            t6 = step(VEC, sv, [t5], lambda e:
                      e.reciprocal(out=acol[:, 5:6], in_=acol[:, 4:5]))
            t7a = step(VEC, sv, [t6, txl], lambda e:
                       e.tensor_scalar(out=at_t[:], in0=at_t[:], scalar1=acol[:, 5:6],
                                       scalar2=None, op0=ALU.mult))
            t7 = step(VEC, sv, [t7a], lambda e:
                      e.tensor_tensor(out=x2_t[:], in0=xc_t[:], in1=at_t[:], op=ALU.add))

            t8a = step(VEC, sv, [t7], lambda e:
                       e.reduce_sum(out=acol[:, 0:1], in_=x2_t[:], axis=AX.X))
            t8 = step(VEC, sv, [t8a], lambda e:
                      e.tensor_scalar(out=acol[:, 1:2], in0=acol[:, 0:1], scalar1=1.0 / D,
                                      scalar2=None, op0=ALU.mult))
            t9 = step(VEC, sv, [t8], lambda e:
                      e.tensor_scalar(out=at_t[:], in0=x2_t[:], scalar1=acol[:, 1:2],
                                      scalar2=None, op0=ALU.subtract))
            t10 = step(SCA, ss, [t9], lambda e:
                       e.activation(out=sqtile[:], in_=at_t[:], func=AF.Square))
            t11a = step(VEC, sv, [t10], lambda e:
                        e.reduce_sum(out=acol[:, 2:3], in_=sqtile[:], axis=AX.X))
            t11 = step(VEC, sv, [t11a], lambda e:
                       e.tensor_scalar(out=acol[:, 3:4], in0=acol[:, 2:3], scalar1=1.0 / D,
                                       scalar2=1e-5, op0=ALU.mult, op1=ALU.add))
            t12 = step(SCA, ss, [t11], lambda e:
                       e.activation(out=acol[:, 4:5], in_=acol[:, 3:4], func=AF.Sqrt))
            t13 = step(VEC, sv, [t12], lambda e:
                       e.reciprocal(out=acol[:, 5:6], in_=acol[:, 4:5]))
            t14 = step(VEC, sv, [t13], lambda e:
                       e.tensor_scalar(out=at_t[:], in0=at_t[:], scalar1=acol[:, 5:6],
                                       scalar2=None, op0=ALU.mult))

            tqz = t14
            for kt in range(16):
                tt = step(TEN, st, [tqz], lambda e, kt=kt:
                          e.transpose(P[4][:, 0:128].bitcast(F32),
                                      at_t[:, kt * 128:(kt + 1) * 128], idf[:]))
                tqz = step(VEC, sv, [tt, tok_smc2], lambda e, kt=kt:
                           e.tensor_scalar(out=d1qT[:, kt * 128:(kt + 1) * 128],
                                           in0=P[4][:, 0:128], scalar1=smc[:, 4:5],
                                           scalar2=None, op0=ALU.mult))

            t_fm = step(TEN, st, [tqz, tq2], lambda e: (
                [[e.matmul(P[j][:], d1qT[:, kt * 128:(kt + 1) * 128],
                           q2_r[kt][:, j * 512:(j + 1) * 512],
                           start=(kt == 0), stop=(kt == 15)) for kt in range(16)]
                 for j in range(4)])[-1][-1])
            prev_ffn = t_fm
            t_fe = step(VEC, sv, [t_fm], lambda e: (
                [e.tensor_scalar(out=f_t[:, j * 512:(j + 1) * 512], in0=P[j][:],
                                 scalar1=smc[:, 6:7], scalar2=None, op0=ALU.mult)
                 for j in range(4)])[-1])

            u1a = step(VEC, sv, [t_fe], lambda e:
                       e.reduce_sum(out=acol[:, 0:1], in_=f_t[:], axis=AX.X))
            u1 = step(VEC, sv, [u1a], lambda e:
                      e.tensor_scalar(out=acol[:, 1:2], in0=acol[:, 0:1], scalar1=1.0 / D,
                                      scalar2=None, op0=ALU.mult))
            u2 = step(VEC, sv, [u1], lambda e:
                      e.tensor_scalar(out=f_t[:], in0=f_t[:], scalar1=acol[:, 1:2],
                                      scalar2=None, op0=ALU.subtract))
            u3 = step(SCA, ss, [u2], lambda e:
                      e.activation(out=sqtile[:], in_=f_t[:], func=AF.Square))
            u4a = step(VEC, sv, [u3], lambda e:
                       e.reduce_sum(out=acol[:, 2:3], in_=sqtile[:], axis=AX.X))
            u4 = step(VEC, sv, [u4a], lambda e:
                      e.tensor_scalar(out=acol[:, 3:4], in0=acol[:, 2:3], scalar1=1.0 / D,
                                      scalar2=1e-5, op0=ALU.mult, op1=ALU.add))
            u5 = step(SCA, ss, [u4], lambda e:
                      e.activation(out=acol[:, 4:5], in_=acol[:, 3:4], func=AF.Sqrt))
            u6 = step(VEC, sv, [u5], lambda e:
                      e.reciprocal(out=acol[:, 5:6], in_=acol[:, 4:5]))
            u7a = step(VEC, sv, [u6], lambda e:
                       e.tensor_scalar(out=f_t[:], in0=f_t[:], scalar1=acol[:, 5:6],
                                       scalar2=None, op0=ALU.mult))
            u7 = step(VEC, sv, [u7a], lambda e:
                      e.tensor_tensor(out=f_t[:], in0=f_t[:], in1=x2_t[:], op=ALU.add))

            ty = step(GPS, sg, [u7], lambda e, i=i:
                      e.dma_start(out=y_seg[i * 128:(i + 1) * 128, :], in_=f_t[:]), k=16)
            y_dma.append(ty)
            prev_fe = t_fe

        GPS.append(lambda e: e.wait_ge(sg.h, sg.n))
        _finish(block, SYNC, VEC, SCA, TEN, GPS)
    return nc


def _finish(block, SYNC, VEC, SCA, TEN, GPS):
    @block.sync
    def _(e):
        for f in SYNC:
            f(e)

    @block.vector
    def _(e):
        for f in VEC:
            f(e)

    @block.scalar
    def _(e):
        for f in SCA:
            f(e)

    @block.tensor
    def _(e):
        for f in TEN:
            f(e)

    @block.gpsimd
    def _(e):
        for f in GPS:
            f(e)


_NC_CACHE = {}


def _get_nc():
    if STAGE not in _NC_CACHE:
        _NC_CACHE[STAGE] = build_nc(STAGE)
    return _NC_CACHE[STAGE]


def make_in_maps(inputs):
    x = np.ascontiguousarray(np.asarray(inputs["L0_attn_x"], np.float32).reshape(T, D))
    wip = np.asarray(inputs["L0_attn_in_proj_w"], np.float32)
    convw = np.asarray(inputs["L0_attn_conv_w"], np.float32)
    dt_bias = np.asarray(inputs["L0_attn_dt_bias"], np.float32)
    a_log = np.asarray(inputs["L0_attn_A_log"], np.float32)
    nw = np.asarray(inputs["L0_attn_norm_weight"], np.float32)
    wout = np.asarray(inputs["L0_attn_out_proj_w"], np.float32)
    w2 = np.asarray(inputs["L0_ffn_input2"], np.float32)
    scl = np.asarray(inputs["L0_ffn_scale"], np.float32).reshape(1, 1)
    scl1 = np.asarray(inputs["L0_ffn_scale1"], np.float32).reshape(1, 1)

    ii = np.arange(C)
    mA_ = np.where(ii[None, :] < ii[:, None], 0.0, NEG).astype(np.float32)
    mR_ = np.where(ii[None, :] > ii[:, None], 0.0, NEG).astype(np.float32)
    mF_ = np.where(ii[None, :] >= ii[:, None], 0.0, NEG).astype(np.float32)
    tri = (ii[:, None] <= ii[None, :]).astype(ml_dtypes.bfloat16)
    idb_ = np.eye(128, dtype=ml_dtypes.bfloat16)
    idf_ = np.eye(128, dtype=np.float32)

    in_maps = []
    for c_ in range(NCORES):
        kh = c_
        q_rows = wip[kh * DK:(kh + 1) * DK]
        k_rows = wip[512 + kh * DK:512 + (kh + 1) * DK]
        v_rows = wip[1024 + c_ * 256:1024 + (c_ + 1) * 256]
        z_rows = wip[3072 + c_ * 256:3072 + (c_ + 1) * 256]
        b_rows = wip[5120 + 2 * c_:5120 + 2 * c_ + 2]
        a_rows = wip[5136 + 2 * c_:5136 + 2 * c_ + 2]
        w_qkv = np.concatenate([q_rows, k_rows, v_rows], axis=0)
        w_ba = np.concatenate([b_rows, a_rows,
                               np.zeros((12, D), np.float32)], axis=0)

        cw_q = convw[kh * DK:(kh + 1) * DK]
        cw_k = convw[512 + kh * DK:512 + (kh + 1) * DK]
        cw_v = convw[1024 + c_ * 256:1024 + (c_ + 1) * 256]
        dq = np.zeros((128, KS * DK), np.float32)
        dk_ = np.zeros((128, KS * DK), np.float32)
        dv0 = np.zeros((128, KS * 128), np.float32)
        dv1 = np.zeros((128, KS * 128), np.float32)
        for tap in range(KS):
            dq[0:DK, tap * DK:(tap + 1) * DK] = np.diag(cw_q[:, tap])
            dk_[DK:128, tap * DK:(tap + 1) * DK] = np.diag(cw_k[:, tap])
            dv0[:, tap * 128:(tap + 1) * 128] = np.diag(cw_v[0:128, tap])
            dv1[:, tap * 128:(tap + 1) * 128] = np.diag(cw_v[128:256, tap])

        smalls = np.zeros((1, 192), np.float32)
        smalls[0, 0:2] = dt_bias[2 * c_:2 * c_ + 2]
        smalls[0, 2:4] = a_log[2 * c_:2 * c_ + 2]
        smalls[0, 4:5] = scl[0]
        smalls[0, 5:6] = scl1[0]
        smalls[0, 64:64 + DV] = nw

        in_maps.append({
            "x_seg": np.ascontiguousarray(x[c_ * SEG:(c_ + 1) * SEG]),
            "w_qkvT": np.ascontiguousarray(w_qkv.T).astype(ml_dtypes.bfloat16),
            "w_zT": np.ascontiguousarray(z_rows.T).astype(ml_dtypes.bfloat16),
            "w_baT": np.ascontiguousarray(w_ba.T).astype(ml_dtypes.bfloat16),
            "w_outT": np.ascontiguousarray(wout.T).astype(ml_dtypes.bfloat16),
            "w2_slice": np.ascontiguousarray(w2[c_ * 256:(c_ + 1) * 256]),
            "convd_q": dq.astype(ml_dtypes.bfloat16),
            "convd_k": dk_.astype(ml_dtypes.bfloat16),
            "convd_v0": dv0.astype(ml_dtypes.bfloat16),
            "convd_v1": dv1.astype(ml_dtypes.bfloat16),
            "smalls": smalls,
            "maskA": mA_, "maskR": mR_, "maskFt": mF_,
            "triu_b": tri, "ident_b": idb_, "ident_f": idf_,
            "gtoff": (np.arange(16)[None, :] * 128 * T
                      + np.arange(128)[:, None] * T + 2048 * c_).astype(np.int32),
        })
    return in_maps


def kernel(**inputs):
    nc = _get_nc()
    in_maps = make_in_maps(inputs)
    res = run_bass_kernel_spmd(nc, in_maps, core_ids=list(range(NCORES)))
    out = np.concatenate([res.results[c_]["y_seg"] for c_ in range(NCORES)], axis=0)
    return out.reshape(1, T, D)
